# version 05
# baseline (speedup 1.0000x reference)
import numpy as np
from contextlib import ExitStack
import concourse.bass as bass
import concourse.mybir as mybir
from concourse.bass_utils import run_bass_kernel_spmd

F32 = mybir.dt.float32
BF16 = mybir.dt.bfloat16
ALU = mybir.AluOpType
AF = mybir.ActivationFunctionType
AX = mybir.AxisListType

D = 1024
T = 2048
NB = 2
NS = 16
DTR = 64
DC = 8
SPAN = 512
NSPAN = T // SPAN
TPS = SPAN // 128
LN2C = float(np.log(2.0))
EPS = 1e-5
M23 = float(1.5 * 2.0 ** 23)

_CACHE = {}


class Sem:
    def __init__(self, nc, es, name):
        self.h = es.enter_context(nc.semaphore(name))
        self.n = 0


class Prog:
    def __init__(self):
        self.items = {"sync": [], "vector": [], "scalar": [], "tensor": [], "gpsimd": []}

    def add(self, eng, fn):
        self.items[eng].append(fn)


def build_nc():
    nc = bass.Bass("TRN2", target_bir_lowering=False, debug=False,
                   detect_race_conditions=False)

    f_h = nc.dram_tensor("f", [NB * T, D], F32, kind="ExternalInput").ap()
    lng_h = nc.dram_tensor("ln_g", [D], F32, kind="ExternalInput").ap()
    lnb_h = nc.dram_tensor("ln_b", [D], F32, kind="ExternalInput").ap()
    wx_h = nc.dram_tensor("Wx", [D, D], F32, kind="ExternalInput").ap()
    bx_h = nc.dram_tensor("bx", [D], F32, kind="ExternalInput").ap()
    cw_h = nc.dram_tensor("conv_w", [D, 4], F32, kind="ExternalInput").ap()
    cb_h = nc.dram_tensor("conv_b", [D], F32, kind="ExternalInput").ap()
    wxp_h = nc.dram_tensor("Wxp", [D, 96], F32, kind="ExternalInput").ap()
    wdt_h = nc.dram_tensor("Wdt", [DTR, D], F32, kind="ExternalInput").ap()
    bdt_h = nc.dram_tensor("bdt", [D], F32, kind="ExternalInput").ap()
    alog_h = nc.dram_tensor("A_log", [D, NS], F32, kind="ExternalInput").ap()
    dp_h = nc.dram_tensor("Dp", [D], F32, kind="ExternalInput").ap()
    fw_h = nc.dram_tensor("fw", [D, D], F32, kind="ExternalInput").ap()
    fb_h = nc.dram_tensor("fb", [D], F32, kind="ExternalInput").ap()
    dw_h = nc.dram_tensor("dw", [D, 32], F32, kind="ExternalInput").ap()
    da_h = nc.dram_tensor("da", [1], F32, kind="ExternalInput").ap()
    out_h = nc.dram_tensor("out", [NB * T, D], F32, kind="ExternalOutput").ap()

    bcd = [nc.dram_tensor(f"bcd{b}", [32, T], BF16, kind="Internal").ap() for b in range(NB)]
    da_d = nc.dram_tensor("da_d", [1, 1], F32, kind="Internal").ap()

    es = ExitStack()
    S = lambda name, shape, dt: es.enter_context(nc.sbuf_tensor(name, shape, dt))

    wx_bf = [S(f"wx{i}", [128, D], BF16) for i in range(DC)]
    wqT = [S(f"wqT{i}", [128, D], BF16) for i in range(DC)]
    wxp_bf = [S(f"wxp{i}", [128, 96], BF16) for i in range(DC)]
    wdt_bf = S("wdt", [DTR, D], BF16)
    gam_bc = S("gam", [128, D], BF16)
    v0_bf = S("v0", [1, D], BF16)
    fb_bf = S("fbias", [1, D], BF16)
    ones512 = S("ones512", [1, SPAN], BF16)
    ident = S("ident", [128, 128], BF16)
    A_f = [S(f"A{i}", [128, NS], F32) for i in range(DC)]
    cw_c = [S(f"cw{i}", [128, 4], F32) for i in range(DC)]
    cb_c = [S(f"cb{i}", [128, 1], F32) for i in range(DC)]
    bdt_c = [S(f"bdtc{i}", [128, 1], F32) for i in range(DC)]
    dp_c = [S(f"dpc{i}", [128, 1], F32) for i in range(DC)]
    beta_c = [S(f"betac{i}", [128, 1], BF16) for i in range(DC)]
    eps_c = S("epsc", [128, 1], F32)
    one_c = S("onec", [128, 1], F32)
    n2l2_c = S("n2l2", [128, 1], F32)
    p2l2_c = S("p2l2", [128, 1], F32)
    da_col = S("dacol", [128, 1], F32)
    da_t = S("dat", [1, 1], F32)
    da_t2 = S("dat2", [1, 1], F32)

    Fp = [S(f"Fp{i}", [128, D], F32) for i in range(5)]
    out_stg = S("out_stg", [128, D], F32)
    bigs = [S(f"big{i}", [128, T], BF16) for i in range(24)]
    dltT = bigs[0:8]
    u_t = bigs[8:16]
    xcy = bigs[16:24]
    fng_t = [bigs[8 + i // 2][:, (i % 2) * D:(i % 2 + 1) * D] for i in range(4)]
    x1_t = [bigs[tt // 2][:, (tt % 2) * D:(tt % 2 + 1) * D] for tt in range(16)]
    ytm_t = [bigs[8 + tt // 2][:, (tt % 2) * D:(tt % 2 + 1) * D] for tt in range(16)]
    xq_t = [bigs[16 + i // 2][:, (i % 2) * D:(i % 2 + 1) * D] for i in range(4)]
    ln3_half = bigs[18][:, 0:D]
    x0b_half = bigs[18][:, D:2 * D]

    x1buf = [S(f"x1b{i}", [128, SPAN + 3], BF16) for i in range(DC)]
    Bbc = S("Bbc", [128, T], BF16)
    Cbc = S("Cbc", [128, T], BF16)
    a_t = S("a_t", [128, T], BF16)
    b_t = S("b_t", [128, T], BF16)
    h_t = S("h_t", [128, T], BF16)
    fnT = [Bbc[:, i * SPAN:(i + 1) * SPAN] for i in range(4)] + \
          [Cbc[:, i * SPAN:(i + 1) * SPAN] for i in range(4)]
    dtT = h_t[0:DTR, 0:SPAN]
    bcsb = b_t[0:32, 0:SPAN]
    cv0 = a_t[:, 0:SPAN]
    cv1 = a_t[:, SPAN:2 * SPAN]
    v0_fr = Fp[4][0:1, 0:D]
    bx_row = Fp[3][0:1, 0:D]

    sm4 = [S(f"sm4_{i}", [128, 4], F32) for i in range(8)]
    sumx_s, sumq_s, mu_s, m2_s, rstd_s, nmr_s, t4a_s, t4b_s = sm4
    sq = [S(f"sq{i}", [128, 128], F32) for i in range(8)]
    hmid = S("hmid", [128, NS], F32)
    dwq = S("dwq", [128, 32], F32)

    psA_f = es.enter_context(nc.psum_tensor("psA", [128, SPAN], F32))
    psB_f = es.enter_context(nc.psum_tensor("psB", [128, SPAN], F32))
    psA = psA_f[:].bitcast(BF16)[:, 0:SPAN]
    psB = psB_f[:].bitcast(BF16)[:, 0:SPAN]
    psXd_f = es.enter_context(nc.psum_tensor("psXd", [128, SPAN], F32))
    psXd = psXd_f[0:96, :]
    psWx = es.enter_context(nc.psum_tensor("psWx", [128, SPAN], F32))
    psZ = es.enter_context(nc.psum_tensor("psZ", [128, SPAN], F32))
    psF0 = es.enter_context(nc.psum_tensor("psF0", [128, SPAN], F32))
    psF1 = es.enter_context(nc.psum_tensor("psF1", [128, SPAN], F32))
    psY0 = es.enter_context(nc.psum_tensor("psY0", [128, SPAN], F32))

    sLD = Sem(nc, es, "sLD")
    sPE = Sem(nc, es, "sPE")
    sDV = Sem(nc, es, "sDV")
    sAC = Sem(nc, es, "sAC")
    sGP = Sem(nc, es, "sGP")
    sST = Sem(nc, es, "sST")

    P = Prog()

    def dma(dst, src):
        def fn(e, dst=dst, src=src):
            e.dma_start(out=dst, in_=src).then_inc(sLD.h, 16)
        P.add("sync", fn)
        sLD.n += 16

    def dma_store(dst, src):
        def fn(e, dst=dst, src=src):
            e.dma_start(out=dst, in_=src).then_inc(sST.h, 16)
        P.add("sync", fn)
        sST.n += 16

    def ew(eng, sem, cnt):
        if cnt <= 0:
            return
        def fn(e, s=sem.h, c=cnt):
            e.wait_ge(s, c)
        P.add(eng, fn)

    ENG2SEM = {}

    def run(eng, sem, fn, waits=()):
        for (ws, wc) in waits:
            ew(eng, ws, wc)
        def wrapped(e, fn=fn, s=sem):
            last = fn(e)
            if last is None:
                last = e.nop()
            last.then_inc(s.h, 1)
        P.add(eng, wrapped)
        sem.n += 1

    def gp_init(g):
        g.memset(ident[:], 0.0)
        g.drain()
        g.affine_select(out=ident[:], in_=ident[:], compare_op=ALU.not_equal,
                        fill=1.0, base=0, pattern=[[-1, 128]], channel_multiplier=1)
        g.memset(ones512[:], 1.0)
        g.memset(eps_c[:], EPS)
        g.memset(one_c[:], 1.0)
        g.memset(n2l2_c[:], -2.0 * LN2C)
        return g.memset(p2l2_c[:], 2.0 * LN2C)
    run("gpsimd", sGP, gp_init)
    ew("scalar", sGP, sGP.n)
    ew("vector", sGP, sGP.n)

    for i in range(DC):
        dma(Fp[0][:], wx_h[i * 128:(i + 1) * 128, :])
        run("vector", sDV, (lambda v, i=i: v.tensor_copy(wx_bf[i][:], Fp[0][:])),
            waits=[(sLD, sLD.n)])
        ew("sync", sDV, sDV.n)
    for i in range(DC):
        dma(Fp[0][:, 0:96], wxp_h[i * 128:(i + 1) * 128, :])
        run("vector", sDV, (lambda v, i=i: v.tensor_copy(wxp_bf[i][:], Fp[0][:, 0:96])),
            waits=[(sLD, sLD.n)])
        ew("sync", sDV, sDV.n)
    dma(Fp[0][0:DTR, :], wdt_h[:, :])
    run("vector", sDV, (lambda v: v.tensor_copy(wdt_bf[:], Fp[0][0:DTR, :])),
        waits=[(sLD, sLD.n)])
    ew("sync", sDV, sDV.n)
    dma(Fp[0][:], lng_h.rearrange("(one d) -> one d", one=1).broadcast_to([128, D]))
    run("vector", sDV, (lambda v: v.tensor_copy(gam_bc[:], Fp[0][:])),
        waits=[(sLD, sLD.n)])
    ew("sync", sDV, sDV.n)

    for i in range(DC):
        sl = slice(i * 128, (i + 1) * 128)
        dma(A_f[i][:], alog_h[sl, :])
        dma(cw_c[i][:], cw_h[sl, :])
        dma(cb_c[i][:], cb_h[sl].rearrange("(p one) -> p one", one=1))
        dma(bdt_c[i][:], bdt_h[sl].rearrange("(p one) -> p one", one=1))
        dma(dp_c[i][:], dp_h[sl].rearrange("(p one) -> p one", one=1))
        dma(Fp[1][:, 0:1], lnb_h[sl].rearrange("(p one) -> p one", one=1))
        run("vector", sDV, (lambda v, i=i: v.tensor_copy(beta_c[i][:], Fp[1][:, 0:1])),
            waits=[(sLD, sLD.n)])
        ew("sync", sDV, sDV.n)
    dma(bx_row, bx_h.rearrange("(one d) -> one d", one=1))
    dma(Fp[1][0:1, 0:D], fb_h.rearrange("(one d) -> one d", one=1))
    dma(da_t[:], da_h.rearrange("(one o) -> one o", one=1))
    run("vector", sDV, (lambda v: v.tensor_copy(fb_bf[:], Fp[1][0:1, 0:D])),
        waits=[(sLD, sLD.n)])

    def fA(s):
        last = None
        for i in range(DC):
            last = s.activation(Fp[2][:, i * NS:(i + 1) * NS], A_f[i][:], AF.Exp)
        return last
    run("scalar", sAC, fA, waits=[(sLD, sLD.n)])

    def fAn(v):
        last = None
        for i in range(DC):
            last = v.tensor_scalar(A_f[i][:], Fp[2][:, i * NS:(i + 1) * NS], -1.0, None, ALU.mult)
        return last
    run("vector", sDV, fAn, waits=[(sAC, sAC.n)])

    run("scalar", sAC, (lambda s: s.activation(da_t2[:], da_t[:], AF.Tanh, scale=4.0)),
        waits=[(sLD, sLD.n)])
    run("vector", sDV, (lambda v: v.tensor_scalar(da_t[:], da_t2[:], 0.5, None, ALU.mult)),
        waits=[(sAC, sAC.n)])
    ew("sync", sDV, sDV.n)
    dma(da_d[:], da_t[:])
    ew("sync", sLD, sLD.n)
    dma(da_col[:], da_d[0:1, :].broadcast_to([128, 1]))

    def fV0(t):
        last = None
        for i in range(DC):
            t.matmul(psWx[0:1, :], beta_c[i][:], wx_bf[i][:, 0:SPAN],
                     start=(i == 0), stop=(i == DC - 1))
        for i in range(DC):
            last = t.matmul(psZ[0:1, :], beta_c[i][:], wx_bf[i][:, SPAN:2 * SPAN],
                            start=(i == 0), stop=(i == DC - 1))
        return last
    run("tensor", sPE, fV0, waits=[(sDV, sDV.n), (sGP, sGP.n)])

    def fV0e(v):
        v.tensor_tensor(v0_fr[0:1, 0:SPAN], psWx[0:1, :], bx_row[0:1, 0:SPAN], ALU.add)
        v.tensor_tensor(v0_fr[0:1, SPAN:2 * SPAN], psZ[0:1, :], bx_row[0:1, SPAN:2 * SPAN], ALU.add)
        return v.tensor_copy(v0_bf[:], v0_fr[0:1, 0:D])
    run("vector", sDV, fV0e, waits=[(sPE, sPE.n), (sLD, sLD.n)])

    def quant_chain(pairs, dd_ap, scratch, d4_waits=()):
        npair = len(pairs)
        G = 32 * npair
        am, am2, l_t, eh, p2, m_t, fx, e_t = sq
        S1, S2, S3, S4 = scratch

        def d1(v, pairs=tuple(pairs), G=G):
            for p, (srcp, _) in enumerate(pairs):
                v.tensor_reduce(am[:, p * 32:(p + 1) * 32],
                                srcp.rearrange("p (a b) -> p a b", b=32),
                                AX.X, ALU.max, apply_absolute_value=True)
            v.drain()
            v.tensor_scalar(am2[:, 0:G], am[:, 0:G], 1e-30, None, ALU.max)
            v.drain()
            return v.nop()
        run("vector", sDV, d1)

        def a1(s, G=G):
            return s.activation(l_t[:, 0:G], am2[:, 0:G], AF.Ln)
        run("scalar", sAC, a1, waits=[(sDV, sDV.n)])

        def d2(v, G=G):
            v.tensor_scalar(eh[:, 0:G], l_t[:, 0:G], 1.0 / LN2C, M23, ALU.mult, ALU.add)
            v.drain()
            v.tensor_scalar(eh[:, 0:G], eh[:, 0:G], M23, None, ALU.subtract)
            v.drain()
            return v.nop()
        run("vector", sDV, d2, waits=[(sAC, sAC.n)])

        def a2(s, G=G):
            return s.activation(p2[:, 0:G], eh[:, 0:G], AF.Exp, scale=-LN2C)
        run("scalar", sAC, a2, waits=[(sDV, sDV.n)])

        def d3(v, G=G, dd=dd_ap):
            v.tensor_tensor(m_t[:, 0:G], am2[:, 0:G], p2[:, 0:G], ALU.mult)
            v.drain()
            v.tensor_scalar(fx[:, 0:G], m_t[:, 0:G], 1.0, None, ALU.is_lt)
            v.drain()
            v.tensor_tensor(e_t[:, 0:G], eh[:, 0:G], fx[:, 0:G], ALU.subtract)
            v.drain()
            if dd.shape[1] == 1:
                v.tensor_scalar(e_t[:, 0:G], e_t[:, 0:G], dd, None, ALU.add)
            else:
                v.tensor_tensor(e_t[:, 0:G], e_t[:, 0:G], dd, ALU.add)
            v.drain()
            return v.nop()
        run("vector", sDV, d3, waits=[(sAC, sAC.n)])

        def a3(s, G=G):
            s.activation(l_t[:, 0:G], e_t[:, 0:G], AF.Exp, scale=LN2C, bias=n2l2_c[:])
            return s.activation(p2[:, 0:G], e_t[:, 0:G], AF.Exp, scale=-LN2C, bias=p2l2_c[:])
        run("scalar", sAC, a3, waits=[(sDV, sDV.n)])

        def d4(v, pairs=tuple(pairs)):
            last = None
            for p, (srcp, dstp) in enumerate(pairs):
                x3 = srcp.rearrange("p (a b) -> p a b", b=32)
                isc = p2[:, p * 32:(p + 1) * 32].unsqueeze(2).broadcast_to([128, 32, 32])
                scl = l_t[:, p * 32:(p + 1) * 32].unsqueeze(2).broadcast_to([128, 32, 32])
                xs = S1
                v.tensor_tensor(xs.rearrange("p (a b) -> p a b", b=32), x3, isc, ALU.mult)
                v.tensor_scalar(xs, xs, 6.0, -6.0, ALU.min, ALU.max)
                aq = S2
                v.tensor_tensor(aq, xs, xs, ALU.mult)
                q1 = S3
                v.tensor_scalar(q1, aq, 4.0, None, ALU.is_ge)
                q4 = S2
                v.tensor_scalar(q4, aq, 16.0, None, ALU.is_ge)
                tq = S4
                v.scalar_tensor_tensor(tq, q4, 0.5, q1, ALU.mult, ALU.add)
                stp = S3
                v.scalar_tensor_tensor(stp, q1, 0.5, q4, ALU.mult, ALU.add)
                v.tensor_scalar(stp, stp, 0.5, None, ALU.add)
                rstp = S4
                v.tensor_scalar(rstp, tq, -1.0, 2.0, ALU.mult, ALU.add)
                qt = S2
                v.tensor_tensor(qt, xs, rstp, ALU.mult)
                v.tensor_scalar(qt, qt, M23, None, ALU.add)
                v.tensor_scalar(qt, qt, M23, None, ALU.subtract)
                v.tensor_tensor(qt, qt, stp, ALU.mult)
                last = v.tensor_tensor(dstp.rearrange("p (a b) -> p a b", b=32),
                                       qt.rearrange("p (a b) -> p a b", b=32), scl, ALU.mult)
            return last
        run("vector", sDV, d4, waits=[(sAC, sAC.n)] + list(d4_waits))

    for oc in range(DC):
        dma(Fp[0][:], fw_h[oc * 128:(oc + 1) * 128, :])
        dma(dwq[:], dw_h[oc * 128:(oc + 1) * 128, :])
        run("scalar", sAC, (lambda s: s.activation(sq[1][:, 0:32], dwq[:], AF.Tanh, scale=4.0)),
            waits=[(sLD, sLD.n)])
        run("vector", sDV, (lambda v: v.tensor_scalar(dwq[:], sq[1][:, 0:32], 0.5, None, ALU.mult)),
            waits=[(sAC, sAC.n)])
        wq_dst = bigs[0][:, 0:D]
        quant_chain([(Fp[0][:], wq_dst)], dwq[:], (Fp[1][:], Fp[2][:], Fp[3][:], Fp[4][:]),
                    d4_waits=[(sPE, sPE.n)])

        def t_wq(t, src=wq_dst):
            last = None
            for g in range(2):
                ps = psA if g == 0 else psB
                for q in range(4):
                    dcx = g * 4 + q
                    last = t.transpose(ps[:, q * 128:(q + 1) * 128],
                                       src[:, dcx * 128:(dcx + 1) * 128], ident[:])
            return last
        run("tensor", sPE, t_wq, waits=[(sDV, sDV.n)])

        def e_wq(v, oc=oc):
            last = None
            for g in range(2):
                ps = psA if g == 0 else psB
                for q in range(4):
                    dcx = g * 4 + q
                    last = v.tensor_copy(wqT[dcx][:, oc * 128:(oc + 1) * 128],
                                         ps[:, q * 128:(q + 1) * 128])
            return last
        run("vector", sDV, e_wq, waits=[(sPE, sPE.n)])
        ew("sync", sDV, sDV.n)

    def ln_smalls(col_n):
        def d(v, n=col_n):
            v.tensor_scalar(mu_s[:, 0:n], sumx_s[:, 0:n], 1.0 / D, None, ALU.mult)
            v.tensor_scalar(m2_s[:, 0:n], sumq_s[:, 0:n], 1.0 / D, None, ALU.mult)
            v.drain()
            v.tensor_tensor(t4a_s[:, 0:n], mu_s[:, 0:n], mu_s[:, 0:n], ALU.mult)
            v.drain()
            v.tensor_tensor(t4b_s[:, 0:n], m2_s[:, 0:n], t4a_s[:, 0:n], ALU.subtract)
            v.drain()
            return v.nop()
        run("vector", sDV, d, waits=[(sAC, sAC.n)])

        def a(s, n=col_n):
            s.activation(t4a_s[:, 0:n], t4b_s[:, 0:n], AF.Ln, bias=eps_c[:])
            s.drain()
            s.activation(rstd_s[:, 0:n], t4a_s[:, 0:n], AF.Exp, scale=-0.5)
            s.drain()
            return s.nop()
        run("scalar", sAC, a, waits=[(sDV, sDV.n)])

    for b in range(NB):
        for sp in range(NSPAN):
            t0 = sp * SPAN
            ew("sync", sDV, sDV.n)
            ew("sync", sAC, sAC.n)
            for q in range(TPS):
                row = b * T + t0 + q * 128
                dma(Fp[q][:], f_h[row:row + 128, :])
            ld_all = sLD.n
            def sq1(s):
                last = None
                for q in range(TPS):
                    s.activation(Fp[4][:], Fp[q][:], AF.Copy, accum_out=sumx_s[:, q:q + 1])
                    last = s.activation(Fp[4][:], Fp[q][:], AF.Square,
                                        accum_out=sumq_s[:, q:q + 1])
                return last
            run("scalar", sAC, sq1, waits=[(sLD, ld_all), (sDV, sDV.n)])
            ln_smalls(TPS)
            def ap1(v):
                last = None
                for q in range(TPS):
                    v.tensor_scalar(Fp[q][:], Fp[q][:], mu_s[:, q:q + 1], None, ALU.subtract)
                    last = v.tensor_scalar(Fp[q][:], Fp[q][:], rstd_s[:, q:q + 1], None, ALU.mult)
                return last
            run("vector", sDV, ap1, waits=[(sAC, sAC.n)])
            def sq2(s):
                last = None
                for q in range(TPS):
                    s.activation(Fp[4][:], Fp[q][:], AF.Copy, accum_out=sumx_s[:, q:q + 1])
                    last = s.activation(Fp[4][:], Fp[q][:], AF.Square,
                                        accum_out=sumq_s[:, q:q + 1])
                return last
            run("scalar", sAC, sq2, waits=[(sDV, sDV.n)])
            ln_smalls(TPS)
            def ap2(v):
                last = None
                for q in range(TPS):
                    v.tensor_scalar(Fp[q][:], Fp[q][:], mu_s[:, q:q + 1], None, ALU.subtract)
                    v.tensor_scalar(Fp[q][:], Fp[q][:], rstd_s[:, q:q + 1], None, ALU.mult)
                    last = v.tensor_tensor(fng_t[q][:], Fp[q][:], gam_bc[:], ALU.mult)
                return last
            run("vector", sDV, ap2, waits=[(sAC, sAC.n)])

            fng_done = sDV.n
            evac_marks = {}
            for dcx in range(DC):
                ps = psA if dcx % 2 == 0 else psB
                waits = [(sDV, fng_done)]
                if dcx >= 2:
                    waits.append((sDV, evac_marks[dcx - 2]))

                def tr(t, dcx=dcx, ps=ps):
                    last = None
                    for q in range(TPS):
                        last = t.transpose(ps[:, q * 128:(q + 1) * 128],
                                           fng_t[q][:, dcx * 128:(dcx + 1) * 128], ident[:])
                    return last
                run("tensor", sPE, tr, waits=waits)

                def ev(v, dcx=dcx, ps=ps):
                    return v.tensor_copy(fnT[dcx], ps[:])
                run("vector", sDV, ev, waits=[(sPE, sPE.n)])
                evac_marks[dcx] = sDV.n

            if sp == 0:
                def cc(v):
                    last = None
                    for jc in range(DC):
                        last = v.memset(x1buf[jc][:, 0:3], 0.0)
                    return last
                run("vector", sDV, cc)
            else:
                def cc(v):
                    last = None
                    for jc in range(DC):
                        last = v.tensor_copy(x1buf[jc][:, 0:3], x1buf[jc][:, SPAN:SPAN + 3])
                    return last
                run("vector", sDV, cc)

            fnT_done = sDV.n
            wx_evac = {}
            for jc in range(DC):
                ps = psWx if jc % 2 == 0 else psZ
                waits = [(sDV, fnT_done)]
                if jc >= 2:
                    waits.append((sAC, wx_evac[jc - 2]))
                else:
                    waits.append((sAC, sAC.n))

                def mmwx(t, jc=jc, ps=ps):
                    for dcx in range(DC):
                        t.matmul(ps[:], wx_bf[dcx][:, jc * 128:(jc + 1) * 128], fnT[dcx],
                                 start=(dcx == 0), stop=False)
                    return t.matmul(ps[:], v0_bf[0:1, jc * 128:(jc + 1) * 128], ones512[0:1, :],
                                    start=False, stop=True)
                run("tensor", sPE, mmwx, waits=waits)

                def evx(s, jc=jc, ps=ps):
                    return s.activation(x1buf[jc][:, 3:3 + SPAN], ps[:], AF.Copy)
                run("scalar", sAC, evx, waits=[(sPE, sPE.n), (sDV, sDV.n)])
                wx_evac[jc] = sAC.n

                def cnv(v, jc=jc):
                    v.tensor_scalar(cv0, x1buf[jc][:, 0:SPAN], cw_c[jc][:, 0:1], None, ALU.mult)
                    v.scalar_tensor_tensor(cv1, x1buf[jc][:, 1:1 + SPAN], cw_c[jc][:, 1:2],
                                           cv0, ALU.mult, ALU.add)
                    v.scalar_tensor_tensor(cv0[:], x1buf[jc][:, 2:2 + SPAN], cw_c[jc][:, 2:3],
                                           cv1[:], ALU.mult, ALU.add)
                    return v.scalar_tensor_tensor(cv1, x1buf[jc][:, 3:3 + SPAN], cw_c[jc][:, 3:4],
                                                  cv0, ALU.mult, ALU.add)
                run("vector", sDV, cnv, waits=[(sAC, sAC.n)])

                def sil(s, jc=jc, t0=t0):
                    return s.activation(xcy[jc][:, t0:t0 + SPAN], cv1, AF.Silu, bias=cb_c[jc][:])
                run("scalar", sAC, sil, waits=[(sDV, sDV.n)])

            def mmxp(t, t0=t0):
                last = None
                for dcx in range(DC):
                    last = t.matmul(psXd[:], wxp_bf[dcx][:], xcy[dcx][:, t0:t0 + SPAN],
                                    start=(dcx == 0), stop=(dcx == DC - 1))
                return last
            run("tensor", sPE, mmxp, waits=[(sAC, sAC.n)])

            run("scalar", sAC, (lambda s: s.activation(dtT, psXd[0:DTR, :], AF.Copy)),
                waits=[(sPE, sPE.n)])
            run("vector", sDV, (lambda v: v.tensor_copy(bcsb, psXd[DTR:96, :])),
                waits=[(sPE, sPE.n), (sLD, sLD.n)])
            ew("sync", sDV, sDV.n)
            def st_bc(e, b=b, t0=t0):
                e.dma_start(out=bcd[b][:, t0:t0 + SPAN], in_=bcsb).then_inc(sLD.h, 16)
            P.add("sync", st_bc)
            sLD.n += 16

            sp_done = sAC.n
            sp_evac = {}
            for dcx in range(DC):
                ps = psWx if dcx % 2 == 0 else psZ
                waits = [(sAC, sp_done)]
                if dcx >= 2:
                    waits.append((sAC, sp_evac[dcx - 2]))

                def mmdt(t, dcx=dcx, ps=ps):
                    return t.matmul(ps[:], wdt_bf[:, dcx * 128:(dcx + 1) * 128], dtT,
                                    start=True, stop=True)
                run("tensor", sPE, mmdt, waits=waits)

                def spa(s, dcx=dcx, ps=ps):
                    return s.activation(Fp[0][:, 0:SPAN], ps[:], AF.Copy)
                run("scalar", sAC, spa, waits=[(sPE, sPE.n), (sDV, sDV.n)])
                sp_evac[dcx] = sAC.n

                def spd(v, dcx=dcx):
                    v.tensor_scalar(Fp[1][:, 0:SPAN], Fp[0][:, 0:SPAN], bdt_c[dcx][:],
                                    0.0, ALU.add, ALU.max)
                    v.tensor_scalar(Fp[2][:, 0:SPAN], Fp[0][:, 0:SPAN], bdt_c[dcx][:],
                                    0.0, ALU.add, ALU.min)
                    return v.tensor_tensor(Fp[4][:, 0:SPAN], Fp[2][:, 0:SPAN],
                                           Fp[1][:, 0:SPAN], ALU.subtract)
                run("vector", sDV, spd, waits=[(sAC, sAC.n)])

                def spe(s):
                    s.activation(Fp[2][:, 0:SPAN], Fp[4][:, 0:SPAN], AF.Exp)
                    return s.activation(Fp[3][:, 0:SPAN], Fp[2][:, 0:SPAN], AF.Ln, bias=one_c[:])
                run("scalar", sAC, spe, waits=[(sDV, sDV.n)])

                def spf(v, dcx=dcx, t0=t0):
                    return v.tensor_tensor(dltT[dcx][:, t0:t0 + SPAN], Fp[3][:, 0:SPAN],
                                           Fp[1][:, 0:SPAN], ALU.add)
                run("vector", sDV, spf, waits=[(sAC, sAC.n)])

        def s2init(v):
            for dcx in range(DC):
                v.tensor_tensor(u_t[dcx][:], dltT[dcx][:], xcy[dcx][:], ALU.mult)
            last = None
            for dcx in range(DC):
                last = v.tensor_scalar(xcy[dcx][:], xcy[dcx][:], dp_c[dcx][:], None, ALU.mult)
            return last
        run("vector", sDV, s2init)
        ew("scalar", sDV, sDV.n)

        psy8 = [psWx, psZ, psF0, psF1, psA_f, psB_f, psXd_f, psY0]
        a_bufs = [Fp[0][:, 0:SPAN].bitcast(BF16), Fp[0][:, SPAN:2 * SPAN].bitcast(BF16),
                  Cbc[:, 0:T // 2], Cbc[:, T // 2:T]]
        z_bufs = [Fp[1][:, 0:SPAN].bitcast(BF16), Fp[1][:, SPAN:2 * SPAN].bitcast(BF16)]
        HT = T // 2
        bc_bufs = [(Fp[2][:, 0:SPAN].bitcast(BF16), Fp[3][:, 0:SPAN].bitcast(BF16)),
                   (Fp[2][:, SPAN:2 * SPAN].bitcast(BF16), Fp[3][:, SPAN:2 * SPAN].bitcast(BF16)),
                   (Fp[4][:, 0:SPAN].bitcast(BF16), Fp[4][:, SPAN:2 * SPAN].bitcast(BF16)),
                   (b_t[:, HT:2 * HT], h_t[:, HT:2 * HT])]
        NPF = 4
        if b == 0:
            ew("sync", sPE, sPE.n)
        ew("sync", sLD, sLD.n)
        ew("sync", sAC, sAC.n)
        ew("scalar", sDV, sDV.n)
        ew("scalar", sPE, sPE.n)
        s2init_dv = sDV.n
        for dcx in range(DC):
            for half in range(2):
                ht0 = half * HT
                ew("sync", sDV, sDV.n)
                bc_ld = {}
                for pf in range(NPF):
                    dma(bc_bufs[pf][0], bcd[b][pf:pf + 1, ht0:ht0 + HT].broadcast_to([128, HT]))
                    dma(bc_bufs[pf][1], bcd[b][16 + pf:17 + pf, ht0:ht0 + HT].broadcast_to([128, HT]))
                    bc_ld[pf] = sLD.n

                def tinit(t, dcx=dcx, ht0=ht0):
                    last = None
                    for q in range(8):
                        last = t.matmul(psy8[q][:, 0:128],
                                        xcy[dcx][:, ht0 + q * 128:ht0 + (q + 1) * 128],
                                        ident[:], start=True, stop=False)
                    return last
                run("tensor", sPE, tinit, waits=[(sDV, s2init_dv), (sAC, sAC.n)])

                scn_marks = {}
                zt_marks = {}
                a_marks = {}
                for n in range(NS):
                    aw = []
                    if n >= 4:
                        aw.append((sDV, scn_marks[n - 4]))
                    def abuild(s, n=n, dcx=dcx, ht0=ht0):
                        return s.activation(a_bufs[n % 4], dltT[dcx][:, ht0:ht0 + HT],
                                            AF.Copy, scale=A_f[dcx][:, n:n + 1])
                    run("scalar", sAC, abuild, waits=aw)
                    a_marks[n] = sAC.n

                    dw = [(sLD, bc_ld[n]), (sAC, a_marks[n])]
                    if n >= 2:
                        dw.append((sPE, zt_marks[n - 2]))
                    def scn(v, n=n, dcx=dcx, half=half, ht0=ht0):
                        Bb, Cb = bc_bufs[n % NPF]
                        v.tensor_tensor(b_t[:, 0:HT], u_t[dcx][:, ht0:ht0 + HT], Bb, ALU.mult)
                        init = 0.0 if half == 0 else hmid[:, n:n + 1]
                        v.tensor_tensor_scan(h_t[:, 0:HT], a_bufs[n % 4], b_t[:, 0:HT],
                                             init, ALU.mult, ALU.add)
                        last = v.tensor_tensor(z_bufs[n % 2], h_t[:, 0:HT], Cb, ALU.mult)
                        if half == 0:
                            last = v.tensor_copy(hmid[:, n:n + 1], h_t[:, HT - 1:HT])
                        return last
                    run("vector", sDV, scn, waits=dw)
                    scn_marks[n] = sDV.n

                    def zt(t, n=n):
                        last = None
                        for q in range(8):
                            last = t.matmul(psy8[q][:, 0:128],
                                            z_bufs[n % 2][:, q * 128:(q + 1) * 128], ident[:],
                                            start=False, stop=(n == NS - 1))
                        return last
                    run("tensor", sPE, zt, waits=[(sDV, scn_marks[n])])
                    zt_marks[n] = sPE.n

                    if n + NPF < NS:
                        ew("sync", sDV, scn_marks[n])
                        dma(bc_bufs[n % NPF][0],
                            bcd[b][n + NPF:n + NPF + 1, ht0:ht0 + HT].broadcast_to([128, HT]))
                        dma(bc_bufs[n % NPF][1],
                            bcd[b][16 + n + NPF:17 + n + NPF, ht0:ht0 + HT].broadcast_to([128, HT]))
                        bc_ld[n + NPF] = sLD.n

                def yev(s, dcx=dcx, half=half):
                    last = None
                    for q in range(8):
                        tt = half * 8 + q
                        last = s.activation(ytm_t[tt][:, dcx * 128:(dcx + 1) * 128],
                                            psy8[q][:, 0:128], AF.Copy)
                    return last
                run("scalar", sAC, yev, waits=[(sPE, sPE.n)])

        ew("vector", sAC, sAC.n)
        x0b4 = [xcy[2 + q // 2][:, (q % 2) * D:(q % 2 + 1) * D] for q in range(4)]
        ln34 = [xcy[4 + q // 2][:, (q % 2) * D:(q % 2 + 1) * D] for q in range(4)]
        q_scr = (b_t[:].bitcast(F32), h_t[:].bitcast(F32), a_t[:].bitcast(F32), Fp[4][:])
        for sp in range(NSPAN):
            t0 = sp * SPAN
            ew("sync", sDV, sDV.n)
            ew("sync", sAC, sAC.n)
            for q in range(TPS):
                row = b * T + t0 + q * 128
                dma(Fp[q][:], f_h[row:row + 128, :])
            ld_all = sLD.n
            def sq3(s, sp=sp):
                last = None
                for q in range(TPS):
                    s.activation(Fp[4][:], ytm_t[sp * TPS + q][:], AF.Copy,
                                 accum_out=sumx_s[:, q:q + 1])
                    last = s.activation(Fp[4][:], ytm_t[sp * TPS + q][:], AF.Square,
                                        accum_out=sumq_s[:, q:q + 1])
                return last
            run("scalar", sAC, sq3, waits=[(sDV, sDV.n)])
            ln_smalls(TPS)
            def ap3(v, sp=sp):
                last = None
                for q in range(TPS):
                    tt = sp * TPS + q
                    v.tensor_scalar(ln34[q], ytm_t[tt][:], mu_s[:, q:q + 1], None, ALU.subtract)
                    last = v.tensor_scalar(ln34[q], ln34[q], rstd_s[:, q:q + 1], None, ALU.mult)
                return last
            run("vector", sDV, ap3, waits=[(sAC, sAC.n)])
            def cst(s):
                last = None
                for q in range(TPS):
                    last = s.activation(x0b4[q], Fp[q][:], AF.Copy)
                return last
            run("scalar", sAC, cst, waits=[(sLD, ld_all), (sPE, sPE.n)])
            def res1(v, sp=sp):
                last = None
                for q in range(TPS):
                    last = v.tensor_tensor(x1_t[sp * TPS + q][:], ln34[q], x0b4[q], ALU.add)
                return last
            run("vector", sDV, res1, waits=[(sAC, sAC.n)])
            def sq4(s, sp=sp):
                last = None
                for q in range(TPS):
                    s.activation(Fp[4][:], x1_t[sp * TPS + q][:], AF.Copy,
                                 accum_out=sumx_s[:, q:q + 1])
                    last = s.activation(Fp[4][:], x1_t[sp * TPS + q][:], AF.Square,
                                        accum_out=sumq_s[:, q:q + 1])
                return last
            run("scalar", sAC, sq4, waits=[(sDV, sDV.n)])
            ln_smalls(TPS)
            def ap4(v, sp=sp):
                last = None
                for q in range(TPS):
                    tt = sp * TPS + q
                    v.tensor_scalar(Fp[q][:], x1_t[tt][:], mu_s[:, q:q + 1], None, ALU.subtract)
                    last = v.tensor_scalar(Fp[q][:], Fp[q][:], rstd_s[:, q:q + 1], None, ALU.mult)
                return last
            run("vector", sDV, ap4, waits=[(sAC, sAC.n)])
            quant_chain([(Fp[q][:], xq_t[q][:]) for q in range(TPS)], da_col[:],
                        q_scr, d4_waits=[(sPE, sPE.n)])

            xq_done = sDV.n
            evac_marks = {}
            for dcx in range(DC):
                ps = psA if dcx % 2 == 0 else psB
                waits = [(sDV, xq_done)]
                if dcx >= 2:
                    waits.append((sDV, evac_marks[dcx - 2]))

                def trq(t, dcx=dcx, ps=ps):
                    last = None
                    for q in range(TPS):
                        last = t.transpose(ps[:, q * 128:(q + 1) * 128],
                                           xq_t[q][:, dcx * 128:(dcx + 1) * 128], ident[:])
                    return last
                run("tensor", sPE, trq, waits=waits)

                def evq(v, dcx=dcx, ps=ps):
                    return v.tensor_copy(fnT[dcx], ps[:])
                run("vector", sDV, evq, waits=[(sPE, sPE.n)])
                evac_marks[dcx] = sDV.n

            xqT_done = sDV.n
            evf_marks = {}
            for q in range(TPS):
                tt = sp * TPS + q
                row = b * T + t0 + q * 128
                pF0, pF1 = (psF0, psF1) if q % 2 == 0 else (psA_f, psB_f)
                waits = [(sDV, xqT_done), (sAC, sAC.n if q < 2 else evf_marks[q - 2])]
                if q >= 1:
                    waits.append((sDV, sDV.n))

                def mmf(t, q=q, pF0=pF0, pF1=pF1):
                    for dcx in range(DC):
                        sl = slice(q * 128, (q + 1) * 128)
                        t.matmul(pF0[:], fnT[dcx][:, sl], wqT[dcx][:, 0:SPAN],
                                 start=(dcx == 0), stop=False)
                        t.matmul(pF1[:], fnT[dcx][:, sl], wqT[dcx][:, SPAN:2 * SPAN],
                                 start=(dcx == 0), stop=False)
                    t.matmul(pF0[:], ones512[0:1, 0:128], fb_bf[0:1, 0:SPAN],
                             start=False, stop=True)
                    return t.matmul(pF1[:], ones512[0:1, 0:128], fb_bf[0:1, SPAN:2 * SPAN],
                                    start=False, stop=True)
                run("tensor", sPE, mmf, waits=waits)

                def evf(s, pF0=pF0, pF1=pF1):
                    s.activation(Fp[1][:, 0:SPAN], pF0[:], AF.Copy, accum_out=t4a_s[:, 0:1])
                    s.activation(Fp[1][:, SPAN:2 * SPAN], pF1[:], AF.Copy, accum_out=t4b_s[:, 0:1])
                    return s.activation(Fp[2][:], Fp[1][:], AF.Square, accum_out=sumq_s[:, 0:1])
                run("scalar", sAC, evf, waits=[(sPE, sPE.n)])
                evf_marks[q] = sAC.n

                def evs(v):
                    v.tensor_tensor(sumx_s[:, 0:1], t4a_s[:, 0:1], t4b_s[:, 0:1], ALU.add)
                    v.drain()
                    return v.nop()
                run("vector", sDV, evs, waits=[(sAC, sAC.n)])
                ln_smalls(1)
                if sST.n > 0:
                    ew("vector", sST, sST.n)
                def ap5(v, tt=tt):
                    v.tensor_scalar(Fp[2][:], Fp[1][:], mu_s[:, 0:1], None, ALU.subtract)
                    v.tensor_scalar(Fp[2][:], Fp[2][:], rstd_s[:, 0:1], None, ALU.mult)
                    return v.tensor_tensor(out_stg[:], Fp[2][:], x1_t[tt][:], ALU.add)
                run("vector", sDV, ap5, waits=[(sAC, sAC.n)])
                ew("sync", sDV, sDV.n)
                dma_store(out_h[row:row + 128, :], out_stg[:])

    def fin(e):
        e.wait_ge(sST.h, sST.n)
    P.add("sync", fin)

    with nc.Block() as blk:
        blk.sync(lambda e: [fn(e) for fn in P.items["sync"]])
        blk.vector(lambda e: [fn(e) for fn in P.items["vector"]])
        blk.scalar(lambda e: [fn(e) for fn in P.items["scalar"]])
        blk.tensor(lambda e: [fn(e) for fn in P.items["tensor"]])
        blk.gpsimd(lambda e: [fn(e) for fn in P.items["gpsimd"]])
    es.close()
    return nc


def kernel(L0_attn_f, L0_attn_ln_g, L0_attn_ln_b, L0_attn_Wx, L0_attn_bx,
           L0_attn_conv_w, L0_attn_conv_b, L0_attn_Wxp, L0_attn_Wdt, L0_attn_bdt,
           L0_attn_A_log, L0_attn_Dp, L0_ffn_weight, L0_ffn_bias, L0_ffn_delta_w,
           L0_ffn_delta_act):
    if "nc" not in _CACHE:
        _CACHE["nc"] = build_nc()
    nc = _CACHE["nc"]
    f = np.ascontiguousarray(np.asarray(L0_attn_f, dtype=np.float32))
    common = {
        "ln_g": np.ascontiguousarray(L0_attn_ln_g, dtype=np.float32),
        "ln_b": np.ascontiguousarray(L0_attn_ln_b, dtype=np.float32),
        "Wx": np.ascontiguousarray(L0_attn_Wx, dtype=np.float32),
        "bx": np.ascontiguousarray(L0_attn_bx, dtype=np.float32),
        "conv_w": np.ascontiguousarray(L0_attn_conv_w, dtype=np.float32),
        "conv_b": np.ascontiguousarray(L0_attn_conv_b, dtype=np.float32),
        "Wxp": np.ascontiguousarray(L0_attn_Wxp, dtype=np.float32),
        "Wdt": np.ascontiguousarray(L0_attn_Wdt, dtype=np.float32),
        "bdt": np.ascontiguousarray(L0_attn_bdt, dtype=np.float32),
        "A_log": np.ascontiguousarray(L0_attn_A_log, dtype=np.float32),
        "Dp": np.ascontiguousarray(L0_attn_Dp, dtype=np.float32),
        "fw": np.ascontiguousarray(L0_ffn_weight, dtype=np.float32),
        "fb": np.ascontiguousarray(L0_ffn_bias, dtype=np.float32),
        "dw": np.ascontiguousarray(L0_ffn_delta_w, dtype=np.float32),
        "da": np.ascontiguousarray(L0_ffn_delta_act, dtype=np.float32),
    }
    in_maps = []
    for c in range(8):
        m = dict(common)
        m["f"] = np.ascontiguousarray(f[2 * c:2 * c + 2].reshape(NB * T, D))
        in_maps.append(m)
    res = run_bass_kernel_spmd(nc, in_maps, core_ids=list(range(8)))
    out = np.concatenate([r["out"].reshape(NB, T, D) for r in res.results], axis=0)
    return np.ascontiguousarray(out, dtype=np.float32)


# version 06
# speedup vs baseline: 1.0672x; 1.0672x over previous
import numpy as np
from contextlib import ExitStack
import concourse.bass as bass
import concourse.mybir as mybir
from concourse.bass_utils import run_bass_kernel_spmd

F32 = mybir.dt.float32
BF16 = mybir.dt.bfloat16
ALU = mybir.AluOpType
AF = mybir.ActivationFunctionType
AX = mybir.AxisListType

D = 1024
T = 2048
NB = 2
NS = 16
DTR = 64
DC = 8
SPAN = 512
NSPAN = T // SPAN
TPS = SPAN // 128
LN2C = float(np.log(2.0))
EPS = 1e-5
M23 = float(1.5 * 2.0 ** 23)

_CACHE = {}


class Sem:
    def __init__(self, nc, es, name):
        self.h = es.enter_context(nc.semaphore(name))
        self.n = 0


class Prog:
    def __init__(self):
        self.items = {"sync": [], "vector": [], "scalar": [], "tensor": [], "gpsimd": []}

    def add(self, eng, fn):
        self.items[eng].append(fn)


def build_nc():
    nc = bass.Bass("TRN2", target_bir_lowering=False, debug=False,
                   detect_race_conditions=False)

    f_h = nc.dram_tensor("f", [NB * T, D], F32, kind="ExternalInput").ap()
    lng_h = nc.dram_tensor("ln_g", [D], F32, kind="ExternalInput").ap()
    lnb_h = nc.dram_tensor("ln_b", [D], F32, kind="ExternalInput").ap()
    wx_h = nc.dram_tensor("Wx", [D, D], F32, kind="ExternalInput").ap()
    bx_h = nc.dram_tensor("bx", [D], F32, kind="ExternalInput").ap()
    cw_h = nc.dram_tensor("conv_w", [D, 4], F32, kind="ExternalInput").ap()
    cb_h = nc.dram_tensor("conv_b", [D], F32, kind="ExternalInput").ap()
    wxp_h = nc.dram_tensor("Wxp", [D, 96], F32, kind="ExternalInput").ap()
    wdt_h = nc.dram_tensor("Wdt", [DTR, D], F32, kind="ExternalInput").ap()
    bdt_h = nc.dram_tensor("bdt", [D], F32, kind="ExternalInput").ap()
    alog_h = nc.dram_tensor("A_log", [D, NS], F32, kind="ExternalInput").ap()
    dp_h = nc.dram_tensor("Dp", [D], F32, kind="ExternalInput").ap()
    fw_h = nc.dram_tensor("fw", [D, D], F32, kind="ExternalInput").ap()
    fb_h = nc.dram_tensor("fb", [D], F32, kind="ExternalInput").ap()
    dw_h = nc.dram_tensor("dw", [D, 32], F32, kind="ExternalInput").ap()
    da_h = nc.dram_tensor("da", [1], F32, kind="ExternalInput").ap()
    out_h = nc.dram_tensor("out", [NB * T, D], F32, kind="ExternalOutput").ap()

    bcd = [nc.dram_tensor(f"bcd{b}", [32, T], BF16, kind="Internal").ap() for b in range(NB)]
    da_d = nc.dram_tensor("da_d", [1, 1], F32, kind="Internal").ap()

    es = ExitStack()
    S = lambda name, shape, dt: es.enter_context(nc.sbuf_tensor(name, shape, dt))

    wx_bf = [S(f"wx{i}", [128, D], BF16) for i in range(DC)]
    wqT = [S(f"wqT{i}", [128, D], BF16) for i in range(DC)]
    wxp_bf = [S(f"wxp{i}", [128, 96], BF16) for i in range(DC)]
    wdt_bf = S("wdt", [DTR, D], BF16)
    gam_bc = S("gam", [128, D], BF16)
    v0_bf = S("v0", [1, D], BF16)
    fb_bf = S("fbias", [1, D], BF16)
    ones512 = S("ones512", [1, SPAN], BF16)
    ident = S("ident", [128, 128], BF16)
    A_f = [S(f"A{i}", [128, NS], F32) for i in range(DC)]
    cw_c = [S(f"cw{i}", [128, 4], F32) for i in range(DC)]
    cb_c = [S(f"cb{i}", [128, 1], F32) for i in range(DC)]
    bdt_c = [S(f"bdtc{i}", [128, 1], F32) for i in range(DC)]
    dp_c = [S(f"dpc{i}", [128, 1], F32) for i in range(DC)]
    beta_c = [S(f"betac{i}", [128, 1], BF16) for i in range(DC)]
    eps_c = S("epsc", [128, 1], F32)
    one_c = S("onec", [128, 1], F32)
    n2l2_c = S("n2l2", [128, 1], F32)
    p2l2_c = S("p2l2", [128, 1], F32)
    da_col = S("dacol", [128, 1], F32)
    da_t = S("dat", [1, 1], F32)
    da_t2 = S("dat2", [1, 1], F32)

    Fp = [S(f"Fp{i}", [128, D], F32) for i in range(5)]
    out_stg = S("out_stg", [128, D], F32)
    bigs = [S(f"big{i}", [128, T], BF16) for i in range(24)]
    dltT = bigs[0:8]
    u_t = bigs[8:16]
    xcy = bigs[16:24]
    fng_t = [bigs[8 + i // 2][:, (i % 2) * D:(i % 2 + 1) * D] for i in range(4)]
    x1_t = [bigs[tt // 2][:, (tt % 2) * D:(tt % 2 + 1) * D] for tt in range(16)]
    ytm_t = [bigs[8 + tt // 2][:, (tt % 2) * D:(tt % 2 + 1) * D] for tt in range(16)]
    xq_t = [bigs[16 + i // 2][:, (i % 2) * D:(i % 2 + 1) * D] for i in range(4)]
    ln3_half = bigs[18][:, 0:D]
    x0b_half = bigs[18][:, D:2 * D]

    x1buf = [S(f"x1b{i}", [128, SPAN + 3], BF16) for i in range(DC)]
    Bbc = S("Bbc", [128, T], BF16)
    Cbc = S("Cbc", [128, T], BF16)
    a_t = S("a_t", [128, T], BF16)
    b_t = S("b_t", [128, T], BF16)
    h_t = S("h_t", [128, T], BF16)
    fnT = [Bbc[:, i * SPAN:(i + 1) * SPAN] for i in range(4)] + \
          [Cbc[:, i * SPAN:(i + 1) * SPAN] for i in range(4)]
    dtT = h_t[0:DTR, 0:SPAN]
    bcsb = b_t[0:32, 0:SPAN]
    cv0 = a_t[:, 0:SPAN]
    cv1 = a_t[:, SPAN:2 * SPAN]
    v0_fr = Fp[4][0:1, 0:D]
    bx_row = Fp[3][0:1, 0:D]

    sm4 = [S(f"sm4_{i}", [128, 4], F32) for i in range(8)]
    sumx_s, sumq_s, mu_s, m2_s, rstd_s, nmr_s, t4a_s, t4b_s = sm4
    sq = [S(f"sq{i}", [128, 128], F32) for i in range(8)]
    hmid = S("hmid", [128, NS], F32)
    dwq = S("dwq", [128, 32], F32)

    psA_f = es.enter_context(nc.psum_tensor("psA", [128, SPAN], F32))
    psB_f = es.enter_context(nc.psum_tensor("psB", [128, SPAN], F32))
    psA = psA_f[:].bitcast(BF16)[:, 0:SPAN]
    psB = psB_f[:].bitcast(BF16)[:, 0:SPAN]
    psXd_f = es.enter_context(nc.psum_tensor("psXd", [128, SPAN], F32))
    psXd = psXd_f[0:96, :]
    psWx = es.enter_context(nc.psum_tensor("psWx", [128, SPAN], F32))
    psZ = es.enter_context(nc.psum_tensor("psZ", [128, SPAN], F32))
    psF0 = es.enter_context(nc.psum_tensor("psF0", [128, SPAN], F32))
    psF1 = es.enter_context(nc.psum_tensor("psF1", [128, SPAN], F32))
    psY0 = es.enter_context(nc.psum_tensor("psY0", [128, SPAN], F32))

    sLD = Sem(nc, es, "sLD")
    sPE = Sem(nc, es, "sPE")
    sDV = Sem(nc, es, "sDV")
    sAC = Sem(nc, es, "sAC")
    sGP = Sem(nc, es, "sGP")
    sST = Sem(nc, es, "sST")

    P = Prog()

    def dma(dst, src):
        def fn(e, dst=dst, src=src):
            e.dma_start(out=dst, in_=src).then_inc(sLD.h, 16)
        P.add("sync", fn)
        sLD.n += 16

    def dma_store(dst, src):
        def fn(e, dst=dst, src=src):
            e.dma_start(out=dst, in_=src).then_inc(sST.h, 16)
        P.add("sync", fn)
        sST.n += 16

    def ew(eng, sem, cnt):
        if cnt <= 0:
            return
        def fn(e, s=sem.h, c=cnt):
            e.wait_ge(s, c)
        P.add(eng, fn)

    ENG2SEM = {}

    def run(eng, sem, fn, waits=()):
        for (ws, wc) in waits:
            ew(eng, ws, wc)
        def wrapped(e, fn=fn, s=sem):
            last = fn(e)
            if last is None:
                last = e.nop()
            last.then_inc(s.h, 1)
        P.add(eng, wrapped)
        sem.n += 1

    def gp_init(g):
        g.memset(ident[:], 0.0)
        g.drain()
        g.affine_select(out=ident[:], in_=ident[:], compare_op=ALU.not_equal,
                        fill=1.0, base=0, pattern=[[-1, 128]], channel_multiplier=1)
        g.memset(ones512[:], 1.0)
        g.memset(eps_c[:], EPS)
        g.memset(one_c[:], 1.0)
        g.memset(n2l2_c[:], -2.0 * LN2C)
        return g.memset(p2l2_c[:], 2.0 * LN2C)
    run("gpsimd", sGP, gp_init)
    ew("scalar", sGP, sGP.n)
    ew("vector", sGP, sGP.n)

    for i in range(DC):
        dma(Fp[0][:], wx_h[i * 128:(i + 1) * 128, :])
        run("vector", sDV, (lambda v, i=i: v.tensor_copy(wx_bf[i][:], Fp[0][:])),
            waits=[(sLD, sLD.n)])
        ew("sync", sDV, sDV.n)
    for i in range(DC):
        dma(Fp[0][:, 0:96], wxp_h[i * 128:(i + 1) * 128, :])
        run("vector", sDV, (lambda v, i=i: v.tensor_copy(wxp_bf[i][:], Fp[0][:, 0:96])),
            waits=[(sLD, sLD.n)])
        ew("sync", sDV, sDV.n)
    dma(Fp[0][0:DTR, :], wdt_h[:, :])
    run("vector", sDV, (lambda v: v.tensor_copy(wdt_bf[:], Fp[0][0:DTR, :])),
        waits=[(sLD, sLD.n)])
    ew("sync", sDV, sDV.n)
    dma(Fp[0][:], lng_h.rearrange("(one d) -> one d", one=1).broadcast_to([128, D]))
    run("vector", sDV, (lambda v: v.tensor_copy(gam_bc[:], Fp[0][:])),
        waits=[(sLD, sLD.n)])
    ew("sync", sDV, sDV.n)

    for i in range(DC):
        sl = slice(i * 128, (i + 1) * 128)
        dma(A_f[i][:], alog_h[sl, :])
        dma(cw_c[i][:], cw_h[sl, :])
        dma(cb_c[i][:], cb_h[sl].rearrange("(p one) -> p one", one=1))
        dma(bdt_c[i][:], bdt_h[sl].rearrange("(p one) -> p one", one=1))
        dma(dp_c[i][:], dp_h[sl].rearrange("(p one) -> p one", one=1))
        dma(Fp[1][:, 0:1], lnb_h[sl].rearrange("(p one) -> p one", one=1))
        run("vector", sDV, (lambda v, i=i: v.tensor_copy(beta_c[i][:], Fp[1][:, 0:1])),
            waits=[(sLD, sLD.n)])
        ew("sync", sDV, sDV.n)
    dma(bx_row, bx_h.rearrange("(one d) -> one d", one=1))
    dma(Fp[1][0:1, 0:D], fb_h.rearrange("(one d) -> one d", one=1))
    dma(da_t[:], da_h.rearrange("(one o) -> one o", one=1))
    run("vector", sDV, (lambda v: v.tensor_copy(fb_bf[:], Fp[1][0:1, 0:D])),
        waits=[(sLD, sLD.n)])

    def fA(s):
        last = None
        for i in range(DC):
            last = s.activation(Fp[2][:, i * NS:(i + 1) * NS], A_f[i][:], AF.Exp)
        return last
    run("scalar", sAC, fA, waits=[(sLD, sLD.n)])

    def fAn(v):
        last = None
        for i in range(DC):
            last = v.tensor_scalar(A_f[i][:], Fp[2][:, i * NS:(i + 1) * NS], -1.0, None, ALU.mult)
        return last
    run("vector", sDV, fAn, waits=[(sAC, sAC.n)])

    run("scalar", sAC, (lambda s: s.activation(da_t2[:], da_t[:], AF.Tanh, scale=4.0)),
        waits=[(sLD, sLD.n)])
    run("vector", sDV, (lambda v: v.tensor_scalar(da_t[:], da_t2[:], 0.5, None, ALU.mult)),
        waits=[(sAC, sAC.n)])
    ew("sync", sDV, sDV.n)
    dma(da_d[:], da_t[:])
    ew("sync", sLD, sLD.n)
    dma(da_col[:], da_d[0:1, :].broadcast_to([128, 1]))

    def fV0(t):
        last = None
        for i in range(DC):
            t.matmul(psWx[0:1, :], beta_c[i][:], wx_bf[i][:, 0:SPAN],
                     start=(i == 0), stop=(i == DC - 1))
        for i in range(DC):
            last = t.matmul(psZ[0:1, :], beta_c[i][:], wx_bf[i][:, SPAN:2 * SPAN],
                            start=(i == 0), stop=(i == DC - 1))
        return last
    run("tensor", sPE, fV0, waits=[(sDV, sDV.n), (sGP, sGP.n)])

    def fV0e(v):
        v.tensor_tensor(v0_fr[0:1, 0:SPAN], psWx[0:1, :], bx_row[0:1, 0:SPAN], ALU.add)
        v.tensor_tensor(v0_fr[0:1, SPAN:2 * SPAN], psZ[0:1, :], bx_row[0:1, SPAN:2 * SPAN], ALU.add)
        return v.tensor_copy(v0_bf[:], v0_fr[0:1, 0:D])
    run("vector", sDV, fV0e, waits=[(sPE, sPE.n), (sLD, sLD.n)])

    def quant_chain(pairs, dd_ap, scratch, d4_waits=()):
        npair = len(pairs)
        G = 32 * npair
        am, am2, l_t, eh, p2, m_t, fx, e_t = sq
        S1, S2, S3, S4 = scratch

        def d1(v, pairs=tuple(pairs), G=G):
            for p, (srcp, _) in enumerate(pairs):
                v.tensor_reduce(am[:, p * 32:(p + 1) * 32],
                                srcp.rearrange("p (a b) -> p a b", b=32),
                                AX.X, ALU.max, apply_absolute_value=True)
            v.drain()
            v.tensor_scalar(am2[:, 0:G], am[:, 0:G], 1e-30, None, ALU.max)
            v.drain()
            return v.nop()
        run("vector", sDV, d1)

        def a1(s, G=G):
            return s.activation(l_t[:, 0:G], am2[:, 0:G], AF.Ln)
        run("scalar", sAC, a1, waits=[(sDV, sDV.n)])

        def d2(v, G=G):
            v.tensor_scalar(eh[:, 0:G], l_t[:, 0:G], 1.0 / LN2C, M23, ALU.mult, ALU.add)
            v.drain()
            v.tensor_scalar(eh[:, 0:G], eh[:, 0:G], M23, None, ALU.subtract)
            v.drain()
            return v.nop()
        run("vector", sDV, d2, waits=[(sAC, sAC.n)])

        def a2(s, G=G):
            return s.activation(p2[:, 0:G], eh[:, 0:G], AF.Exp, scale=-LN2C)
        run("scalar", sAC, a2, waits=[(sDV, sDV.n)])

        def d3(v, G=G, dd=dd_ap):
            v.tensor_tensor(m_t[:, 0:G], am2[:, 0:G], p2[:, 0:G], ALU.mult)
            v.drain()
            v.tensor_scalar(fx[:, 0:G], m_t[:, 0:G], 1.0, None, ALU.is_lt)
            v.drain()
            v.tensor_tensor(e_t[:, 0:G], eh[:, 0:G], fx[:, 0:G], ALU.subtract)
            v.drain()
            if dd.shape[1] == 1:
                v.tensor_scalar(e_t[:, 0:G], e_t[:, 0:G], dd, None, ALU.add)
            else:
                v.tensor_tensor(e_t[:, 0:G], e_t[:, 0:G], dd, ALU.add)
            v.drain()
            return v.nop()
        run("vector", sDV, d3, waits=[(sAC, sAC.n)])

        def a3(s, G=G):
            s.activation(l_t[:, 0:G], e_t[:, 0:G], AF.Exp, scale=LN2C, bias=n2l2_c[:])
            return s.activation(p2[:, 0:G], e_t[:, 0:G], AF.Exp, scale=-LN2C, bias=p2l2_c[:])
        run("scalar", sAC, a3, waits=[(sDV, sDV.n)])

        def d4(v, pairs=tuple(pairs)):
            last = None
            for p, (srcp, dstp) in enumerate(pairs):
                x3 = srcp.rearrange("p (a b) -> p a b", b=32)
                isc = p2[:, p * 32:(p + 1) * 32].unsqueeze(2).broadcast_to([128, 32, 32])
                scl = l_t[:, p * 32:(p + 1) * 32].unsqueeze(2).broadcast_to([128, 32, 32])
                xs = S1
                v.tensor_tensor(xs.rearrange("p (a b) -> p a b", b=32), x3, isc, ALU.mult)
                v.tensor_scalar(xs, xs, 6.0, -6.0, ALU.min, ALU.max)
                aq = S2
                v.tensor_tensor(aq, xs, xs, ALU.mult)
                q1 = S3
                v.tensor_scalar(q1, aq, 4.0, None, ALU.is_ge)
                q4 = S2
                v.tensor_scalar(q4, aq, 16.0, None, ALU.is_ge)
                tq = S4
                v.scalar_tensor_tensor(tq, q4, 0.5, q1, ALU.mult, ALU.add)
                stp = S3
                v.scalar_tensor_tensor(stp, q1, 0.5, q4, ALU.mult, ALU.add)
                v.tensor_scalar(stp, stp, 0.5, None, ALU.add)
                rstp = S4
                v.tensor_scalar(rstp, tq, -1.0, 2.0, ALU.mult, ALU.add)
                qt = S2
                v.tensor_tensor(qt, xs, rstp, ALU.mult)
                v.tensor_scalar(qt, qt, M23, None, ALU.add)
                v.tensor_scalar(qt, qt, M23, None, ALU.subtract)
                v.tensor_tensor(qt, qt, stp, ALU.mult)
                last = v.tensor_tensor(dstp.rearrange("p (a b) -> p a b", b=32),
                                       qt.rearrange("p (a b) -> p a b", b=32), scl, ALU.mult)
            return last
        run("vector", sDV, d4, waits=[(sAC, sAC.n)] + list(d4_waits))

    for oc in range(DC):
        dma(Fp[0][:], fw_h[oc * 128:(oc + 1) * 128, :])
        dma(dwq[:], dw_h[oc * 128:(oc + 1) * 128, :])
        run("scalar", sAC, (lambda s: s.activation(sq[1][:, 0:32], dwq[:], AF.Tanh, scale=4.0)),
            waits=[(sLD, sLD.n)])
        run("vector", sDV, (lambda v: v.tensor_scalar(dwq[:], sq[1][:, 0:32], 0.5, None, ALU.mult)),
            waits=[(sAC, sAC.n)])
        wq_dst = bigs[0][:, 0:D]
        quant_chain([(Fp[0][:], wq_dst)], dwq[:], (Fp[1][:], Fp[2][:], Fp[3][:], Fp[4][:]),
                    d4_waits=[(sPE, sPE.n)])

        def t_wq(t, src=wq_dst):
            last = None
            for g in range(2):
                ps = psA if g == 0 else psB
                for q in range(4):
                    dcx = g * 4 + q
                    last = t.transpose(ps[:, q * 128:(q + 1) * 128],
                                       src[:, dcx * 128:(dcx + 1) * 128], ident[:])
            return last
        run("tensor", sPE, t_wq, waits=[(sDV, sDV.n)])

        def e_wq(v, oc=oc):
            last = None
            for g in range(2):
                ps = psA if g == 0 else psB
                for q in range(4):
                    dcx = g * 4 + q
                    last = v.tensor_copy(wqT[dcx][:, oc * 128:(oc + 1) * 128],
                                         ps[:, q * 128:(q + 1) * 128])
            return last
        run("vector", sDV, e_wq, waits=[(sPE, sPE.n)])
        ew("sync", sDV, sDV.n)

    def ln_smalls(col_n, c0=0):
        def d(v, n=col_n, c0=c0):
            sl = slice(c0, c0 + n)
            v.tensor_scalar(mu_s[:, sl], sumx_s[:, sl], 1.0 / D, None, ALU.mult)
            v.tensor_scalar(m2_s[:, sl], sumq_s[:, sl], 1.0 / D, None, ALU.mult)
            v.drain()
            v.tensor_tensor(t4a_s[:, sl], mu_s[:, sl], mu_s[:, sl], ALU.mult)
            v.drain()
            v.tensor_tensor(t4b_s[:, sl], m2_s[:, sl], t4a_s[:, sl], ALU.subtract)
            v.drain()
            return v.nop()
        run("vector", sDV, d, waits=[(sAC, sAC.n)])

        def a(s, n=col_n, c0=c0):
            sl = slice(c0, c0 + n)
            s.activation(t4a_s[:, sl], t4b_s[:, sl], AF.Ln, bias=eps_c[:])
            s.drain()
            s.activation(rstd_s[:, sl], t4a_s[:, sl], AF.Exp, scale=-0.5)
            s.drain()
            return s.nop()
        run("scalar", sAC, a, waits=[(sDV, sDV.n)])

    for b in range(NB):
        for sp in range(NSPAN):
            t0 = sp * SPAN
            ew("sync", sDV, sDV.n)
            ew("sync", sAC, sAC.n)
            for q in range(TPS):
                row = b * T + t0 + q * 128
                dma(Fp[q][:], f_h[row:row + 128, :])
            ld_all = sLD.n
            def sq1(s):
                last = None
                for q in range(TPS):
                    s.activation(Fp[4][:], Fp[q][:], AF.Copy, accum_out=sumx_s[:, q:q + 1])
                    last = s.activation(Fp[4][:], Fp[q][:], AF.Square,
                                        accum_out=sumq_s[:, q:q + 1])
                return last
            run("scalar", sAC, sq1, waits=[(sLD, ld_all), (sDV, sDV.n)])
            ln_smalls(TPS)
            def ap1(v):
                last = None
                for q in range(TPS):
                    v.tensor_scalar(Fp[q][:], Fp[q][:], mu_s[:, q:q + 1], None, ALU.subtract)
                    last = v.tensor_scalar(Fp[q][:], Fp[q][:], rstd_s[:, q:q + 1], None, ALU.mult)
                return last
            run("vector", sDV, ap1, waits=[(sAC, sAC.n)])
            def sq2(s):
                last = None
                for q in range(TPS):
                    s.activation(Fp[4][:], Fp[q][:], AF.Copy, accum_out=sumx_s[:, q:q + 1])
                    last = s.activation(Fp[4][:], Fp[q][:], AF.Square,
                                        accum_out=sumq_s[:, q:q + 1])
                return last
            run("scalar", sAC, sq2, waits=[(sDV, sDV.n)])
            ln_smalls(TPS)
            def ap2(v):
                last = None
                for q in range(TPS):
                    v.tensor_scalar(Fp[q][:], Fp[q][:], mu_s[:, q:q + 1], None, ALU.subtract)
                    v.tensor_scalar(Fp[q][:], Fp[q][:], rstd_s[:, q:q + 1], None, ALU.mult)
                    last = v.tensor_tensor(fng_t[q][:], Fp[q][:], gam_bc[:], ALU.mult)
                return last
            run("vector", sDV, ap2, waits=[(sAC, sAC.n)])

            fng_done = sDV.n
            evac_marks = {}
            for dcx in range(DC):
                ps = psA if dcx % 2 == 0 else psB
                waits = [(sDV, fng_done)]
                if dcx >= 2:
                    waits.append((sDV, evac_marks[dcx - 2]))

                def tr(t, dcx=dcx, ps=ps):
                    last = None
                    for q in range(TPS):
                        last = t.transpose(ps[:, q * 128:(q + 1) * 128],
                                           fng_t[q][:, dcx * 128:(dcx + 1) * 128], ident[:])
                    return last
                run("tensor", sPE, tr, waits=waits)

                def ev(v, dcx=dcx, ps=ps):
                    return v.tensor_copy(fnT[dcx], ps[:])
                run("vector", sDV, ev, waits=[(sPE, sPE.n)])
                evac_marks[dcx] = sDV.n

            if sp == 0:
                def cc(v):
                    last = None
                    for jc in range(DC):
                        last = v.memset(x1buf[jc][:, 0:3], 0.0)
                    return last
                run("vector", sDV, cc)
            else:
                def cc(v):
                    last = None
                    for jc in range(DC):
                        last = v.tensor_copy(x1buf[jc][:, 0:3], x1buf[jc][:, SPAN:SPAN + 3])
                    return last
                run("vector", sDV, cc)

            fnT_done = sDV.n
            wx_evac = {}
            cvsets = [(a_t[:, 0:SPAN], a_t[:, SPAN:2 * SPAN]),
                      (a_t[:, 2 * SPAN:3 * SPAN], a_t[:, 3 * SPAN:4 * SPAN])]

            def emit_cnv_sil(jc):
                cvA, cvB = cvsets[jc % 2]
                def cnv(v, jc=jc, cvA=cvA, cvB=cvB):
                    v.tensor_scalar(cvA, x1buf[jc][:, 0:SPAN], cw_c[jc][:, 0:1], None, ALU.mult)
                    v.scalar_tensor_tensor(cvB, x1buf[jc][:, 1:1 + SPAN], cw_c[jc][:, 1:2],
                                           cvA, ALU.mult, ALU.add)
                    v.tensor_scalar(cvA, x1buf[jc][:, 2:2 + SPAN], cw_c[jc][:, 2:3], None, ALU.mult)
                    v.tensor_tensor(cvA, cvA, cvB, ALU.add)
                    return v.scalar_tensor_tensor(cvB, x1buf[jc][:, 3:3 + SPAN], cw_c[jc][:, 3:4],
                                                  cvA, ALU.mult, ALU.add)
                run("vector", sDV, cnv, waits=[(sAC, wx_evac[jc])])
                def sil(s, jc=jc, t0=t0, cvB=cvB):
                    return s.activation(xcy[jc][:, t0:t0 + SPAN], cvB, AF.Silu, bias=cb_c[jc][:])
                run("scalar", sAC, sil, waits=[(sDV, sDV.n)])

            for jc in range(DC):
                ps = psWx if jc % 2 == 0 else psZ
                waits = [(sDV, fnT_done)]
                if jc >= 2:
                    waits.append((sAC, wx_evac[jc - 2]))
                else:
                    waits.append((sAC, sAC.n))

                def mmwx(t, jc=jc, ps=ps):
                    for dcx in range(DC):
                        t.matmul(ps[:], wx_bf[dcx][:, jc * 128:(jc + 1) * 128], fnT[dcx],
                                 start=(dcx == 0), stop=False)
                    return t.matmul(ps[:], v0_bf[0:1, jc * 128:(jc + 1) * 128], ones512[0:1, :],
                                    start=False, stop=True)
                run("tensor", sPE, mmwx, waits=waits)

                def evx(s, jc=jc, ps=ps):
                    return s.activation(x1buf[jc][:, 3:3 + SPAN], ps[:], AF.Copy)
                run("scalar", sAC, evx, waits=[(sPE, sPE.n), (sDV, sDV.n)])
                wx_evac[jc] = sAC.n
                if jc >= 1:
                    emit_cnv_sil(jc - 1)
            emit_cnv_sil(DC - 1)

            def mmxp(t, t0=t0):
                last = None
                for dcx in range(DC):
                    last = t.matmul(psXd[:], wxp_bf[dcx][:], xcy[dcx][:, t0:t0 + SPAN],
                                    start=(dcx == 0), stop=(dcx == DC - 1))
                return last
            run("tensor", sPE, mmxp, waits=[(sAC, sAC.n)])

            run("scalar", sAC, (lambda s: s.activation(dtT, psXd[0:DTR, :], AF.Copy)),
                waits=[(sPE, sPE.n)])
            run("vector", sDV, (lambda v: v.tensor_copy(bcsb, psXd[DTR:96, :])),
                waits=[(sPE, sPE.n), (sLD, sLD.n)])
            ew("sync", sDV, sDV.n)
            def st_bc(e, b=b, t0=t0):
                e.dma_start(out=bcd[b][:, t0:t0 + SPAN], in_=bcsb).then_inc(sLD.h, 16)
            P.add("sync", st_bc)
            sLD.n += 16

            sp_done = sAC.n
            sp_evac = {}
            for dcx in range(DC):
                ps = psWx if dcx % 2 == 0 else psZ
                waits = [(sAC, sp_done)]
                if dcx >= 2:
                    waits.append((sAC, sp_evac[dcx - 2]))

                def mmdt(t, dcx=dcx, ps=ps):
                    return t.matmul(ps[:], wdt_bf[:, dcx * 128:(dcx + 1) * 128], dtT,
                                    start=True, stop=True)
                run("tensor", sPE, mmdt, waits=waits)

                def spa(s, dcx=dcx, ps=ps):
                    return s.activation(Fp[0][:, 0:SPAN], ps[:], AF.Copy)
                run("scalar", sAC, spa, waits=[(sPE, sPE.n), (sDV, sDV.n)])
                sp_evac[dcx] = sAC.n

                def spd(v, dcx=dcx):
                    v.tensor_scalar(Fp[1][:, 0:SPAN], Fp[0][:, 0:SPAN], bdt_c[dcx][:],
                                    0.0, ALU.add, ALU.max)
                    v.tensor_scalar(Fp[2][:, 0:SPAN], Fp[0][:, 0:SPAN], bdt_c[dcx][:],
                                    0.0, ALU.add, ALU.min)
                    return v.tensor_tensor(Fp[4][:, 0:SPAN], Fp[2][:, 0:SPAN],
                                           Fp[1][:, 0:SPAN], ALU.subtract)
                run("vector", sDV, spd, waits=[(sAC, sAC.n)])

                def spe(s):
                    s.activation(Fp[2][:, 0:SPAN], Fp[4][:, 0:SPAN], AF.Exp)
                    return s.activation(Fp[3][:, 0:SPAN], Fp[2][:, 0:SPAN], AF.Ln, bias=one_c[:])
                run("scalar", sAC, spe, waits=[(sDV, sDV.n)])

                def spf(v, dcx=dcx, t0=t0):
                    return v.tensor_tensor(dltT[dcx][:, t0:t0 + SPAN], Fp[3][:, 0:SPAN],
                                           Fp[1][:, 0:SPAN], ALU.add)
                run("vector", sDV, spf, waits=[(sAC, sAC.n)])

        def s2init(v):
            for dcx in range(DC):
                v.tensor_tensor(u_t[dcx][:], dltT[dcx][:], xcy[dcx][:], ALU.mult)
            last = None
            for dcx in range(DC):
                last = v.tensor_scalar(xcy[dcx][:], xcy[dcx][:], dp_c[dcx][:], None, ALU.mult)
            return last
        run("vector", sDV, s2init)
        ew("scalar", sDV, sDV.n)

        psy8 = [psWx, psZ, psF0, psF1, psA_f, psB_f, psXd_f, psY0]
        a_bufs = [Fp[0][:, 0:SPAN].bitcast(BF16), Fp[0][:, SPAN:2 * SPAN].bitcast(BF16),
                  Cbc[:, 0:T // 2], Cbc[:, T // 2:T]]
        z_bufs = [Fp[1][:, 0:SPAN].bitcast(BF16), Fp[1][:, SPAN:2 * SPAN].bitcast(BF16)]
        HT = T // 2
        bc_bufs = [(Fp[2][:, 0:SPAN].bitcast(BF16), Fp[3][:, 0:SPAN].bitcast(BF16)),
                   (Fp[2][:, SPAN:2 * SPAN].bitcast(BF16), Fp[3][:, SPAN:2 * SPAN].bitcast(BF16)),
                   (Fp[4][:, 0:SPAN].bitcast(BF16), Fp[4][:, SPAN:2 * SPAN].bitcast(BF16)),
                   (b_t[:, HT:2 * HT], h_t[:, HT:2 * HT])]
        NPF = 4
        if b == 0:
            ew("sync", sPE, sPE.n)
        ew("sync", sLD, sLD.n)
        ew("sync", sAC, sAC.n)
        ew("scalar", sDV, sDV.n)
        ew("scalar", sPE, sPE.n)
        s2init_dv = sDV.n
        for dcx in range(DC):
            for half in range(2):
                ht0 = half * HT
                ew("sync", sDV, sDV.n)
                bc_ld = {}
                for pf in range(NPF):
                    dma(bc_bufs[pf][0], bcd[b][pf:pf + 1, ht0:ht0 + HT].broadcast_to([128, HT]))
                    dma(bc_bufs[pf][1], bcd[b][16 + pf:17 + pf, ht0:ht0 + HT].broadcast_to([128, HT]))
                    bc_ld[pf] = sLD.n

                def tinit(t, dcx=dcx, ht0=ht0):
                    last = None
                    for q in range(8):
                        last = t.matmul(psy8[q][:, 0:128],
                                        xcy[dcx][:, ht0 + q * 128:ht0 + (q + 1) * 128],
                                        ident[:], start=True, stop=False)
                    return last
                run("tensor", sPE, tinit, waits=[(sDV, s2init_dv), (sAC, sAC.n)])

                scn_marks = {}
                zt_marks = {}
                a_marks = {}
                for n in range(NS):
                    aw = []
                    if n >= 4:
                        aw.append((sDV, scn_marks[n - 4]))
                    def abuild(s, n=n, dcx=dcx, ht0=ht0):
                        return s.activation(a_bufs[n % 4], dltT[dcx][:, ht0:ht0 + HT],
                                            AF.Copy, scale=A_f[dcx][:, n:n + 1])
                    run("scalar", sAC, abuild, waits=aw)
                    a_marks[n] = sAC.n

                    dw = [(sLD, bc_ld[n]), (sAC, a_marks[n])]
                    if n >= 2:
                        dw.append((sPE, zt_marks[n - 2]))
                    def scn(v, n=n, dcx=dcx, half=half, ht0=ht0):
                        Bb, Cb = bc_bufs[n % NPF]
                        v.tensor_tensor(b_t[:, 0:HT], u_t[dcx][:, ht0:ht0 + HT], Bb, ALU.mult)
                        init = 0.0 if half == 0 else hmid[:, n:n + 1]
                        v.tensor_tensor_scan(h_t[:, 0:HT], a_bufs[n % 4], b_t[:, 0:HT],
                                             init, ALU.mult, ALU.add)
                        last = v.tensor_tensor(z_bufs[n % 2], h_t[:, 0:HT], Cb, ALU.mult)
                        if half == 0:
                            last = v.tensor_copy(hmid[:, n:n + 1], h_t[:, HT - 1:HT])
                        return last
                    run("vector", sDV, scn, waits=dw)
                    scn_marks[n] = sDV.n

                    def zt(t, n=n):
                        last = None
                        for q in range(8):
                            last = t.matmul(psy8[q][:, 0:128],
                                            z_bufs[n % 2][:, q * 128:(q + 1) * 128], ident[:],
                                            start=False, stop=(n == NS - 1))
                        return last
                    run("tensor", sPE, zt, waits=[(sDV, scn_marks[n])])
                    zt_marks[n] = sPE.n

                    if n + NPF < NS:
                        ew("sync", sDV, scn_marks[n])
                        dma(bc_bufs[n % NPF][0],
                            bcd[b][n + NPF:n + NPF + 1, ht0:ht0 + HT].broadcast_to([128, HT]))
                        dma(bc_bufs[n % NPF][1],
                            bcd[b][16 + n + NPF:17 + n + NPF, ht0:ht0 + HT].broadcast_to([128, HT]))
                        bc_ld[n + NPF] = sLD.n

                def yev(s, dcx=dcx, half=half):
                    last = None
                    for q in range(8):
                        tt = half * 8 + q
                        last = s.activation(ytm_t[tt][:, dcx * 128:(dcx + 1) * 128],
                                            psy8[q][:, 0:128], AF.Copy)
                    return last
                run("scalar", sAC, yev, waits=[(sPE, sPE.n)])

        ew("vector", sAC, sAC.n)
        x0b4 = [xcy[2 + q // 2][:, (q % 2) * D:(q % 2 + 1) * D] for q in range(4)]
        ln34 = [xcy[4 + q // 2][:, (q % 2) * D:(q % 2 + 1) * D] for q in range(4)]
        q_scr = (b_t[:].bitcast(F32), h_t[:].bitcast(F32), a_t[:].bitcast(F32), Fp[4][:])
        for sp in range(NSPAN):
            t0 = sp * SPAN
            ew("sync", sDV, sDV.n)
            ew("sync", sAC, sAC.n)
            for q in range(TPS):
                row = b * T + t0 + q * 128
                dma(Fp[q][:], f_h[row:row + 128, :])
            ld_all = sLD.n
            def sq3(s, sp=sp):
                last = None
                for q in range(TPS):
                    s.activation(Fp[4][:], ytm_t[sp * TPS + q][:], AF.Copy,
                                 accum_out=sumx_s[:, q:q + 1])
                    last = s.activation(Fp[4][:], ytm_t[sp * TPS + q][:], AF.Square,
                                        accum_out=sumq_s[:, q:q + 1])
                return last
            run("scalar", sAC, sq3, waits=[(sDV, sDV.n)])
            ln_smalls(TPS)
            def ap3(v, sp=sp):
                last = None
                for q in range(TPS):
                    tt = sp * TPS + q
                    v.tensor_scalar(ln34[q], ytm_t[tt][:], mu_s[:, q:q + 1], None, ALU.subtract)
                    last = v.tensor_scalar(ln34[q], ln34[q], rstd_s[:, q:q + 1], None, ALU.mult)
                return last
            run("vector", sDV, ap3, waits=[(sAC, sAC.n)])
            def cst(s):
                last = None
                for q in range(TPS):
                    last = s.activation(x0b4[q], Fp[q][:], AF.Copy)
                return last
            run("scalar", sAC, cst, waits=[(sLD, ld_all), (sPE, sPE.n)])
            def res1(v, sp=sp):
                last = None
                for q in range(TPS):
                    last = v.tensor_tensor(x1_t[sp * TPS + q][:], ln34[q], x0b4[q], ALU.add)
                return last
            run("vector", sDV, res1, waits=[(sAC, sAC.n)])
            def sq4(s, sp=sp):
                last = None
                for q in range(TPS):
                    s.activation(Fp[4][:], x1_t[sp * TPS + q][:], AF.Copy,
                                 accum_out=sumx_s[:, q:q + 1])
                    last = s.activation(Fp[4][:], x1_t[sp * TPS + q][:], AF.Square,
                                        accum_out=sumq_s[:, q:q + 1])
                return last
            run("scalar", sAC, sq4, waits=[(sDV, sDV.n)])
            ln_smalls(TPS)
            def ap4(v, sp=sp):
                last = None
                for q in range(TPS):
                    tt = sp * TPS + q
                    v.tensor_scalar(Fp[q][:], x1_t[tt][:], mu_s[:, q:q + 1], None, ALU.subtract)
                    last = v.tensor_scalar(Fp[q][:], Fp[q][:], rstd_s[:, q:q + 1], None, ALU.mult)
                return last
            run("vector", sDV, ap4, waits=[(sAC, sAC.n)])
            quant_chain([(Fp[q][:], xq_t[q][:]) for q in range(TPS)], da_col[:],
                        q_scr, d4_waits=[(sPE, sPE.n)])

            xq_done = sDV.n
            evac_marks = {}
            for dcx in range(DC):
                ps = psA if dcx % 2 == 0 else psB
                waits = [(sDV, xq_done)]
                if dcx >= 2:
                    waits.append((sDV, evac_marks[dcx - 2]))

                def trq(t, dcx=dcx, ps=ps):
                    last = None
                    for q in range(TPS):
                        last = t.transpose(ps[:, q * 128:(q + 1) * 128],
                                           xq_t[q][:, dcx * 128:(dcx + 1) * 128], ident[:])
                    return last
                run("tensor", sPE, trq, waits=waits)

                def evq(v, dcx=dcx, ps=ps):
                    return v.tensor_copy(fnT[dcx], ps[:])
                run("vector", sDV, evq, waits=[(sPE, sPE.n)])
                evac_marks[dcx] = sDV.n

            xqT_done = sDV.n
            evf_marks = {}
            for q in range(TPS):
                tt = sp * TPS + q
                row = b * T + t0 + q * 128
                pF0, pF1 = (psF0, psF1) if q % 2 == 0 else (psA_f, psB_f)
                waits = [(sDV, xqT_done), (sAC, sAC.n if q < 2 else evf_marks[q - 2])]

                def mmf(t, q=q, pF0=pF0, pF1=pF1):
                    for dcx in range(DC):
                        sl = slice(q * 128, (q + 1) * 128)
                        t.matmul(pF0[:], fnT[dcx][:, sl], wqT[dcx][:, 0:SPAN],
                                 start=(dcx == 0), stop=False)
                        t.matmul(pF1[:], fnT[dcx][:, sl], wqT[dcx][:, SPAN:2 * SPAN],
                                 start=(dcx == 0), stop=False)
                    t.matmul(pF0[:], ones512[0:1, 0:128], fb_bf[0:1, 0:SPAN],
                             start=False, stop=True)
                    return t.matmul(pF1[:], ones512[0:1, 0:128], fb_bf[0:1, SPAN:2 * SPAN],
                                    start=False, stop=True)
                run("tensor", sPE, mmf, waits=waits)

                def evf(s, q=q, pF0=pF0, pF1=pF1):
                    s.activation(Fp[q][:, 0:SPAN], pF0[:], AF.Copy, accum_out=t4a_s[:, q:q + 1])
                    s.activation(Fp[q][:, SPAN:2 * SPAN], pF1[:], AF.Copy,
                                 accum_out=t4b_s[:, q:q + 1])
                    return s.activation(b_t[:].bitcast(F32), Fp[q][:], AF.Square,
                                        accum_out=sumq_s[:, q:q + 1])
                run("scalar", sAC, evf, waits=[(sPE, sPE.n)])
                evf_marks[q] = sAC.n

                def evs(v, q=q):
                    v.tensor_tensor(sumx_s[:, q:q + 1], t4a_s[:, q:q + 1],
                                    t4b_s[:, q:q + 1], ALU.add)
                    v.drain()
                    return v.nop()
                run("vector", sDV, evs, waits=[(sAC, evf_marks[q])])
                ln_smalls(1, c0=q)
                if sST.n > 0:
                    ew("vector", sST, sST.n)
                def ap5(v, tt=tt, q=q):
                    scr5 = h_t[:].bitcast(F32)
                    v.tensor_scalar(scr5, Fp[q][:], mu_s[:, q:q + 1], None, ALU.subtract)
                    v.tensor_scalar(scr5, scr5, rstd_s[:, q:q + 1], None, ALU.mult)
                    return v.tensor_tensor(out_stg[:], scr5, x1_t[tt][:], ALU.add)
                run("vector", sDV, ap5, waits=[(sAC, sAC.n)])
                ew("sync", sDV, sDV.n)
                dma_store(out_h[row:row + 128, :], out_stg[:])

    def fin(e):
        e.wait_ge(sST.h, sST.n)
    P.add("sync", fin)

    with nc.Block() as blk:
        blk.sync(lambda e: [fn(e) for fn in P.items["sync"]])
        blk.vector(lambda e: [fn(e) for fn in P.items["vector"]])
        blk.scalar(lambda e: [fn(e) for fn in P.items["scalar"]])
        blk.tensor(lambda e: [fn(e) for fn in P.items["tensor"]])
        blk.gpsimd(lambda e: [fn(e) for fn in P.items["gpsimd"]])
    es.close()
    return nc


def kernel(L0_attn_f, L0_attn_ln_g, L0_attn_ln_b, L0_attn_Wx, L0_attn_bx,
           L0_attn_conv_w, L0_attn_conv_b, L0_attn_Wxp, L0_attn_Wdt, L0_attn_bdt,
           L0_attn_A_log, L0_attn_Dp, L0_ffn_weight, L0_ffn_bias, L0_ffn_delta_w,
           L0_ffn_delta_act):
    if "nc" not in _CACHE:
        _CACHE["nc"] = build_nc()
    nc = _CACHE["nc"]
    f = np.ascontiguousarray(np.asarray(L0_attn_f, dtype=np.float32))
    common = {
        "ln_g": np.ascontiguousarray(L0_attn_ln_g, dtype=np.float32),
        "ln_b": np.ascontiguousarray(L0_attn_ln_b, dtype=np.float32),
        "Wx": np.ascontiguousarray(L0_attn_Wx, dtype=np.float32),
        "bx": np.ascontiguousarray(L0_attn_bx, dtype=np.float32),
        "conv_w": np.ascontiguousarray(L0_attn_conv_w, dtype=np.float32),
        "conv_b": np.ascontiguousarray(L0_attn_conv_b, dtype=np.float32),
        "Wxp": np.ascontiguousarray(L0_attn_Wxp, dtype=np.float32),
        "Wdt": np.ascontiguousarray(L0_attn_Wdt, dtype=np.float32),
        "bdt": np.ascontiguousarray(L0_attn_bdt, dtype=np.float32),
        "A_log": np.ascontiguousarray(L0_attn_A_log, dtype=np.float32),
        "Dp": np.ascontiguousarray(L0_attn_Dp, dtype=np.float32),
        "fw": np.ascontiguousarray(L0_ffn_weight, dtype=np.float32),
        "fb": np.ascontiguousarray(L0_ffn_bias, dtype=np.float32),
        "dw": np.ascontiguousarray(L0_ffn_delta_w, dtype=np.float32),
        "da": np.ascontiguousarray(L0_ffn_delta_act, dtype=np.float32),
    }
    in_maps = []
    for c in range(8):
        m = dict(common)
        m["f"] = np.ascontiguousarray(f[2 * c:2 * c + 2].reshape(NB * T, D))
        in_maps.append(m)
    res = run_bass_kernel_spmd(nc, in_maps, core_ids=list(range(8)))
    out = np.concatenate([r["out"].reshape(NB, T, D) for r in res.results], axis=0)
    return np.ascontiguousarray(out, dtype=np.float32)


# version 07
# speedup vs baseline: 1.0699x; 1.0025x over previous
import numpy as np
from contextlib import ExitStack
import concourse.bass as bass
import concourse.mybir as mybir
from concourse.bass_utils import run_bass_kernel_spmd

F32 = mybir.dt.float32
BF16 = mybir.dt.bfloat16
ALU = mybir.AluOpType
AF = mybir.ActivationFunctionType
AX = mybir.AxisListType

D = 1024
T = 2048
NB = 2
NS = 16
DTR = 64
DC = 8
SPAN = 512
NSPAN = T // SPAN
TPS = SPAN // 128
LN2C = float(np.log(2.0))
EPS = 1e-5
M23 = float(1.5 * 2.0 ** 23)

_CACHE = {}


class Sem:
    def __init__(self, nc, es, name):
        self.h = es.enter_context(nc.semaphore(name))
        self.n = 0


class Prog:
    def __init__(self):
        self.items = {"sync": [], "vector": [], "scalar": [], "tensor": [], "gpsimd": []}

    def add(self, eng, fn):
        self.items[eng].append(fn)


def build_nc():
    nc = bass.Bass("TRN2", target_bir_lowering=False, debug=False,
                   detect_race_conditions=False)

    f_h = nc.dram_tensor("f", [NB * T, D], F32, kind="ExternalInput").ap()
    lng_h = nc.dram_tensor("ln_g", [D], F32, kind="ExternalInput").ap()
    lnb_h = nc.dram_tensor("ln_b", [D], F32, kind="ExternalInput").ap()
    wx_h = nc.dram_tensor("Wx", [D, D], F32, kind="ExternalInput").ap()
    bx_h = nc.dram_tensor("bx", [D], F32, kind="ExternalInput").ap()
    cw_h = nc.dram_tensor("conv_w", [D, 4], F32, kind="ExternalInput").ap()
    cb_h = nc.dram_tensor("conv_b", [D], F32, kind="ExternalInput").ap()
    wxp_h = nc.dram_tensor("Wxp", [D, 96], F32, kind="ExternalInput").ap()
    wdt_h = nc.dram_tensor("Wdt", [DTR, D], F32, kind="ExternalInput").ap()
    bdt_h = nc.dram_tensor("bdt", [D], F32, kind="ExternalInput").ap()
    alog_h = nc.dram_tensor("A_log", [D, NS], F32, kind="ExternalInput").ap()
    dp_h = nc.dram_tensor("Dp", [D], F32, kind="ExternalInput").ap()
    fw_h = nc.dram_tensor("fw", [D, D], F32, kind="ExternalInput").ap()
    fb_h = nc.dram_tensor("fb", [D], F32, kind="ExternalInput").ap()
    dw_h = nc.dram_tensor("dw", [D, 32], F32, kind="ExternalInput").ap()
    da_h = nc.dram_tensor("da", [1], F32, kind="ExternalInput").ap()
    out_h = nc.dram_tensor("out", [NB * T, D], F32, kind="ExternalOutput").ap()

    bcd = [nc.dram_tensor(f"bcd{b}", [32, T], BF16, kind="Internal").ap() for b in range(NB)]
    da_d = nc.dram_tensor("da_d", [1, 1], F32, kind="Internal").ap()

    es = ExitStack()
    S = lambda name, shape, dt: es.enter_context(nc.sbuf_tensor(name, shape, dt))

    wx_bf = [S(f"wx{i}", [128, D], BF16) for i in range(DC)]
    wqT = [S(f"wqT{i}", [128, D], BF16) for i in range(DC)]
    wxp_bf = [S(f"wxp{i}", [128, 96], BF16) for i in range(DC)]
    wdt_bf = S("wdt", [DTR, D], BF16)
    gam_bc = S("gam", [128, D], BF16)
    v0_bf = S("v0", [1, D], BF16)
    fb_bf = S("fbias", [1, D], BF16)
    ones512 = S("ones512", [1, SPAN], BF16)
    ident = S("ident", [128, 128], BF16)
    A_f = [S(f"A{i}", [128, NS], F32) for i in range(DC)]
    cw_c = [S(f"cw{i}", [128, 4], F32) for i in range(DC)]
    cb_c = [S(f"cb{i}", [128, 1], F32) for i in range(DC)]
    bdt_c = [S(f"bdtc{i}", [128, 1], F32) for i in range(DC)]
    dp_c = [S(f"dpc{i}", [128, 1], F32) for i in range(DC)]
    beta_c = [S(f"betac{i}", [128, 1], BF16) for i in range(DC)]
    eps_c = S("epsc", [128, 1], F32)
    one_c = S("onec", [128, 1], F32)
    n2l2_c = S("n2l2", [128, 1], F32)
    d2eps_c = S("d2eps", [128, 1], F32)
    lnd_c = S("lndc", [128, 1], F32)
    p2l2_c = S("p2l2", [128, 1], F32)
    da_col = S("dacol", [128, 1], F32)
    da_t = S("dat", [1, 1], F32)
    da_t2 = S("dat2", [1, 1], F32)

    Fp = [S(f"Fp{i}", [128, D], F32) for i in range(5)]
    out_stg = S("out_stg", [128, D], F32)
    bigs = [S(f"big{i}", [128, T], BF16) for i in range(24)]
    dltT = bigs[0:8]
    u_t = bigs[8:16]
    xcy = bigs[16:24]
    fng_t = [bigs[8 + i // 2][:, (i % 2) * D:(i % 2 + 1) * D] for i in range(4)]
    x1_t = [bigs[tt // 2][:, (tt % 2) * D:(tt % 2 + 1) * D] for tt in range(16)]
    ytm_t = [bigs[8 + tt // 2][:, (tt % 2) * D:(tt % 2 + 1) * D] for tt in range(16)]
    xq_t = [bigs[16 + i // 2][:, (i % 2) * D:(i % 2 + 1) * D] for i in range(4)]
    ln3_half = bigs[18][:, 0:D]
    x0b_half = bigs[18][:, D:2 * D]

    x1buf = [S(f"x1b{i}", [128, SPAN + 3], BF16) for i in range(DC)]
    Bbc = S("Bbc", [128, T], BF16)
    Cbc = S("Cbc", [128, T], BF16)
    a_t = S("a_t", [128, T], BF16)
    b_t = S("b_t", [128, T], BF16)
    h_t = S("h_t", [128, T], BF16)
    fnT = [Bbc[:, i * SPAN:(i + 1) * SPAN] for i in range(4)] + \
          [Cbc[:, i * SPAN:(i + 1) * SPAN] for i in range(4)]
    dtT = h_t[0:DTR, 0:SPAN]
    bcsb = b_t[0:32, 0:SPAN]
    cv0 = a_t[:, 0:SPAN]
    cv1 = a_t[:, SPAN:2 * SPAN]
    v0_fr = Fp[4][0:1, 0:D]
    bx_row = Fp[3][0:1, 0:D]

    sm4 = [S(f"sm4_{i}", [128, 4], F32) for i in range(8)]
    sumx_s, sumq_s, mu_s, m2_s, rstd_s, nmr_s, t4a_s, t4b_s = sm4
    sq = [S(f"sq{i}", [128, 128], F32) for i in range(8)]
    hmid = S("hmid", [128, NS], F32)
    dwq = S("dwq", [128, 32], F32)

    psA_f = es.enter_context(nc.psum_tensor("psA", [128, SPAN], F32))
    psB_f = es.enter_context(nc.psum_tensor("psB", [128, SPAN], F32))
    psA = psA_f[:].bitcast(BF16)[:, 0:SPAN]
    psB = psB_f[:].bitcast(BF16)[:, 0:SPAN]
    psXd_f = es.enter_context(nc.psum_tensor("psXd", [128, SPAN], F32))
    psXd = psXd_f[0:96, :]
    psWx = es.enter_context(nc.psum_tensor("psWx", [128, SPAN], F32))
    psZ = es.enter_context(nc.psum_tensor("psZ", [128, SPAN], F32))
    psF0 = es.enter_context(nc.psum_tensor("psF0", [128, SPAN], F32))
    psF1 = es.enter_context(nc.psum_tensor("psF1", [128, SPAN], F32))
    psY0 = es.enter_context(nc.psum_tensor("psY0", [128, SPAN], F32))

    sLD = Sem(nc, es, "sLD")
    sPE = Sem(nc, es, "sPE")
    sDV = Sem(nc, es, "sDV")
    sAC = Sem(nc, es, "sAC")
    sGP = Sem(nc, es, "sGP")
    sST = Sem(nc, es, "sST")

    P = Prog()

    def dma(dst, src):
        def fn(e, dst=dst, src=src):
            e.dma_start(out=dst, in_=src).then_inc(sLD.h, 16)
        P.add("sync", fn)
        sLD.n += 16

    def dma_store(dst, src):
        def fn(e, dst=dst, src=src):
            e.dma_start(out=dst, in_=src).then_inc(sST.h, 16)
        P.add("sync", fn)
        sST.n += 16

    def ew(eng, sem, cnt):
        if cnt <= 0:
            return
        def fn(e, s=sem.h, c=cnt):
            e.wait_ge(s, c)
        P.add(eng, fn)

    ENG2SEM = {}

    def run(eng, sem, fn, waits=()):
        for (ws, wc) in waits:
            ew(eng, ws, wc)
        def wrapped(e, fn=fn, s=sem):
            last = fn(e)
            if last is None:
                last = e.nop()
            last.then_inc(s.h, 1)
        P.add(eng, wrapped)
        sem.n += 1

    def gp_init(g):
        g.memset(ident[:], 0.0)
        g.drain()
        g.affine_select(out=ident[:], in_=ident[:], compare_op=ALU.not_equal,
                        fill=1.0, base=0, pattern=[[-1, 128]], channel_multiplier=1)
        g.memset(ones512[:], 1.0)
        g.memset(eps_c[:], EPS)
        g.memset(one_c[:], 1.0)
        g.memset(n2l2_c[:], -2.0 * LN2C)
        g.memset(d2eps_c[:], float(D) * float(D) * EPS)
        g.memset(lnd_c[:], float(np.log(D)))
        return g.memset(p2l2_c[:], 2.0 * LN2C)
    run("gpsimd", sGP, gp_init)
    ew("scalar", sGP, sGP.n)
    ew("vector", sGP, sGP.n)

    for i in range(DC):
        dma(Fp[0][:], wx_h[i * 128:(i + 1) * 128, :])
        run("vector", sDV, (lambda v, i=i: v.tensor_copy(wx_bf[i][:], Fp[0][:])),
            waits=[(sLD, sLD.n)])
        ew("sync", sDV, sDV.n)
    for i in range(DC):
        dma(Fp[0][:, 0:96], wxp_h[i * 128:(i + 1) * 128, :])
        run("vector", sDV, (lambda v, i=i: v.tensor_copy(wxp_bf[i][:], Fp[0][:, 0:96])),
            waits=[(sLD, sLD.n)])
        ew("sync", sDV, sDV.n)
    dma(Fp[0][0:DTR, :], wdt_h[:, :])
    run("vector", sDV, (lambda v: v.tensor_copy(wdt_bf[:], Fp[0][0:DTR, :])),
        waits=[(sLD, sLD.n)])
    ew("sync", sDV, sDV.n)
    dma(Fp[0][:], lng_h.rearrange("(one d) -> one d", one=1).broadcast_to([128, D]))
    run("vector", sDV, (lambda v: v.tensor_copy(gam_bc[:], Fp[0][:])),
        waits=[(sLD, sLD.n)])
    ew("sync", sDV, sDV.n)

    for i in range(DC):
        sl = slice(i * 128, (i + 1) * 128)
        dma(A_f[i][:], alog_h[sl, :])
        dma(cw_c[i][:], cw_h[sl, :])
        dma(cb_c[i][:], cb_h[sl].rearrange("(p one) -> p one", one=1))
        dma(bdt_c[i][:], bdt_h[sl].rearrange("(p one) -> p one", one=1))
        dma(dp_c[i][:], dp_h[sl].rearrange("(p one) -> p one", one=1))
        dma(Fp[1][:, 0:1], lnb_h[sl].rearrange("(p one) -> p one", one=1))
        run("vector", sDV, (lambda v, i=i: v.tensor_copy(beta_c[i][:], Fp[1][:, 0:1])),
            waits=[(sLD, sLD.n)])
        ew("sync", sDV, sDV.n)
    dma(bx_row, bx_h.rearrange("(one d) -> one d", one=1))
    dma(Fp[1][0:1, 0:D], fb_h.rearrange("(one d) -> one d", one=1))
    dma(da_t[:], da_h.rearrange("(one o) -> one o", one=1))
    run("vector", sDV, (lambda v: v.tensor_copy(fb_bf[:], Fp[1][0:1, 0:D])),
        waits=[(sLD, sLD.n)])

    def fA(s):
        last = None
        for i in range(DC):
            last = s.activation(Fp[2][:, i * NS:(i + 1) * NS], A_f[i][:], AF.Exp)
        return last
    run("scalar", sAC, fA, waits=[(sLD, sLD.n)])

    def fAn(v):
        last = None
        for i in range(DC):
            last = v.tensor_scalar(A_f[i][:], Fp[2][:, i * NS:(i + 1) * NS], -1.0, None, ALU.mult)
        return last
    run("vector", sDV, fAn, waits=[(sAC, sAC.n)])

    run("scalar", sAC, (lambda s: s.activation(da_t2[:], da_t[:], AF.Tanh, scale=4.0)),
        waits=[(sLD, sLD.n)])
    run("vector", sDV, (lambda v: v.tensor_scalar(da_t[:], da_t2[:], 0.5, None, ALU.mult)),
        waits=[(sAC, sAC.n)])
    ew("sync", sDV, sDV.n)
    dma(da_d[:], da_t[:])
    ew("sync", sLD, sLD.n)
    dma(da_col[:], da_d[0:1, :].broadcast_to([128, 1]))

    def fV0(t):
        last = None
        for i in range(DC):
            t.matmul(psWx[0:1, :], beta_c[i][:], wx_bf[i][:, 0:SPAN],
                     start=(i == 0), stop=(i == DC - 1))
        for i in range(DC):
            last = t.matmul(psZ[0:1, :], beta_c[i][:], wx_bf[i][:, SPAN:2 * SPAN],
                            start=(i == 0), stop=(i == DC - 1))
        return last
    run("tensor", sPE, fV0, waits=[(sDV, sDV.n), (sGP, sGP.n)])

    def fV0e(v):
        v.tensor_tensor(v0_fr[0:1, 0:SPAN], psWx[0:1, :], bx_row[0:1, 0:SPAN], ALU.add)
        v.tensor_tensor(v0_fr[0:1, SPAN:2 * SPAN], psZ[0:1, :], bx_row[0:1, SPAN:2 * SPAN], ALU.add)
        return v.tensor_copy(v0_bf[:], v0_fr[0:1, 0:D])
    run("vector", sDV, fV0e, waits=[(sPE, sPE.n), (sLD, sLD.n)])

    def quant_chain(pairs, dd_ap, scratch, d4_waits=()):
        npair = len(pairs)
        G = 32 * npair
        am, am2, l_t, eh, p2, m_t, fx, e_t = sq
        S1, S2, S3, S4 = scratch

        def d1(v, pairs=tuple(pairs), G=G):
            for p, (srcp, _) in enumerate(pairs):
                v.tensor_reduce(am[:, p * 32:(p + 1) * 32],
                                srcp.rearrange("p (a b) -> p a b", b=32),
                                AX.X, ALU.max, apply_absolute_value=True)
            v.drain()
            v.tensor_scalar(am2[:, 0:G], am[:, 0:G], 1e-30, None, ALU.max)
            v.drain()
            return v.nop()
        run("vector", sDV, d1)

        def a1(s, G=G):
            return s.activation(l_t[:, 0:G], am2[:, 0:G], AF.Ln)
        run("scalar", sAC, a1, waits=[(sDV, sDV.n)])

        def d2(v, G=G):
            v.tensor_scalar(eh[:, 0:G], l_t[:, 0:G], 1.0 / LN2C, M23, ALU.mult, ALU.add)
            v.drain()
            v.tensor_scalar(eh[:, 0:G], eh[:, 0:G], M23, None, ALU.subtract)
            v.drain()
            return v.nop()
        run("vector", sDV, d2, waits=[(sAC, sAC.n)])

        def a2(s, G=G):
            return s.activation(p2[:, 0:G], eh[:, 0:G], AF.Exp, scale=-LN2C)
        run("scalar", sAC, a2, waits=[(sDV, sDV.n)])

        def d3(v, G=G, dd=dd_ap):
            v.tensor_tensor(m_t[:, 0:G], am2[:, 0:G], p2[:, 0:G], ALU.mult)
            v.drain()
            v.tensor_scalar(fx[:, 0:G], m_t[:, 0:G], 1.0, None, ALU.is_lt)
            v.drain()
            v.tensor_tensor(e_t[:, 0:G], eh[:, 0:G], fx[:, 0:G], ALU.subtract)
            v.drain()
            if dd.shape[1] == 1:
                v.tensor_scalar(e_t[:, 0:G], e_t[:, 0:G], dd, None, ALU.add)
            else:
                v.tensor_tensor(e_t[:, 0:G], e_t[:, 0:G], dd, ALU.add)
            v.drain()
            return v.nop()
        run("vector", sDV, d3, waits=[(sAC, sAC.n)])

        def a3(s, G=G):
            s.activation(l_t[:, 0:G], e_t[:, 0:G], AF.Exp, scale=LN2C, bias=n2l2_c[:])
            return s.activation(p2[:, 0:G], e_t[:, 0:G], AF.Exp, scale=-LN2C, bias=p2l2_c[:])
        run("scalar", sAC, a3, waits=[(sDV, sDV.n)])

        def d4(v, pairs=tuple(pairs)):
            last = None
            for p, (srcp, dstp) in enumerate(pairs):
                x3 = srcp.rearrange("p (a b) -> p a b", b=32)
                isc = p2[:, p * 32:(p + 1) * 32].unsqueeze(2).broadcast_to([128, 32, 32])
                scl = l_t[:, p * 32:(p + 1) * 32].unsqueeze(2).broadcast_to([128, 32, 32])
                xs = S1
                v.tensor_tensor(xs.rearrange("p (a b) -> p a b", b=32), x3, isc, ALU.mult)
                v.tensor_scalar(xs, xs, 6.0, -6.0, ALU.min, ALU.max)
                aq = S2
                v.tensor_tensor(aq, xs, xs, ALU.mult)
                q1 = S3
                v.tensor_scalar(q1, aq, 4.0, None, ALU.is_ge)
                q4 = S2
                v.tensor_scalar(q4, aq, 16.0, None, ALU.is_ge)
                tq = S4
                v.scalar_tensor_tensor(tq, q4, 0.5, q1, ALU.mult, ALU.add)
                stp = S3
                v.scalar_tensor_tensor(stp, q1, 0.5, q4, ALU.mult, ALU.add)
                v.tensor_scalar(stp, stp, 0.5, None, ALU.add)
                rstp = S4
                v.tensor_scalar(rstp, tq, -1.0, 2.0, ALU.mult, ALU.add)
                qt = S2
                v.tensor_tensor(qt, xs, rstp, ALU.mult)
                v.tensor_scalar(qt, qt, M23, None, ALU.add)
                v.tensor_scalar(qt, qt, M23, None, ALU.subtract)
                v.tensor_tensor(qt, qt, stp, ALU.mult)
                last = v.tensor_tensor(dstp.rearrange("p (a b) -> p a b", b=32),
                                       qt.rearrange("p (a b) -> p a b", b=32), scl, ALU.mult)
            return last
        run("vector", sDV, d4, waits=[(sAC, sAC.n)] + list(d4_waits))

    for oc in range(DC):
        dma(Fp[0][:], fw_h[oc * 128:(oc + 1) * 128, :])
        dma(dwq[:], dw_h[oc * 128:(oc + 1) * 128, :])
        run("scalar", sAC, (lambda s: s.activation(sq[1][:, 0:32], dwq[:], AF.Tanh, scale=4.0)),
            waits=[(sLD, sLD.n)])
        run("vector", sDV, (lambda v: v.tensor_scalar(dwq[:], sq[1][:, 0:32], 0.5, None, ALU.mult)),
            waits=[(sAC, sAC.n)])
        wq_dst = bigs[0][:, 0:D]
        quant_chain([(Fp[0][:], wq_dst)], dwq[:], (Fp[1][:], Fp[2][:], Fp[3][:], Fp[4][:]),
                    d4_waits=[(sPE, sPE.n)])

        def t_wq(t, src=wq_dst):
            last = None
            for g in range(2):
                ps = psA if g == 0 else psB
                for q in range(4):
                    dcx = g * 4 + q
                    last = t.transpose(ps[:, q * 128:(q + 1) * 128],
                                       src[:, dcx * 128:(dcx + 1) * 128], ident[:])
            return last
        run("tensor", sPE, t_wq, waits=[(sDV, sDV.n)])

        def e_wq(v, oc=oc):
            last = None
            for g in range(2):
                ps = psA if g == 0 else psB
                for q in range(4):
                    dcx = g * 4 + q
                    last = v.tensor_copy(wqT[dcx][:, oc * 128:(oc + 1) * 128],
                                         ps[:, q * 128:(q + 1) * 128])
            return last
        run("vector", sDV, e_wq, waits=[(sPE, sPE.n)])
        ew("sync", sDV, sDV.n)

    def ln_smalls(col_n, c0=0):
        def d(v, n=col_n, c0=c0):
            sl = slice(c0, c0 + n)
            v.tensor_tensor(t4a_s[:, sl], sumx_s[:, sl], sumx_s[:, sl], ALU.mult)
            v.tensor_scalar(mu_s[:, sl], sumx_s[:, sl], 1.0 / D, None, ALU.mult)
            v.drain()
            v.scalar_tensor_tensor(t4b_s[:, sl], sumq_s[:, sl], float(D), t4a_s[:, sl],
                                   ALU.mult, ALU.subtract)
            v.drain()
            return v.nop()
        run("vector", sDV, d, waits=[(sAC, sAC.n)])

        def a(s, n=col_n, c0=c0):
            sl = slice(c0, c0 + n)
            s.activation(t4a_s[:, sl], t4b_s[:, sl], AF.Ln, bias=d2eps_c[:])
            s.drain()
            s.activation(rstd_s[:, sl], t4a_s[:, sl], AF.Exp, scale=-0.5, bias=lnd_c[:])
            s.drain()
            return s.nop()
        run("scalar", sAC, a, waits=[(sDV, sDV.n)])

    for b in range(NB):
        for sp in range(NSPAN):
            t0 = sp * SPAN
            ew("sync", sDV, sDV.n)
            ew("sync", sAC, sAC.n)
            for q in range(TPS):
                row = b * T + t0 + q * 128
                dma(Fp[q][:], f_h[row:row + 128, :])
            ld_all = sLD.n
            def sq1(s):
                last = None
                for q in range(TPS):
                    s.activation(Fp[4][:], Fp[q][:], AF.Copy, accum_out=sumx_s[:, q:q + 1])
                    last = s.activation(Fp[4][:], Fp[q][:], AF.Square,
                                        accum_out=sumq_s[:, q:q + 1])
                return last
            run("scalar", sAC, sq1, waits=[(sLD, ld_all), (sDV, sDV.n)])
            ln_smalls(TPS)
            def ap1(v):
                last = None
                for q in range(TPS):
                    v.tensor_scalar(Fp[q][:], Fp[q][:], mu_s[:, q:q + 1], None, ALU.subtract)
                    last = v.tensor_scalar(Fp[q][:], Fp[q][:], rstd_s[:, q:q + 1], None, ALU.mult)
                return last
            run("vector", sDV, ap1, waits=[(sAC, sAC.n)])
            def sq2(s):
                last = None
                for q in range(TPS):
                    s.activation(Fp[4][:], Fp[q][:], AF.Copy, accum_out=sumx_s[:, q:q + 1])
                    last = s.activation(Fp[4][:], Fp[q][:], AF.Square,
                                        accum_out=sumq_s[:, q:q + 1])
                return last
            run("scalar", sAC, sq2, waits=[(sDV, sDV.n)])
            ln_smalls(TPS)
            def ap2(v):
                last = None
                for q in range(TPS):
                    v.tensor_scalar(Fp[q][:], Fp[q][:], mu_s[:, q:q + 1], None, ALU.subtract)
                    v.tensor_scalar(Fp[q][:], Fp[q][:], rstd_s[:, q:q + 1], None, ALU.mult)
                    last = v.tensor_tensor(fng_t[q][:], Fp[q][:], gam_bc[:], ALU.mult)
                return last
            run("vector", sDV, ap2, waits=[(sAC, sAC.n)])

            fng_done = sDV.n
            evac_marks = {}
            for dcx in range(DC):
                ps = psA if dcx % 2 == 0 else psB
                waits = [(sDV, fng_done)]
                if dcx >= 2:
                    waits.append((sAC, evac_marks[dcx - 2]))

                def tr(t, dcx=dcx, ps=ps):
                    last = None
                    for q in range(TPS):
                        last = t.transpose(ps[:, q * 128:(q + 1) * 128],
                                           fng_t[q][:, dcx * 128:(dcx + 1) * 128], ident[:])
                    return last
                run("tensor", sPE, tr, waits=waits)

                def ev(s, dcx=dcx, ps=ps):
                    return s.activation(fnT[dcx], ps[:], AF.Copy)
                run("scalar", sAC, ev, waits=[(sPE, sPE.n)])
                evac_marks[dcx] = sAC.n

            if sp == 0:
                def cc(v):
                    last = None
                    for jc in range(DC):
                        last = v.memset(x1buf[jc][:, 0:3], 0.0)
                    return last
                run("vector", sDV, cc)
            else:
                def cc(v):
                    last = None
                    for jc in range(DC):
                        last = v.tensor_copy(x1buf[jc][:, 0:3], x1buf[jc][:, SPAN:SPAN + 3])
                    return last
                run("vector", sDV, cc)

            fnT_done_ac = sAC.n
            fnT_done = sDV.n
            wx_evac = {}
            cvsets = [(a_t[:, 0:SPAN], a_t[:, SPAN:2 * SPAN]),
                      (a_t[:, 2 * SPAN:3 * SPAN], a_t[:, 3 * SPAN:4 * SPAN])]

            def emit_cnv_sil(jc):
                cvA, cvB = cvsets[jc % 2]
                def cnv(v, jc=jc, cvA=cvA, cvB=cvB):
                    v.tensor_scalar(cvA, x1buf[jc][:, 0:SPAN], cw_c[jc][:, 0:1], None, ALU.mult)
                    v.scalar_tensor_tensor(cvB, x1buf[jc][:, 1:1 + SPAN], cw_c[jc][:, 1:2],
                                           cvA, ALU.mult, ALU.add)
                    v.tensor_scalar(cvA, x1buf[jc][:, 2:2 + SPAN], cw_c[jc][:, 2:3], None, ALU.mult)
                    v.tensor_tensor(cvA, cvA, cvB, ALU.add)
                    return v.scalar_tensor_tensor(cvB, x1buf[jc][:, 3:3 + SPAN], cw_c[jc][:, 3:4],
                                                  cvA, ALU.mult, ALU.add)
                run("vector", sDV, cnv, waits=[(sAC, wx_evac[jc])])
                def sil(s, jc=jc, t0=t0, cvB=cvB):
                    return s.activation(xcy[jc][:, t0:t0 + SPAN], cvB, AF.Silu, bias=cb_c[jc][:])
                run("scalar", sAC, sil, waits=[(sDV, sDV.n)])

            for jc in range(DC):
                ps = psWx if jc % 2 == 0 else psZ
                waits = [(sDV, fnT_done), (sAC, fnT_done_ac)]
                if jc >= 2:
                    waits.append((sAC, wx_evac[jc - 2]))
                else:
                    waits.append((sAC, sAC.n))

                def mmwx(t, jc=jc, ps=ps):
                    for dcx in range(DC):
                        t.matmul(ps[:], wx_bf[dcx][:, jc * 128:(jc + 1) * 128], fnT[dcx],
                                 start=(dcx == 0), stop=False)
                    return t.matmul(ps[:], v0_bf[0:1, jc * 128:(jc + 1) * 128], ones512[0:1, :],
                                    start=False, stop=True)
                run("tensor", sPE, mmwx, waits=waits)

                def evx(s, jc=jc, ps=ps):
                    return s.activation(x1buf[jc][:, 3:3 + SPAN], ps[:], AF.Copy)
                run("scalar", sAC, evx, waits=[(sPE, sPE.n), (sDV, sDV.n)])
                wx_evac[jc] = sAC.n
                if jc >= 1:
                    emit_cnv_sil(jc - 1)
            emit_cnv_sil(DC - 1)

            def mmxp(t, t0=t0):
                last = None
                for dcx in range(DC):
                    last = t.matmul(psXd[:], wxp_bf[dcx][:], xcy[dcx][:, t0:t0 + SPAN],
                                    start=(dcx == 0), stop=(dcx == DC - 1))
                return last
            run("tensor", sPE, mmxp, waits=[(sAC, sAC.n)])

            run("scalar", sAC, (lambda s: s.activation(dtT, psXd[0:DTR, :], AF.Copy)),
                waits=[(sPE, sPE.n)])
            run("vector", sDV, (lambda v: v.tensor_copy(bcsb, psXd[DTR:96, :])),
                waits=[(sPE, sPE.n), (sLD, sLD.n)])
            ew("sync", sDV, sDV.n)
            def st_bc(e, b=b, t0=t0):
                e.dma_start(out=bcd[b][:, t0:t0 + SPAN], in_=bcsb).then_inc(sLD.h, 16)
            P.add("sync", st_bc)
            sLD.n += 16

            sp_done = sAC.n
            sp_evac = {}

            def emit_softplus(dcx, t0):
                par = dcx % 2
                h0 = par * SPAN
                zb = Fp[0][:, h0:h0 + SPAN]
                mx = Fp[1][:, h0:h0 + SPAN]
                mn = Fp[2][:, h0:h0 + SPAN]
                ngab = Fp[3][:, h0:h0 + SPAN]
                lnb = Fp[4][:, h0:h0 + SPAN]
                def spd(v, dcx=dcx, zb=zb, mx=mx, mn=mn, ngab=ngab):
                    v.tensor_scalar(mx, zb, bdt_c[dcx][:], 0.0, ALU.add, ALU.max)
                    v.tensor_scalar(mn, zb, bdt_c[dcx][:], 0.0, ALU.add, ALU.min)
                    return v.tensor_tensor(ngab, mn, mx, ALU.subtract)
                run("vector", sDV, spd, waits=[(sAC, sp_evac[dcx])])
                spd_m = sDV.n
                def spe(s, ngab=ngab, lnb=lnb):
                    s.activation(lnb, ngab, AF.Exp)
                    s.drain()
                    s.activation(ngab, lnb, AF.Ln, bias=one_c[:])
                    s.drain()
                    return s.nop()
                run("scalar", sAC, spe, waits=[(sDV, spd_m)])
                def spf(v, dcx=dcx, t0=t0, ngab=ngab, mx=mx):
                    return v.tensor_tensor(dltT[dcx][:, t0:t0 + SPAN], ngab, mx, ALU.add)
                run("vector", sDV, spf, waits=[(sAC, sAC.n)])

            for dcx in range(DC):
                ps = psWx if dcx % 2 == 0 else psZ
                waits = [(sAC, sp_done)]
                if dcx >= 2:
                    waits.append((sAC, sp_evac[dcx - 2]))

                def mmdt(t, dcx=dcx, ps=ps):
                    return t.matmul(ps[:], wdt_bf[:, dcx * 128:(dcx + 1) * 128], dtT,
                                    start=True, stop=True)
                run("tensor", sPE, mmdt, waits=waits)

                def spa(s, dcx=dcx, ps=ps):
                    par = dcx % 2
                    return s.activation(Fp[0][:, par * SPAN:(par + 1) * SPAN], ps[:], AF.Copy)
                run("scalar", sAC, spa, waits=[(sPE, sPE.n), (sDV, sDV.n)])
                sp_evac[dcx] = sAC.n
                if dcx >= 1:
                    emit_softplus(dcx - 1, t0)
            emit_softplus(DC - 1, t0)

        def s2init(v):
            for dcx in range(DC):
                v.tensor_tensor(u_t[dcx][:], dltT[dcx][:], xcy[dcx][:], ALU.mult)
            last = None
            for dcx in range(DC):
                last = v.tensor_scalar(xcy[dcx][:], xcy[dcx][:], dp_c[dcx][:], None, ALU.mult)
            return last
        run("vector", sDV, s2init)
        ew("scalar", sDV, sDV.n)

        psy8 = [psWx, psZ, psF0, psF1, psA_f, psB_f, psXd_f, psY0]
        a_bufs = [Fp[0][:, 0:SPAN].bitcast(BF16), Fp[0][:, SPAN:2 * SPAN].bitcast(BF16),
                  Cbc[:, 0:T // 2], Cbc[:, T // 2:T]]
        z_bufs = [Fp[1][:, 0:SPAN].bitcast(BF16), Fp[1][:, SPAN:2 * SPAN].bitcast(BF16)]
        HT = T // 2
        bc_bufs = [(Fp[2][:, 0:SPAN].bitcast(BF16), Fp[3][:, 0:SPAN].bitcast(BF16)),
                   (Fp[2][:, SPAN:2 * SPAN].bitcast(BF16), Fp[3][:, SPAN:2 * SPAN].bitcast(BF16)),
                   (Fp[4][:, 0:SPAN].bitcast(BF16), Fp[4][:, SPAN:2 * SPAN].bitcast(BF16)),
                   (b_t[:, HT:2 * HT], h_t[:, HT:2 * HT])]
        NPF = 4
        if b == 0:
            ew("sync", sPE, sPE.n)
        ew("sync", sLD, sLD.n)
        ew("sync", sAC, sAC.n)
        ew("scalar", sDV, sDV.n)
        ew("scalar", sPE, sPE.n)
        s2init_dv = sDV.n
        for dcx in range(DC):
            for half in range(2):
                ht0 = half * HT
                ew("sync", sDV, sDV.n)
                bc_ld = {}
                for pf in range(NPF):
                    dma(bc_bufs[pf][0], bcd[b][pf:pf + 1, ht0:ht0 + HT].broadcast_to([128, HT]))
                    dma(bc_bufs[pf][1], bcd[b][16 + pf:17 + pf, ht0:ht0 + HT].broadcast_to([128, HT]))
                    bc_ld[pf] = sLD.n

                def tinit(t, dcx=dcx, ht0=ht0):
                    last = None
                    for q in range(8):
                        last = t.matmul(psy8[q][:, 0:128],
                                        xcy[dcx][:, ht0 + q * 128:ht0 + (q + 1) * 128],
                                        ident[:], start=True, stop=False)
                    return last
                run("tensor", sPE, tinit, waits=[(sDV, s2init_dv), (sAC, sAC.n)])

                scn_marks = {}
                zt_marks = {}
                a_marks = {}
                for n in range(NS):
                    aw = []
                    if n >= 4:
                        aw.append((sDV, scn_marks[n - 4]))
                    def abuild(s, n=n, dcx=dcx, ht0=ht0):
                        return s.activation(a_bufs[n % 4], dltT[dcx][:, ht0:ht0 + HT],
                                            AF.Copy, scale=A_f[dcx][:, n:n + 1])
                    run("scalar", sAC, abuild, waits=aw)
                    a_marks[n] = sAC.n

                    dw = [(sLD, bc_ld[n]), (sAC, a_marks[n])]
                    if n >= 2:
                        dw.append((sPE, zt_marks[n - 2]))
                    def scn(v, n=n, dcx=dcx, half=half, ht0=ht0):
                        Bb, Cb = bc_bufs[n % NPF]
                        v.tensor_tensor(b_t[:, 0:HT], u_t[dcx][:, ht0:ht0 + HT], Bb, ALU.mult)
                        init = 0.0 if half == 0 else hmid[:, n:n + 1]
                        v.tensor_tensor_scan(h_t[:, 0:HT], a_bufs[n % 4], b_t[:, 0:HT],
                                             init, ALU.mult, ALU.add)
                        last = v.tensor_tensor(z_bufs[n % 2], h_t[:, 0:HT], Cb, ALU.mult)
                        if half == 0:
                            last = v.tensor_copy(hmid[:, n:n + 1], h_t[:, HT - 1:HT])
                        return last
                    run("vector", sDV, scn, waits=dw)
                    scn_marks[n] = sDV.n

                    def zt(t, n=n):
                        last = None
                        for q in range(8):
                            last = t.matmul(psy8[q][:, 0:128],
                                            z_bufs[n % 2][:, q * 128:(q + 1) * 128], ident[:],
                                            start=False, stop=(n == NS - 1))
                        return last
                    run("tensor", sPE, zt, waits=[(sDV, scn_marks[n])])
                    zt_marks[n] = sPE.n

                    if n + NPF < NS:
                        ew("sync", sDV, scn_marks[n])
                        dma(bc_bufs[n % NPF][0],
                            bcd[b][n + NPF:n + NPF + 1, ht0:ht0 + HT].broadcast_to([128, HT]))
                        dma(bc_bufs[n % NPF][1],
                            bcd[b][16 + n + NPF:17 + n + NPF, ht0:ht0 + HT].broadcast_to([128, HT]))
                        bc_ld[n + NPF] = sLD.n

                def yev(s, dcx=dcx, half=half):
                    last = None
                    for q in range(8):
                        tt = half * 8 + q
                        last = s.activation(ytm_t[tt][:, dcx * 128:(dcx + 1) * 128],
                                            psy8[q][:, 0:128], AF.Copy)
                    return last
                run("scalar", sAC, yev, waits=[(sPE, sPE.n)])

        ew("vector", sAC, sAC.n)
        x0b4 = [xcy[2 + q // 2][:, (q % 2) * D:(q % 2 + 1) * D] for q in range(4)]
        ln34 = [xcy[4 + q // 2][:, (q % 2) * D:(q % 2 + 1) * D] for q in range(4)]
        q_scr = (b_t[:].bitcast(F32), h_t[:].bitcast(F32), a_t[:].bitcast(F32), Fp[4][:])
        for sp in range(NSPAN):
            t0 = sp * SPAN
            ew("sync", sDV, sDV.n)
            ew("sync", sAC, sAC.n)
            for q in range(TPS):
                row = b * T + t0 + q * 128
                dma(Fp[q][:], f_h[row:row + 128, :])
            ld_all = sLD.n
            def sq3(s, sp=sp):
                last = None
                for q in range(TPS):
                    s.activation(Fp[4][:], ytm_t[sp * TPS + q][:], AF.Copy,
                                 accum_out=sumx_s[:, q:q + 1])
                    last = s.activation(Fp[4][:], ytm_t[sp * TPS + q][:], AF.Square,
                                        accum_out=sumq_s[:, q:q + 1])
                return last
            run("scalar", sAC, sq3, waits=[(sDV, sDV.n)])
            ln_smalls(TPS)
            def ap3(v, sp=sp):
                last = None
                for q in range(TPS):
                    tt = sp * TPS + q
                    v.tensor_scalar(ln34[q], ytm_t[tt][:], mu_s[:, q:q + 1], None, ALU.subtract)
                    last = v.tensor_scalar(ln34[q], ln34[q], rstd_s[:, q:q + 1], None, ALU.mult)
                return last
            run("vector", sDV, ap3, waits=[(sAC, sAC.n)])
            def cst(s):
                last = None
                for q in range(TPS):
                    last = s.activation(x0b4[q], Fp[q][:], AF.Copy)
                return last
            run("scalar", sAC, cst, waits=[(sLD, ld_all), (sPE, sPE.n)])
            def res1(v, sp=sp):
                last = None
                for q in range(TPS):
                    last = v.tensor_tensor(x1_t[sp * TPS + q][:], ln34[q], x0b4[q], ALU.add)
                return last
            run("vector", sDV, res1, waits=[(sAC, sAC.n)])
            def sq4(s, sp=sp):
                last = None
                for q in range(TPS):
                    s.activation(Fp[4][:], x1_t[sp * TPS + q][:], AF.Copy,
                                 accum_out=sumx_s[:, q:q + 1])
                    last = s.activation(Fp[4][:], x1_t[sp * TPS + q][:], AF.Square,
                                        accum_out=sumq_s[:, q:q + 1])
                return last
            run("scalar", sAC, sq4, waits=[(sDV, sDV.n)])
            ln_smalls(TPS)
            def ap4(v, sp=sp):
                last = None
                for q in range(TPS):
                    tt = sp * TPS + q
                    v.tensor_scalar(Fp[q][:], x1_t[tt][:], mu_s[:, q:q + 1], None, ALU.subtract)
                    last = v.tensor_scalar(Fp[q][:], Fp[q][:], rstd_s[:, q:q + 1], None, ALU.mult)
                return last
            run("vector", sDV, ap4, waits=[(sAC, sAC.n)])
            quant_chain([(Fp[q][:], xq_t[q][:]) for q in range(TPS)], da_col[:],
                        q_scr, d4_waits=[(sPE, sPE.n)])

            xq_done = sDV.n
            evac_marks = {}
            for dcx in range(DC):
                ps = psA if dcx % 2 == 0 else psB
                waits = [(sDV, xq_done)]
                if dcx >= 2:
                    waits.append((sAC, evac_marks[dcx - 2]))

                def trq(t, dcx=dcx, ps=ps):
                    last = None
                    for q in range(TPS):
                        last = t.transpose(ps[:, q * 128:(q + 1) * 128],
                                           xq_t[q][:, dcx * 128:(dcx + 1) * 128], ident[:])
                    return last
                run("tensor", sPE, trq, waits=waits)

                def evq(s, dcx=dcx, ps=ps):
                    return s.activation(fnT[dcx], ps[:], AF.Copy)
                run("scalar", sAC, evq, waits=[(sPE, sPE.n)])
                evac_marks[dcx] = sAC.n

            xqT_done = sDV.n
            xqT_done_ac = sAC.n
            evf_marks = {}
            for q in range(TPS):
                tt = sp * TPS + q
                row = b * T + t0 + q * 128
                pF0, pF1 = (psF0, psF1) if q % 2 == 0 else (psA_f, psB_f)
                waits = [(sDV, xqT_done), (sAC, max(xqT_done_ac, sAC.n if q < 2 else evf_marks[q - 2]))]

                def mmf(t, q=q, pF0=pF0, pF1=pF1):
                    for dcx in range(DC):
                        sl = slice(q * 128, (q + 1) * 128)
                        t.matmul(pF0[:], fnT[dcx][:, sl], wqT[dcx][:, 0:SPAN],
                                 start=(dcx == 0), stop=False)
                        t.matmul(pF1[:], fnT[dcx][:, sl], wqT[dcx][:, SPAN:2 * SPAN],
                                 start=(dcx == 0), stop=False)
                    t.matmul(pF0[:], ones512[0:1, 0:128], fb_bf[0:1, 0:SPAN],
                             start=False, stop=True)
                    return t.matmul(pF1[:], ones512[0:1, 0:128], fb_bf[0:1, SPAN:2 * SPAN],
                                    start=False, stop=True)
                run("tensor", sPE, mmf, waits=waits)

                def evf(s, q=q, pF0=pF0, pF1=pF1):
                    s.activation(Fp[q][:, 0:SPAN], pF0[:], AF.Copy, accum_out=t4a_s[:, q:q + 1])
                    s.activation(Fp[q][:, SPAN:2 * SPAN], pF1[:], AF.Copy,
                                 accum_out=t4b_s[:, q:q + 1])
                    return s.activation(b_t[:].bitcast(F32), Fp[q][:], AF.Square,
                                        accum_out=sumq_s[:, q:q + 1])
                run("scalar", sAC, evf, waits=[(sPE, sPE.n)])
                evf_marks[q] = sAC.n

                def evs(v, q=q):
                    v.tensor_tensor(sumx_s[:, q:q + 1], t4a_s[:, q:q + 1],
                                    t4b_s[:, q:q + 1], ALU.add)
                    v.drain()
                    return v.nop()
                run("vector", sDV, evs, waits=[(sAC, evf_marks[q])])
                ln_smalls(1, c0=q)
                if sST.n > 0:
                    ew("vector", sST, sST.n)
                def ap5(v, tt=tt, q=q):
                    scr5 = h_t[:].bitcast(F32)
                    v.tensor_scalar(scr5, Fp[q][:], mu_s[:, q:q + 1], None, ALU.subtract)
                    v.tensor_scalar(scr5, scr5, rstd_s[:, q:q + 1], None, ALU.mult)
                    return v.tensor_tensor(out_stg[:], scr5, x1_t[tt][:], ALU.add)
                run("vector", sDV, ap5, waits=[(sAC, sAC.n)])
                ew("sync", sDV, sDV.n)
                dma_store(out_h[row:row + 128, :], out_stg[:])

    def fin(e):
        e.wait_ge(sST.h, sST.n)
    P.add("sync", fin)

    with nc.Block() as blk:
        blk.sync(lambda e: [fn(e) for fn in P.items["sync"]])
        blk.vector(lambda e: [fn(e) for fn in P.items["vector"]])
        blk.scalar(lambda e: [fn(e) for fn in P.items["scalar"]])
        blk.tensor(lambda e: [fn(e) for fn in P.items["tensor"]])
        blk.gpsimd(lambda e: [fn(e) for fn in P.items["gpsimd"]])
    es.close()
    return nc


def kernel(L0_attn_f, L0_attn_ln_g, L0_attn_ln_b, L0_attn_Wx, L0_attn_bx,
           L0_attn_conv_w, L0_attn_conv_b, L0_attn_Wxp, L0_attn_Wdt, L0_attn_bdt,
           L0_attn_A_log, L0_attn_Dp, L0_ffn_weight, L0_ffn_bias, L0_ffn_delta_w,
           L0_ffn_delta_act):
    if "nc" not in _CACHE:
        _CACHE["nc"] = build_nc()
    nc = _CACHE["nc"]
    f = np.ascontiguousarray(np.asarray(L0_attn_f, dtype=np.float32))
    common = {
        "ln_g": np.ascontiguousarray(L0_attn_ln_g, dtype=np.float32),
        "ln_b": np.ascontiguousarray(L0_attn_ln_b, dtype=np.float32),
        "Wx": np.ascontiguousarray(L0_attn_Wx, dtype=np.float32),
        "bx": np.ascontiguousarray(L0_attn_bx, dtype=np.float32),
        "conv_w": np.ascontiguousarray(L0_attn_conv_w, dtype=np.float32),
        "conv_b": np.ascontiguousarray(L0_attn_conv_b, dtype=np.float32),
        "Wxp": np.ascontiguousarray(L0_attn_Wxp, dtype=np.float32),
        "Wdt": np.ascontiguousarray(L0_attn_Wdt, dtype=np.float32),
        "bdt": np.ascontiguousarray(L0_attn_bdt, dtype=np.float32),
        "A_log": np.ascontiguousarray(L0_attn_A_log, dtype=np.float32),
        "Dp": np.ascontiguousarray(L0_attn_Dp, dtype=np.float32),
        "fw": np.ascontiguousarray(L0_ffn_weight, dtype=np.float32),
        "fb": np.ascontiguousarray(L0_ffn_bias, dtype=np.float32),
        "dw": np.ascontiguousarray(L0_ffn_delta_w, dtype=np.float32),
        "da": np.ascontiguousarray(L0_ffn_delta_act, dtype=np.float32),
    }
    in_maps = []
    for c in range(8):
        m = dict(common)
        m["f"] = np.ascontiguousarray(f[2 * c:2 * c + 2].reshape(NB * T, D))
        in_maps.append(m)
    res = run_bass_kernel_spmd(nc, in_maps, core_ids=list(range(8)))
    out = np.concatenate([r["out"].reshape(NB, T, D) for r in res.results], axis=0)
    return np.ascontiguousarray(out, dtype=np.float32)


# version 08
# speedup vs baseline: 1.0959x; 1.0243x over previous
import numpy as np
from contextlib import ExitStack
import concourse.bass as bass
import concourse.mybir as mybir
from concourse.bass_utils import run_bass_kernel_spmd

F32 = mybir.dt.float32
BF16 = mybir.dt.bfloat16
ALU = mybir.AluOpType
AF = mybir.ActivationFunctionType
AX = mybir.AxisListType

D = 1024
T = 2048
NB = 2
NS = 16
DTR = 64
DC = 8
SPAN = 512
NSPAN = T // SPAN
TPS = SPAN // 128
LN2C = float(np.log(2.0))
EPS = 1e-5
M23 = float(1.5 * 2.0 ** 23)

_CACHE = {}


class Sem:
    def __init__(self, nc, es, name):
        self.h = es.enter_context(nc.semaphore(name))
        self.n = 0


class Prog:
    def __init__(self):
        self.items = {"sync": [], "vector": [], "scalar": [], "tensor": [], "gpsimd": []}

    def add(self, eng, fn):
        self.items[eng].append(fn)


def build_nc():
    nc = bass.Bass("TRN2", target_bir_lowering=False, debug=False,
                   detect_race_conditions=False)

    f_h = nc.dram_tensor("f", [NB * T, D], F32, kind="ExternalInput").ap()
    lng_h = nc.dram_tensor("ln_g", [D], F32, kind="ExternalInput").ap()
    lnb_h = nc.dram_tensor("ln_b", [D], F32, kind="ExternalInput").ap()
    wx_h = nc.dram_tensor("Wx", [D, D], F32, kind="ExternalInput").ap()
    bx_h = nc.dram_tensor("bx", [D], F32, kind="ExternalInput").ap()
    cw_h = nc.dram_tensor("conv_w", [D, 4], F32, kind="ExternalInput").ap()
    cb_h = nc.dram_tensor("conv_b", [D], F32, kind="ExternalInput").ap()
    wxp_h = nc.dram_tensor("Wxp", [D, 96], F32, kind="ExternalInput").ap()
    wdt_h = nc.dram_tensor("Wdt", [DTR, D], F32, kind="ExternalInput").ap()
    bdt_h = nc.dram_tensor("bdt", [D], F32, kind="ExternalInput").ap()
    alog_h = nc.dram_tensor("A_log", [D, NS], F32, kind="ExternalInput").ap()
    dp_h = nc.dram_tensor("Dp", [D], F32, kind="ExternalInput").ap()
    fw_h = nc.dram_tensor("fw", [D, D], F32, kind="ExternalInput").ap()
    fb_h = nc.dram_tensor("fb", [D], F32, kind="ExternalInput").ap()
    dw_h = nc.dram_tensor("dw", [D, 32], F32, kind="ExternalInput").ap()
    da_h = nc.dram_tensor("da", [1], F32, kind="ExternalInput").ap()
    out_h = nc.dram_tensor("out", [NB * T, D], F32, kind="ExternalOutput").ap()

    bcd = [nc.dram_tensor(f"bcd{b}", [32, T], BF16, kind="Internal").ap() for b in range(NB)]
    da_d = nc.dram_tensor("da_d", [1, 1], F32, kind="Internal").ap()

    es = ExitStack()
    S = lambda name, shape, dt: es.enter_context(nc.sbuf_tensor(name, shape, dt))

    wx_bf = [S(f"wx{i}", [128, D], BF16) for i in range(DC)]
    wqT = [S(f"wqT{i}", [128, D], BF16) for i in range(DC)]
    wxp_bf = [S(f"wxp{i}", [128, 96], BF16) for i in range(DC)]
    wdt_bf = S("wdt", [DTR, D], BF16)
    gam_bc = S("gam", [128, D], BF16)
    v0_bf = S("v0", [1, D], BF16)
    fb_bf = S("fbias", [1, D], BF16)
    ones512 = S("ones512", [1, SPAN], BF16)
    ident = S("ident", [128, 128], BF16)
    A_f = [S(f"A{i}", [128, NS], F32) for i in range(DC)]
    cw_c = [S(f"cw{i}", [128, 4], F32) for i in range(DC)]
    cb_c = [S(f"cb{i}", [128, 1], F32) for i in range(DC)]
    bdt_c = [S(f"bdtc{i}", [128, 1], F32) for i in range(DC)]
    dp_c = [S(f"dpc{i}", [128, 1], F32) for i in range(DC)]
    beta_c = [S(f"betac{i}", [128, 1], BF16) for i in range(DC)]
    eps_c = S("epsc", [128, 1], F32)
    one_c = S("onec", [128, 1], F32)
    n2l2_c = S("n2l2", [128, 1], F32)
    d2eps_c = S("d2eps", [128, 1], F32)
    lnd_c = S("lndc", [128, 1], F32)
    p2l2_c = S("p2l2", [128, 1], F32)
    da_col = S("dacol", [128, 1], F32)
    da_t = S("dat", [1, 1], F32)
    da_t2 = S("dat2", [1, 1], F32)

    Fp = [S(f"Fp{i}", [128, D], F32) for i in range(5)]
    out_stg = S("out_stg", [128, D], F32)
    bigs = [S(f"big{i}", [128, T], BF16) for i in range(24)]
    dltT = bigs[0:8]
    u_t = bigs[8:16]
    xcy = bigs[16:24]
    fng_t = [bigs[8 + i // 2][:, (i % 2) * D:(i % 2 + 1) * D] for i in range(4)]
    x1_t = [bigs[tt // 2][:, (tt % 2) * D:(tt % 2 + 1) * D] for tt in range(16)]
    ytm_t = [bigs[8 + tt // 2][:, (tt % 2) * D:(tt % 2 + 1) * D] for tt in range(16)]
    xq_t = [bigs[16 + i // 2][:, (i % 2) * D:(i % 2 + 1) * D] for i in range(4)]
    ln3_half = bigs[18][:, 0:D]
    x0b_half = bigs[18][:, D:2 * D]

    x1buf = [S(f"x1b{i}", [128, SPAN + 3], BF16) for i in range(DC)]
    Bbc = S("Bbc", [128, T], BF16)
    Cbc = S("Cbc", [128, T], BF16)
    a_t = S("a_t", [128, T], BF16)
    b_t = S("b_t", [128, T], BF16)
    h_t = S("h_t", [128, T], BF16)
    fnT = [Bbc[:, i * SPAN:(i + 1) * SPAN] for i in range(4)] + \
          [Cbc[:, i * SPAN:(i + 1) * SPAN] for i in range(4)]
    dtT = h_t[0:DTR, 0:SPAN]
    bcsb = b_t[0:32, 0:SPAN]
    cv0 = a_t[:, 0:SPAN]
    cv1 = a_t[:, SPAN:2 * SPAN]
    v0_fr = Fp[4][0:1, 0:D]
    bx_row = Fp[3][0:1, 0:D]

    sm4 = [S(f"sm4_{i}", [128, 4], F32) for i in range(8)]
    sumx_s, sumq_s, mu_s, m2_s, rstd_s, nmr_s, t4a_s, t4b_s = sm4
    sq = [S(f"sq{i}", [128, 128], F32) for i in range(8)]
    hmid = S("hmid", [128, NS], F32)
    dwq = S("dwq", [128, 32], F32)

    psA_f = es.enter_context(nc.psum_tensor("psA", [128, SPAN], F32))
    psB_f = es.enter_context(nc.psum_tensor("psB", [128, SPAN], F32))
    psA = psA_f[:].bitcast(BF16)[:, 0:SPAN]
    psB = psB_f[:].bitcast(BF16)[:, 0:SPAN]
    psXd_f = es.enter_context(nc.psum_tensor("psXd", [128, SPAN], F32))
    psXd = psXd_f[0:96, :]
    psWx = es.enter_context(nc.psum_tensor("psWx", [128, SPAN], F32))
    psZ = es.enter_context(nc.psum_tensor("psZ", [128, SPAN], F32))
    psF0 = es.enter_context(nc.psum_tensor("psF0", [128, SPAN], F32))
    psF1 = es.enter_context(nc.psum_tensor("psF1", [128, SPAN], F32))
    psY0 = es.enter_context(nc.psum_tensor("psY0", [128, SPAN], F32))

    sLD = Sem(nc, es, "sLD")
    sPE = Sem(nc, es, "sPE")
    sDV = Sem(nc, es, "sDV")
    sAC = Sem(nc, es, "sAC")
    sGP = Sem(nc, es, "sGP")
    sST = Sem(nc, es, "sST")

    P = Prog()

    def dma(dst, src):
        def fn(e, dst=dst, src=src):
            e.dma_start(out=dst, in_=src).then_inc(sLD.h, 16)
        P.add("sync", fn)
        sLD.n += 16

    def dma_store(dst, src):
        def fn(e, dst=dst, src=src):
            e.dma_start(out=dst, in_=src).then_inc(sST.h, 16)
        P.add("sync", fn)
        sST.n += 16

    def ew(eng, sem, cnt):
        if cnt <= 0:
            return
        def fn(e, s=sem.h, c=cnt):
            e.wait_ge(s, c)
        P.add(eng, fn)

    ENG2SEM = {}

    def run(eng, sem, fn, waits=()):
        for (ws, wc) in waits:
            ew(eng, ws, wc)
        def wrapped(e, fn=fn, s=sem):
            last = fn(e)
            if last is None:
                last = e.nop()
            last.then_inc(s.h, 1)
        P.add(eng, wrapped)
        sem.n += 1

    def gp_init(g):
        g.memset(ident[:], 0.0)
        g.drain()
        g.affine_select(out=ident[:], in_=ident[:], compare_op=ALU.not_equal,
                        fill=1.0, base=0, pattern=[[-1, 128]], channel_multiplier=1)
        g.memset(ones512[:], 1.0)
        g.memset(eps_c[:], EPS)
        g.memset(one_c[:], 1.0)
        g.memset(n2l2_c[:], -2.0 * LN2C)
        g.memset(d2eps_c[:], float(D) * float(D) * EPS)
        g.memset(lnd_c[:], float(np.log(D)))
        return g.memset(p2l2_c[:], 2.0 * LN2C)
    run("gpsimd", sGP, gp_init)
    ew("scalar", sGP, sGP.n)
    ew("vector", sGP, sGP.n)

    for i in range(DC):
        dma(Fp[0][:], wx_h[i * 128:(i + 1) * 128, :])
        run("vector", sDV, (lambda v, i=i: v.tensor_copy(wx_bf[i][:], Fp[0][:])),
            waits=[(sLD, sLD.n)])
        ew("sync", sDV, sDV.n)
    for i in range(DC):
        dma(Fp[0][:, 0:96], wxp_h[i * 128:(i + 1) * 128, :])
        run("vector", sDV, (lambda v, i=i: v.tensor_copy(wxp_bf[i][:], Fp[0][:, 0:96])),
            waits=[(sLD, sLD.n)])
        ew("sync", sDV, sDV.n)
    dma(Fp[0][0:DTR, :], wdt_h[:, :])
    run("vector", sDV, (lambda v: v.tensor_copy(wdt_bf[:], Fp[0][0:DTR, :])),
        waits=[(sLD, sLD.n)])
    ew("sync", sDV, sDV.n)
    dma(Fp[0][:], lng_h.rearrange("(one d) -> one d", one=1).broadcast_to([128, D]))
    run("vector", sDV, (lambda v: v.tensor_copy(gam_bc[:], Fp[0][:])),
        waits=[(sLD, sLD.n)])
    ew("sync", sDV, sDV.n)

    for i in range(DC):
        sl = slice(i * 128, (i + 1) * 128)
        dma(A_f[i][:], alog_h[sl, :])
        dma(cw_c[i][:], cw_h[sl, :])
        dma(cb_c[i][:], cb_h[sl].rearrange("(p one) -> p one", one=1))
        dma(bdt_c[i][:], bdt_h[sl].rearrange("(p one) -> p one", one=1))
        dma(dp_c[i][:], dp_h[sl].rearrange("(p one) -> p one", one=1))
        dma(Fp[1][:, 0:1], lnb_h[sl].rearrange("(p one) -> p one", one=1))
        run("vector", sDV, (lambda v, i=i: v.tensor_copy(beta_c[i][:], Fp[1][:, 0:1])),
            waits=[(sLD, sLD.n)])
        ew("sync", sDV, sDV.n)
    dma(bx_row, bx_h.rearrange("(one d) -> one d", one=1))
    dma(Fp[1][0:1, 0:D], fb_h.rearrange("(one d) -> one d", one=1))
    dma(da_t[:], da_h.rearrange("(one o) -> one o", one=1))
    run("vector", sDV, (lambda v: v.tensor_copy(fb_bf[:], Fp[1][0:1, 0:D])),
        waits=[(sLD, sLD.n)])

    def fA(s):
        last = None
        for i in range(DC):
            last = s.activation(Fp[2][:, i * NS:(i + 1) * NS], A_f[i][:], AF.Exp)
        return last
    run("scalar", sAC, fA, waits=[(sLD, sLD.n)])

    def fAn(v):
        last = None
        for i in range(DC):
            last = v.tensor_scalar(A_f[i][:], Fp[2][:, i * NS:(i + 1) * NS], -1.0, None, ALU.mult)
        return last
    run("vector", sDV, fAn, waits=[(sAC, sAC.n)])

    run("scalar", sAC, (lambda s: s.activation(da_t2[:], da_t[:], AF.Tanh, scale=4.0)),
        waits=[(sLD, sLD.n)])
    run("vector", sDV, (lambda v: v.tensor_scalar(da_t[:], da_t2[:], 0.5, None, ALU.mult)),
        waits=[(sAC, sAC.n)])
    ew("sync", sDV, sDV.n)
    dma(da_d[:], da_t[:])
    ew("sync", sLD, sLD.n)
    dma(da_col[:], da_d[0:1, :].broadcast_to([128, 1]))

    def fV0(t):
        last = None
        for i in range(DC):
            t.matmul(psWx[0:1, :], beta_c[i][:], wx_bf[i][:, 0:SPAN],
                     start=(i == 0), stop=(i == DC - 1))
        for i in range(DC):
            last = t.matmul(psZ[0:1, :], beta_c[i][:], wx_bf[i][:, SPAN:2 * SPAN],
                            start=(i == 0), stop=(i == DC - 1))
        return last
    run("tensor", sPE, fV0, waits=[(sDV, sDV.n), (sGP, sGP.n)])

    def fV0e(v):
        v.tensor_tensor(v0_fr[0:1, 0:SPAN], psWx[0:1, :], bx_row[0:1, 0:SPAN], ALU.add)
        v.tensor_tensor(v0_fr[0:1, SPAN:2 * SPAN], psZ[0:1, :], bx_row[0:1, SPAN:2 * SPAN], ALU.add)
        return v.tensor_copy(v0_bf[:], v0_fr[0:1, 0:D])
    run("vector", sDV, fV0e, waits=[(sPE, sPE.n), (sLD, sLD.n)])

    def quant_chain(pairs, dd_ap, scratch, d4_waits=()):
        npair = len(pairs)
        G = 32 * npair
        am, am2, l_t, eh, p2, m_t, fx, e_t = sq
        S1, S2, S3, S4 = scratch

        def d1(v, pairs=tuple(pairs), G=G):
            for p, (srcp, _) in enumerate(pairs):
                v.tensor_reduce(am[:, p * 32:(p + 1) * 32],
                                srcp.rearrange("p (a b) -> p a b", b=32),
                                AX.X, ALU.max, apply_absolute_value=True)
            v.drain()
            v.tensor_scalar(am2[:, 0:G], am[:, 0:G], 1e-30, None, ALU.max)
            v.drain()
            return v.nop()
        run("vector", sDV, d1)

        def a1(s, G=G):
            return s.activation(l_t[:, 0:G], am2[:, 0:G], AF.Ln)
        run("scalar", sAC, a1, waits=[(sDV, sDV.n)])

        def d2(v, G=G):
            v.tensor_scalar(eh[:, 0:G], l_t[:, 0:G], 1.0 / LN2C, M23, ALU.mult, ALU.add)
            v.drain()
            v.tensor_scalar(eh[:, 0:G], eh[:, 0:G], M23, None, ALU.subtract)
            v.drain()
            return v.nop()
        run("vector", sDV, d2, waits=[(sAC, sAC.n)])

        def a2(s, G=G):
            return s.activation(p2[:, 0:G], eh[:, 0:G], AF.Exp, scale=-LN2C)
        run("scalar", sAC, a2, waits=[(sDV, sDV.n)])

        def d3(v, G=G, dd=dd_ap):
            v.tensor_tensor(m_t[:, 0:G], am2[:, 0:G], p2[:, 0:G], ALU.mult)
            v.drain()
            v.tensor_scalar(fx[:, 0:G], m_t[:, 0:G], 1.0, None, ALU.is_lt)
            v.drain()
            v.tensor_tensor(e_t[:, 0:G], eh[:, 0:G], fx[:, 0:G], ALU.subtract)
            v.drain()
            if dd.shape[1] == 1:
                v.tensor_scalar(e_t[:, 0:G], e_t[:, 0:G], dd, None, ALU.add)
            else:
                v.tensor_tensor(e_t[:, 0:G], e_t[:, 0:G], dd, ALU.add)
            v.drain()
            return v.nop()
        run("vector", sDV, d3, waits=[(sAC, sAC.n)])

        def a3(s, G=G):
            s.activation(l_t[:, 0:G], e_t[:, 0:G], AF.Exp, scale=LN2C, bias=n2l2_c[:])
            return s.activation(p2[:, 0:G], e_t[:, 0:G], AF.Exp, scale=-LN2C, bias=p2l2_c[:])
        run("scalar", sAC, a3, waits=[(sDV, sDV.n)])

        def d4(v, pairs=tuple(pairs)):
            last = None
            for p, (srcp, dstp) in enumerate(pairs):
                x3 = srcp.rearrange("p (a b) -> p a b", b=32)
                isc = p2[:, p * 32:(p + 1) * 32].unsqueeze(2).broadcast_to([128, 32, 32])
                scl = l_t[:, p * 32:(p + 1) * 32].unsqueeze(2).broadcast_to([128, 32, 32])
                xs = S1
                v.tensor_tensor(xs.rearrange("p (a b) -> p a b", b=32), x3, isc, ALU.mult)
                v.tensor_scalar(xs, xs, 6.0, -6.0, ALU.min, ALU.max)
                aq = S2
                v.tensor_tensor(aq, xs, xs, ALU.mult)
                tmp = S3
                v.tensor_scalar(tmp, aq, 4.0, 0.5, ALU.is_ge, ALU.mult)
                stp = S2
                v.scalar_tensor_tensor(stp, aq, 16.0, tmp, ALU.is_ge, ALU.add)
                v.tensor_scalar(stp, stp, 0.5, None, ALU.add)
                Ms = S3
                v.tensor_scalar(Ms, stp, M23, None, ALU.mult)
                qt = S4
                v.tensor_tensor(qt, xs, Ms, ALU.add)
                v.tensor_tensor(qt, qt, Ms, ALU.subtract)
                last = v.tensor_tensor(dstp.rearrange("p (a b) -> p a b", b=32),
                                       qt.rearrange("p (a b) -> p a b", b=32), scl, ALU.mult)
            return last
        run("vector", sDV, d4, waits=[(sAC, sAC.n)] + list(d4_waits))

    for oc in range(DC):
        dma(Fp[0][:], fw_h[oc * 128:(oc + 1) * 128, :])
        dma(dwq[:], dw_h[oc * 128:(oc + 1) * 128, :])
        run("scalar", sAC, (lambda s: s.activation(sq[1][:, 0:32], dwq[:], AF.Tanh, scale=4.0)),
            waits=[(sLD, sLD.n)])
        run("vector", sDV, (lambda v: v.tensor_scalar(dwq[:], sq[1][:, 0:32], 0.5, None, ALU.mult)),
            waits=[(sAC, sAC.n)])
        wq_dst = bigs[0][:, 0:D]
        quant_chain([(Fp[0][:], wq_dst)], dwq[:], (Fp[1][:], Fp[2][:], Fp[3][:], Fp[4][:]),
                    d4_waits=[(sPE, sPE.n)])

        def t_wq(t, src=wq_dst):
            last = None
            for g in range(2):
                ps = psA if g == 0 else psB
                for q in range(4):
                    dcx = g * 4 + q
                    last = t.transpose(ps[:, q * 128:(q + 1) * 128],
                                       src[:, dcx * 128:(dcx + 1) * 128], ident[:])
            return last
        run("tensor", sPE, t_wq, waits=[(sDV, sDV.n)])

        def e_wq(v, oc=oc):
            last = None
            for g in range(2):
                ps = psA if g == 0 else psB
                for q in range(4):
                    dcx = g * 4 + q
                    last = v.tensor_copy(wqT[dcx][:, oc * 128:(oc + 1) * 128],
                                         ps[:, q * 128:(q + 1) * 128])
            return last
        run("vector", sDV, e_wq, waits=[(sPE, sPE.n)])
        ew("sync", sDV, sDV.n)

    def ln_smalls(col_n, c0=0):
        def d(v, n=col_n, c0=c0):
            sl = slice(c0, c0 + n)
            v.tensor_tensor(t4a_s[:, sl], sumx_s[:, sl], sumx_s[:, sl], ALU.mult)
            v.tensor_scalar(mu_s[:, sl], sumx_s[:, sl], 1.0 / D, None, ALU.mult)
            v.drain()
            v.scalar_tensor_tensor(t4b_s[:, sl], sumq_s[:, sl], float(D), t4a_s[:, sl],
                                   ALU.mult, ALU.subtract)
            v.drain()
            return v.nop()
        run("vector", sDV, d, waits=[(sAC, sAC.n)])

        def a(s, n=col_n, c0=c0):
            sl = slice(c0, c0 + n)
            s.activation(t4a_s[:, sl], t4b_s[:, sl], AF.Ln, bias=d2eps_c[:])
            s.drain()
            s.activation(rstd_s[:, sl], t4a_s[:, sl], AF.Exp, scale=-0.5, bias=lnd_c[:])
            s.drain()
            return s.nop()
        run("scalar", sAC, a, waits=[(sDV, sDV.n)])

    for b in range(NB):
        for sp in range(NSPAN):
            t0 = sp * SPAN
            ew("sync", sDV, sDV.n)
            ew("sync", sAC, sAC.n)
            for q in range(TPS):
                row = b * T + t0 + q * 128
                dma(Fp[q][:], f_h[row:row + 128, :])
            ld_all = sLD.n
            def sq1(s):
                last = None
                for q in range(TPS):
                    s.activation(Fp[4][:], Fp[q][:], AF.Copy, accum_out=sumx_s[:, q:q + 1])
                    last = s.activation(Fp[4][:], Fp[q][:], AF.Square,
                                        accum_out=sumq_s[:, q:q + 1])
                return last
            run("scalar", sAC, sq1, waits=[(sLD, ld_all), (sDV, sDV.n)])
            ln_smalls(TPS)
            def ap1(v):
                last = None
                for q in range(TPS):
                    v.tensor_scalar(Fp[q][:], Fp[q][:], mu_s[:, q:q + 1], None, ALU.subtract)
                    last = v.tensor_scalar(Fp[q][:], Fp[q][:], rstd_s[:, q:q + 1], None, ALU.mult)
                return last
            run("vector", sDV, ap1, waits=[(sAC, sAC.n)])
            def sq2(s):
                last = None
                for q in range(TPS):
                    s.activation(Fp[4][:], Fp[q][:], AF.Copy, accum_out=sumx_s[:, q:q + 1])
                    last = s.activation(Fp[4][:], Fp[q][:], AF.Square,
                                        accum_out=sumq_s[:, q:q + 1])
                return last
            run("scalar", sAC, sq2, waits=[(sDV, sDV.n)])
            ln_smalls(TPS)
            def ap2(v):
                last = None
                for q in range(TPS):
                    v.tensor_scalar(Fp[q][:], Fp[q][:], mu_s[:, q:q + 1], None, ALU.subtract)
                    v.tensor_scalar(Fp[q][:], Fp[q][:], rstd_s[:, q:q + 1], None, ALU.mult)
                    last = v.tensor_tensor(fng_t[q][:], Fp[q][:], gam_bc[:], ALU.mult)
                return last
            run("vector", sDV, ap2, waits=[(sAC, sAC.n)])

            fng_done = sDV.n
            evac_marks = {}
            for dcx in range(DC):
                ps = psA if dcx % 2 == 0 else psB
                waits = [(sDV, fng_done)]
                if dcx >= 2:
                    waits.append((sAC, evac_marks[dcx - 2]))

                def tr(t, dcx=dcx, ps=ps):
                    last = None
                    for q in range(TPS):
                        last = t.transpose(ps[:, q * 128:(q + 1) * 128],
                                           fng_t[q][:, dcx * 128:(dcx + 1) * 128], ident[:])
                    return last
                run("tensor", sPE, tr, waits=waits)

                def ev(s, dcx=dcx, ps=ps):
                    return s.activation(fnT[dcx], ps[:], AF.Copy)
                run("scalar", sAC, ev, waits=[(sPE, sPE.n)])
                evac_marks[dcx] = sAC.n

            if sp == 0:
                def cc(v):
                    last = None
                    for jc in range(DC):
                        last = v.memset(x1buf[jc][:, 0:3], 0.0)
                    return last
                run("vector", sDV, cc)
            else:
                def cc(v):
                    last = None
                    for jc in range(DC):
                        last = v.tensor_copy(x1buf[jc][:, 0:3], x1buf[jc][:, SPAN:SPAN + 3])
                    return last
                run("vector", sDV, cc)

            fnT_done_ac = sAC.n
            fnT_done = sDV.n
            wx_evac = {}
            cvsets = [(a_t[:, 0:SPAN], a_t[:, SPAN:2 * SPAN]),
                      (a_t[:, 2 * SPAN:3 * SPAN], a_t[:, 3 * SPAN:4 * SPAN])]

            def emit_cnv_sil(jc):
                cvA, cvB = cvsets[jc % 2]
                def cnv(v, jc=jc, cvA=cvA, cvB=cvB):
                    v.tensor_scalar(cvA, x1buf[jc][:, 0:SPAN], cw_c[jc][:, 0:1], None, ALU.mult)
                    v.scalar_tensor_tensor(cvB, x1buf[jc][:, 1:1 + SPAN], cw_c[jc][:, 1:2],
                                           cvA, ALU.mult, ALU.add)
                    v.tensor_scalar(cvA, x1buf[jc][:, 2:2 + SPAN], cw_c[jc][:, 2:3], None, ALU.mult)
                    v.tensor_tensor(cvA, cvA, cvB, ALU.add)
                    return v.scalar_tensor_tensor(cvB, x1buf[jc][:, 3:3 + SPAN], cw_c[jc][:, 3:4],
                                                  cvA, ALU.mult, ALU.add)
                run("vector", sDV, cnv, waits=[(sAC, wx_evac[jc])])
                def sil(s, jc=jc, t0=t0, cvB=cvB):
                    return s.activation(xcy[jc][:, t0:t0 + SPAN], cvB, AF.Silu, bias=cb_c[jc][:])
                run("scalar", sAC, sil, waits=[(sDV, sDV.n)])

            for jc in range(DC):
                ps = psWx if jc % 2 == 0 else psZ
                waits = [(sDV, fnT_done), (sAC, fnT_done_ac)]
                if jc >= 2:
                    waits.append((sAC, wx_evac[jc - 2]))
                else:
                    waits.append((sAC, sAC.n))

                def mmwx(t, jc=jc, ps=ps):
                    for dcx in range(DC):
                        t.matmul(ps[:], wx_bf[dcx][:, jc * 128:(jc + 1) * 128], fnT[dcx],
                                 start=(dcx == 0), stop=False)
                    return t.matmul(ps[:], v0_bf[0:1, jc * 128:(jc + 1) * 128], ones512[0:1, :],
                                    start=False, stop=True)
                run("tensor", sPE, mmwx, waits=waits)

                def evx(s, jc=jc, ps=ps):
                    return s.activation(x1buf[jc][:, 3:3 + SPAN], ps[:], AF.Copy)
                run("scalar", sAC, evx, waits=[(sPE, sPE.n), (sDV, sDV.n)])
                wx_evac[jc] = sAC.n
                if jc >= 1:
                    emit_cnv_sil(jc - 1)
            emit_cnv_sil(DC - 1)

            def mmxp(t, t0=t0):
                last = None
                for dcx in range(DC):
                    last = t.matmul(psXd[:], wxp_bf[dcx][:], xcy[dcx][:, t0:t0 + SPAN],
                                    start=(dcx == 0), stop=(dcx == DC - 1))
                return last
            run("tensor", sPE, mmxp, waits=[(sAC, sAC.n)])

            run("scalar", sAC, (lambda s: s.activation(dtT, psXd[0:DTR, :], AF.Copy)),
                waits=[(sPE, sPE.n)])
            run("vector", sDV, (lambda v: v.tensor_copy(bcsb, psXd[DTR:96, :])),
                waits=[(sPE, sPE.n), (sLD, sLD.n)])
            ew("sync", sDV, sDV.n)
            def st_bc(e, b=b, t0=t0):
                e.dma_start(out=bcd[b][:, t0:t0 + SPAN], in_=bcsb).then_inc(sLD.h, 16)
            P.add("sync", st_bc)
            sLD.n += 16

            sp_done = sAC.n
            sp_evac = {}
            spe_marks = {}

            def emit_spd_spe(dcx):
                par = dcx % 2
                h0 = par * SPAN
                zb = Fp[0][:, h0:h0 + SPAN]
                mx = Fp[1][:, h0:h0 + SPAN]
                mn = Fp[2][:, h0:h0 + SPAN]
                ngab = Fp[3][:, h0:h0 + SPAN]
                lnb = Fp[4][:, h0:h0 + SPAN]
                def spd(v, dcx=dcx, zb=zb, mx=mx, mn=mn, ngab=ngab):
                    v.tensor_scalar(mx, zb, bdt_c[dcx][:], 0.0, ALU.add, ALU.max)
                    v.tensor_scalar(mn, zb, bdt_c[dcx][:], 0.0, ALU.add, ALU.min)
                    return v.tensor_tensor(ngab, mn, mx, ALU.subtract)
                run("vector", sDV, spd, waits=[(sAC, sp_evac[dcx])])
                spd_m = sDV.n
                def spe(s, ngab=ngab, lnb=lnb):
                    s.activation(lnb, ngab, AF.Exp)
                    s.drain()
                    s.activation(ngab, lnb, AF.Ln, bias=one_c[:])
                    s.drain()
                    return s.nop()
                run("scalar", sAC, spe, waits=[(sDV, spd_m)])
                spe_marks[dcx] = sAC.n

            def emit_spf(dcx, t0):
                par = dcx % 2
                h0 = par * SPAN
                ngab = Fp[3][:, h0:h0 + SPAN]
                mx = Fp[1][:, h0:h0 + SPAN]
                def spf(v, dcx=dcx, t0=t0, ngab=ngab, mx=mx):
                    return v.tensor_tensor(dltT[dcx][:, t0:t0 + SPAN], ngab, mx, ALU.add)
                run("vector", sDV, spf, waits=[(sAC, spe_marks[dcx])])

            for dcx in range(DC):
                ps = psWx if dcx % 2 == 0 else psZ
                waits = [(sAC, sp_done)]
                if dcx >= 2:
                    waits.append((sAC, sp_evac[dcx - 2]))

                def mmdt(t, dcx=dcx, ps=ps):
                    return t.matmul(ps[:], wdt_bf[:, dcx * 128:(dcx + 1) * 128], dtT,
                                    start=True, stop=True)
                run("tensor", sPE, mmdt, waits=waits)

                def spa(s, dcx=dcx, ps=ps):
                    par = dcx % 2
                    return s.activation(Fp[0][:, par * SPAN:(par + 1) * SPAN], ps[:], AF.Copy)
                run("scalar", sAC, spa, waits=[(sPE, sPE.n), (sDV, sDV.n)])
                sp_evac[dcx] = sAC.n
                emit_spd_spe(dcx)
                if dcx >= 1:
                    emit_spf(dcx - 1, t0)
            emit_spf(DC - 1, t0)

        def s2init(v):
            for dcx in range(DC):
                v.tensor_tensor(u_t[dcx][:], dltT[dcx][:], xcy[dcx][:], ALU.mult)
            last = None
            for dcx in range(DC):
                last = v.tensor_scalar(xcy[dcx][:], xcy[dcx][:], dp_c[dcx][:], None, ALU.mult)
            return last
        run("vector", sDV, s2init)
        ew("scalar", sDV, sDV.n)

        psy8 = [psWx, psZ, psF0, psF1, psA_f, psB_f, psXd_f, psY0]
        a_bufs = [Fp[0][:, 0:SPAN].bitcast(BF16), Fp[0][:, SPAN:2 * SPAN].bitcast(BF16),
                  Cbc[:, 0:T // 2], Cbc[:, T // 2:T]]
        z_bufs = [Fp[1][:, 0:SPAN].bitcast(BF16), Fp[1][:, SPAN:2 * SPAN].bitcast(BF16)]
        HT = T // 2
        bc_bufs = [(Fp[2][:, 0:SPAN].bitcast(BF16), Fp[3][:, 0:SPAN].bitcast(BF16)),
                   (Fp[2][:, SPAN:2 * SPAN].bitcast(BF16), Fp[3][:, SPAN:2 * SPAN].bitcast(BF16)),
                   (Fp[4][:, 0:SPAN].bitcast(BF16), Fp[4][:, SPAN:2 * SPAN].bitcast(BF16)),
                   (b_t[:, HT:2 * HT], h_t[:, HT:2 * HT])]
        NPF = 4
        if b == 0:
            ew("sync", sPE, sPE.n)
        ew("sync", sLD, sLD.n)
        ew("sync", sAC, sAC.n)
        ew("scalar", sDV, sDV.n)
        ew("scalar", sPE, sPE.n)
        s2init_dv = sDV.n
        for dcx in range(DC):
            for half in range(2):
                ht0 = half * HT
                ew("sync", sDV, sDV.n)
                bc_ld = {}
                for pf in range(NPF):
                    dma(bc_bufs[pf][0], bcd[b][pf:pf + 1, ht0:ht0 + HT].broadcast_to([128, HT]))
                    dma(bc_bufs[pf][1], bcd[b][16 + pf:17 + pf, ht0:ht0 + HT].broadcast_to([128, HT]))
                    bc_ld[pf] = sLD.n

                def tinit(t, dcx=dcx, ht0=ht0):
                    last = None
                    for q in range(8):
                        last = t.matmul(psy8[q][:, 0:128],
                                        xcy[dcx][:, ht0 + q * 128:ht0 + (q + 1) * 128],
                                        ident[:], start=True, stop=False)
                    return last
                run("tensor", sPE, tinit, waits=[(sDV, s2init_dv), (sAC, sAC.n)])

                scn_marks = {}
                zt_marks = {}
                a_marks = {}
                for n in range(NS):
                    aw = []
                    if n >= 4:
                        aw.append((sDV, scn_marks[n - 4]))
                    def abuild(s, n=n, dcx=dcx, ht0=ht0):
                        return s.activation(a_bufs[n % 4], dltT[dcx][:, ht0:ht0 + HT],
                                            AF.Copy, scale=A_f[dcx][:, n:n + 1])
                    run("scalar", sAC, abuild, waits=aw)
                    a_marks[n] = sAC.n

                    dw = [(sLD, bc_ld[n]), (sAC, a_marks[n])]
                    if n >= 2:
                        dw.append((sPE, zt_marks[n - 2]))
                    def scn(v, n=n, dcx=dcx, half=half, ht0=ht0):
                        Bb, Cb = bc_bufs[n % NPF]
                        v.tensor_tensor(b_t[:, 0:HT], u_t[dcx][:, ht0:ht0 + HT], Bb, ALU.mult)
                        init = 0.0 if half == 0 else hmid[:, n:n + 1]
                        v.tensor_tensor_scan(h_t[:, 0:HT], a_bufs[n % 4], b_t[:, 0:HT],
                                             init, ALU.mult, ALU.add)
                        last = v.tensor_tensor(z_bufs[n % 2], h_t[:, 0:HT], Cb, ALU.mult)
                        if half == 0:
                            last = v.tensor_copy(hmid[:, n:n + 1], h_t[:, HT - 1:HT])
                        return last
                    run("vector", sDV, scn, waits=dw)
                    scn_marks[n] = sDV.n

                    def zt(t, n=n):
                        last = None
                        for q in range(8):
                            last = t.matmul(psy8[q][:, 0:128],
                                            z_bufs[n % 2][:, q * 128:(q + 1) * 128], ident[:],
                                            start=False, stop=(n == NS - 1))
                        return last
                    run("tensor", sPE, zt, waits=[(sDV, scn_marks[n])])
                    zt_marks[n] = sPE.n

                    if n + NPF < NS:
                        ew("sync", sDV, scn_marks[n])
                        dma(bc_bufs[n % NPF][0],
                            bcd[b][n + NPF:n + NPF + 1, ht0:ht0 + HT].broadcast_to([128, HT]))
                        dma(bc_bufs[n % NPF][1],
                            bcd[b][16 + n + NPF:17 + n + NPF, ht0:ht0 + HT].broadcast_to([128, HT]))
                        bc_ld[n + NPF] = sLD.n

                def yev(s, dcx=dcx, half=half):
                    last = None
                    for q in range(8):
                        tt = half * 8 + q
                        last = s.activation(ytm_t[tt][:, dcx * 128:(dcx + 1) * 128],
                                            psy8[q][:, 0:128], AF.Copy)
                    return last
                run("scalar", sAC, yev, waits=[(sPE, sPE.n)])

        ew("vector", sAC, sAC.n)
        x0b4 = [xcy[2 + q // 2][:, (q % 2) * D:(q % 2 + 1) * D] for q in range(4)]
        ln34 = [xcy[4 + q // 2][:, (q % 2) * D:(q % 2 + 1) * D] for q in range(4)]
        q_scr = (b_t[:].bitcast(F32), h_t[:].bitcast(F32), a_t[:].bitcast(F32), Fp[4][:])
        for sp in range(NSPAN):
            t0 = sp * SPAN
            ew("sync", sDV, sDV.n)
            ew("sync", sAC, sAC.n)
            for q in range(TPS):
                row = b * T + t0 + q * 128
                dma(Fp[q][:], f_h[row:row + 128, :])
            ld_all = sLD.n
            def sq3(s, sp=sp):
                last = None
                for q in range(TPS):
                    s.activation(Fp[4][:], ytm_t[sp * TPS + q][:], AF.Copy,
                                 accum_out=sumx_s[:, q:q + 1])
                    last = s.activation(Fp[4][:], ytm_t[sp * TPS + q][:], AF.Square,
                                        accum_out=sumq_s[:, q:q + 1])
                return last
            run("scalar", sAC, sq3, waits=[(sDV, sDV.n)])
            ln_smalls(TPS)
            def ap3(v, sp=sp):
                last = None
                for q in range(TPS):
                    tt = sp * TPS + q
                    v.tensor_scalar(ln34[q], ytm_t[tt][:], mu_s[:, q:q + 1], None, ALU.subtract)
                    last = v.tensor_scalar(ln34[q], ln34[q], rstd_s[:, q:q + 1], None, ALU.mult)
                return last
            run("vector", sDV, ap3, waits=[(sAC, sAC.n)])
            def cst(s):
                last = None
                for q in range(TPS):
                    last = s.activation(x0b4[q], Fp[q][:], AF.Copy)
                return last
            run("scalar", sAC, cst, waits=[(sLD, ld_all), (sPE, sPE.n)])
            def res1(v, sp=sp):
                last = None
                for q in range(TPS):
                    last = v.tensor_tensor(x1_t[sp * TPS + q][:], ln34[q], x0b4[q], ALU.add)
                return last
            run("vector", sDV, res1, waits=[(sAC, sAC.n)])
            def sq4(s, sp=sp):
                last = None
                for q in range(TPS):
                    s.activation(Fp[4][:], x1_t[sp * TPS + q][:], AF.Copy,
                                 accum_out=sumx_s[:, q:q + 1])
                    last = s.activation(Fp[4][:], x1_t[sp * TPS + q][:], AF.Square,
                                        accum_out=sumq_s[:, q:q + 1])
                return last
            run("scalar", sAC, sq4, waits=[(sDV, sDV.n)])
            ln_smalls(TPS)
            def ap4(v, sp=sp):
                last = None
                for q in range(TPS):
                    tt = sp * TPS + q
                    v.tensor_scalar(Fp[q][:], x1_t[tt][:], mu_s[:, q:q + 1], None, ALU.subtract)
                    last = v.tensor_scalar(Fp[q][:], Fp[q][:], rstd_s[:, q:q + 1], None, ALU.mult)
                return last
            run("vector", sDV, ap4, waits=[(sAC, sAC.n)])
            quant_chain([(Fp[q][:], xq_t[q][:]) for q in range(TPS)], da_col[:],
                        q_scr, d4_waits=[(sPE, sPE.n)])

            xq_done = sDV.n
            evac_marks = {}
            for dcx in range(DC):
                ps = psA if dcx % 2 == 0 else psB
                waits = [(sDV, xq_done)]
                if dcx >= 2:
                    waits.append((sAC, evac_marks[dcx - 2]))

                def trq(t, dcx=dcx, ps=ps):
                    last = None
                    for q in range(TPS):
                        last = t.transpose(ps[:, q * 128:(q + 1) * 128],
                                           xq_t[q][:, dcx * 128:(dcx + 1) * 128], ident[:])
                    return last
                run("tensor", sPE, trq, waits=waits)

                def evq(s, dcx=dcx, ps=ps):
                    return s.activation(fnT[dcx], ps[:], AF.Copy)
                run("scalar", sAC, evq, waits=[(sPE, sPE.n)])
                evac_marks[dcx] = sAC.n

            xqT_done = sDV.n
            xqT_done_ac = sAC.n
            evf_marks = {}
            for q in range(TPS):
                tt = sp * TPS + q
                row = b * T + t0 + q * 128
                pF0, pF1 = (psF0, psF1) if q % 2 == 0 else (psA_f, psB_f)
                waits = [(sDV, xqT_done), (sAC, max(xqT_done_ac, sAC.n if q < 2 else evf_marks[q - 2]))]

                def mmf(t, q=q, pF0=pF0, pF1=pF1):
                    for dcx in range(DC):
                        sl = slice(q * 128, (q + 1) * 128)
                        t.matmul(pF0[:], fnT[dcx][:, sl], wqT[dcx][:, 0:SPAN],
                                 start=(dcx == 0), stop=False)
                        t.matmul(pF1[:], fnT[dcx][:, sl], wqT[dcx][:, SPAN:2 * SPAN],
                                 start=(dcx == 0), stop=False)
                    t.matmul(pF0[:], ones512[0:1, 0:128], fb_bf[0:1, 0:SPAN],
                             start=False, stop=True)
                    return t.matmul(pF1[:], ones512[0:1, 0:128], fb_bf[0:1, SPAN:2 * SPAN],
                                    start=False, stop=True)
                run("tensor", sPE, mmf, waits=waits)

                def evf(s, q=q, pF0=pF0, pF1=pF1):
                    s.activation(Fp[q][:, 0:SPAN], pF0[:], AF.Copy, accum_out=t4a_s[:, q:q + 1])
                    s.activation(Fp[q][:, SPAN:2 * SPAN], pF1[:], AF.Copy,
                                 accum_out=t4b_s[:, q:q + 1])
                    return s.activation(b_t[:].bitcast(F32), Fp[q][:], AF.Square,
                                        accum_out=sumq_s[:, q:q + 1])
                run("scalar", sAC, evf, waits=[(sPE, sPE.n)])
                evf_marks[q] = sAC.n

                def evs(v, q=q):
                    v.tensor_tensor(sumx_s[:, q:q + 1], t4a_s[:, q:q + 1],
                                    t4b_s[:, q:q + 1], ALU.add)
                    v.drain()
                    return v.nop()
                run("vector", sDV, evs, waits=[(sAC, evf_marks[q])])
                ln_smalls(1, c0=q)
                if sST.n > 0:
                    ew("vector", sST, sST.n)
                def ap5(v, tt=tt, q=q):
                    scr5 = h_t[:].bitcast(F32)
                    v.tensor_scalar(scr5, Fp[q][:], mu_s[:, q:q + 1], None, ALU.subtract)
                    v.tensor_scalar(scr5, scr5, rstd_s[:, q:q + 1], None, ALU.mult)
                    return v.tensor_tensor(out_stg[:], scr5, x1_t[tt][:], ALU.add)
                run("vector", sDV, ap5, waits=[(sAC, sAC.n)])
                ew("sync", sDV, sDV.n)
                dma_store(out_h[row:row + 128, :], out_stg[:])

    def fin(e):
        e.wait_ge(sST.h, sST.n)
    P.add("sync", fin)

    with nc.Block() as blk:
        blk.sync(lambda e: [fn(e) for fn in P.items["sync"]])
        blk.vector(lambda e: [fn(e) for fn in P.items["vector"]])
        blk.scalar(lambda e: [fn(e) for fn in P.items["scalar"]])
        blk.tensor(lambda e: [fn(e) for fn in P.items["tensor"]])
        blk.gpsimd(lambda e: [fn(e) for fn in P.items["gpsimd"]])
    es.close()
    return nc


def kernel(L0_attn_f, L0_attn_ln_g, L0_attn_ln_b, L0_attn_Wx, L0_attn_bx,
           L0_attn_conv_w, L0_attn_conv_b, L0_attn_Wxp, L0_attn_Wdt, L0_attn_bdt,
           L0_attn_A_log, L0_attn_Dp, L0_ffn_weight, L0_ffn_bias, L0_ffn_delta_w,
           L0_ffn_delta_act):
    if "nc" not in _CACHE:
        _CACHE["nc"] = build_nc()
    nc = _CACHE["nc"]
    f = np.ascontiguousarray(np.asarray(L0_attn_f, dtype=np.float32))
    common = {
        "ln_g": np.ascontiguousarray(L0_attn_ln_g, dtype=np.float32),
        "ln_b": np.ascontiguousarray(L0_attn_ln_b, dtype=np.float32),
        "Wx": np.ascontiguousarray(L0_attn_Wx, dtype=np.float32),
        "bx": np.ascontiguousarray(L0_attn_bx, dtype=np.float32),
        "conv_w": np.ascontiguousarray(L0_attn_conv_w, dtype=np.float32),
        "conv_b": np.ascontiguousarray(L0_attn_conv_b, dtype=np.float32),
        "Wxp": np.ascontiguousarray(L0_attn_Wxp, dtype=np.float32),
        "Wdt": np.ascontiguousarray(L0_attn_Wdt, dtype=np.float32),
        "bdt": np.ascontiguousarray(L0_attn_bdt, dtype=np.float32),
        "A_log": np.ascontiguousarray(L0_attn_A_log, dtype=np.float32),
        "Dp": np.ascontiguousarray(L0_attn_Dp, dtype=np.float32),
        "fw": np.ascontiguousarray(L0_ffn_weight, dtype=np.float32),
        "fb": np.ascontiguousarray(L0_ffn_bias, dtype=np.float32),
        "dw": np.ascontiguousarray(L0_ffn_delta_w, dtype=np.float32),
        "da": np.ascontiguousarray(L0_ffn_delta_act, dtype=np.float32),
    }
    in_maps = []
    for c in range(8):
        m = dict(common)
        m["f"] = np.ascontiguousarray(f[2 * c:2 * c + 2].reshape(NB * T, D))
        in_maps.append(m)
    res = run_bass_kernel_spmd(nc, in_maps, core_ids=list(range(8)))
    out = np.concatenate([r["out"].reshape(NB, T, D) for r in res.results], axis=0)
    return np.ascontiguousarray(out, dtype=np.float32)


# version 09
# speedup vs baseline: 1.1382x; 1.0386x over previous
import numpy as np
from contextlib import ExitStack
import concourse.bass as bass
import concourse.mybir as mybir
from concourse.bass_utils import run_bass_kernel_spmd

F32 = mybir.dt.float32
BF16 = mybir.dt.bfloat16
ALU = mybir.AluOpType
AF = mybir.ActivationFunctionType
AX = mybir.AxisListType

D = 1024
T = 2048
NB = 2
NS = 16
DTR = 64
DC = 8
SPAN = 512
NSPAN = T // SPAN
TPS = SPAN // 128
LN2C = float(np.log(2.0))
EPS = 1e-5
M23 = float(1.5 * 2.0 ** 23)

_CACHE = {}


class Sem:
    def __init__(self, nc, es, name):
        self.h = es.enter_context(nc.semaphore(name))
        self.n = 0


class Prog:
    def __init__(self):
        self.items = {"sync": [], "vector": [], "scalar": [], "tensor": [], "gpsimd": []}

    def add(self, eng, fn):
        self.items[eng].append(fn)


def build_nc():
    nc = bass.Bass("TRN2", target_bir_lowering=False, debug=False,
                   detect_race_conditions=False)

    f_h = nc.dram_tensor("f", [NB * T, D], F32, kind="ExternalInput").ap()
    lng_h = nc.dram_tensor("ln_g", [D], F32, kind="ExternalInput").ap()
    lnb_h = nc.dram_tensor("ln_b", [D], F32, kind="ExternalInput").ap()
    wx_h = nc.dram_tensor("Wx", [D, D], F32, kind="ExternalInput").ap()
    bx_h = nc.dram_tensor("bx", [D], F32, kind="ExternalInput").ap()
    cw_h = nc.dram_tensor("conv_w", [D, 4], F32, kind="ExternalInput").ap()
    cb_h = nc.dram_tensor("conv_b", [D], F32, kind="ExternalInput").ap()
    wxp_h = nc.dram_tensor("Wxp", [D, 96], F32, kind="ExternalInput").ap()
    wdt_h = nc.dram_tensor("Wdt", [DTR, D], F32, kind="ExternalInput").ap()
    bdt_h = nc.dram_tensor("bdt", [D], F32, kind="ExternalInput").ap()
    alog_h = nc.dram_tensor("A_log", [D, NS], F32, kind="ExternalInput").ap()
    dp_h = nc.dram_tensor("Dp", [D], F32, kind="ExternalInput").ap()
    fw_h = nc.dram_tensor("fw", [D, D], F32, kind="ExternalInput").ap()
    fb_h = nc.dram_tensor("fb", [D], F32, kind="ExternalInput").ap()
    dw_h = nc.dram_tensor("dw", [D, 32], F32, kind="ExternalInput").ap()
    da_h = nc.dram_tensor("da", [1], F32, kind="ExternalInput").ap()
    out_h = nc.dram_tensor("out", [NB * T, D], F32, kind="ExternalOutput").ap()

    bcd = [nc.dram_tensor(f"bcd{b}", [32, T], BF16, kind="Internal").ap() for b in range(NB)]
    da_d = nc.dram_tensor("da_d", [1, 1], F32, kind="Internal").ap()

    es = ExitStack()
    S = lambda name, shape, dt: es.enter_context(nc.sbuf_tensor(name, shape, dt))

    wx_bf = [S(f"wx{i}", [128, D], BF16) for i in range(DC)]
    wqT = [S(f"wqT{i}", [128, D], BF16) for i in range(DC)]
    wxp_bf = [S(f"wxp{i}", [128, 96], BF16) for i in range(DC)]
    wdt_bf = S("wdt", [DTR, D], BF16)
    gam_bc = S("gam", [128, D], BF16)
    v0_bf = S("v0", [1, D], BF16)
    fb_bf = S("fbias", [1, D], BF16)
    ones512 = S("ones512", [1, SPAN], BF16)
    ident = S("ident", [128, 128], BF16)
    A_f = [S(f"A{i}", [128, NS], F32) for i in range(DC)]
    cw_c = [S(f"cw{i}", [128, 4], F32) for i in range(DC)]
    cb_c = [S(f"cb{i}", [128, 1], F32) for i in range(DC)]
    bdt_c = [S(f"bdtc{i}", [128, 1], F32) for i in range(DC)]
    dp_c = [S(f"dpc{i}", [128, 1], F32) for i in range(DC)]
    beta_c = [S(f"betac{i}", [128, 1], BF16) for i in range(DC)]
    eps_c = S("epsc", [128, 1], F32)
    one_c = S("onec", [128, 1], F32)
    n2l2_c = S("n2l2", [128, 1], F32)
    d2eps_c = S("d2eps", [128, 1], F32)
    d2e2_c = S("d2e2", [128, 1], F32)
    lnd_c = S("lndc", [128, 1], F32)
    p2l2_c = S("p2l2", [128, 1], F32)
    da_col = S("dacol", [128, 1], F32)
    da_t = S("dat", [1, 1], F32)
    da_t2 = S("dat2", [1, 1], F32)

    Fp = [S(f"Fp{i}", [128, D], F32) for i in range(5)]
    out_stg = S("out_stg", [128, D], F32)
    bigs = [S(f"big{i}", [128, T], BF16) for i in range(24)]
    dltT = bigs[0:8]
    u_t = bigs[8:16]
    xcy = bigs[16:24]
    fng_t = [bigs[8 + i // 2][:, (i % 2) * D:(i % 2 + 1) * D] for i in range(4)]
    x1_t = [bigs[tt // 2][:, (tt % 2) * D:(tt % 2 + 1) * D] for tt in range(16)]
    ytm_t = [bigs[8 + tt // 2][:, (tt % 2) * D:(tt % 2 + 1) * D] for tt in range(16)]
    xq_t = [bigs[16 + i // 2][:, (i % 2) * D:(i % 2 + 1) * D] for i in range(4)]
    ln3_half = bigs[18][:, 0:D]
    x0b_half = bigs[18][:, D:2 * D]

    x1buf = [S(f"x1b{i}", [128, SPAN + 3], BF16) for i in range(DC)]
    Bbc = S("Bbc", [128, T], BF16)
    Cbc = S("Cbc", [128, T], BF16)
    a_t = S("a_t", [128, T], BF16)
    b_t = S("b_t", [128, T], BF16)
    h_t = S("h_t", [128, T], BF16)
    fnT = [Bbc[:, i * SPAN:(i + 1) * SPAN] for i in range(4)] + \
          [Cbc[:, i * SPAN:(i + 1) * SPAN] for i in range(4)]
    dtT = h_t[0:DTR, 0:SPAN]
    bcsb = b_t[0:32, 0:SPAN]
    cv0 = a_t[:, 0:SPAN]
    cv1 = a_t[:, SPAN:2 * SPAN]
    v0_fr = Fp[4][0:1, 0:D]
    bx_row = Fp[3][0:1, 0:D]

    sm4 = [S(f"sm4_{i}", [128, 4], F32) for i in range(8)]
    sumx_s, sumq_s, mu_s, m2_s, rstd_s, nmr_s, t4a_s, t4b_s = sm4
    sq = [S(f"sq{i}", [128, 128], F32) for i in range(8)]
    hmid = S("hmid", [128, NS], F32)
    dwq = S("dwq", [128, 32], F32)

    psA_f = es.enter_context(nc.psum_tensor("psA", [128, SPAN], F32))
    psB_f = es.enter_context(nc.psum_tensor("psB", [128, SPAN], F32))
    psA = psA_f[:].bitcast(BF16)[:, 0:SPAN]
    psB = psB_f[:].bitcast(BF16)[:, 0:SPAN]
    psXd_f = es.enter_context(nc.psum_tensor("psXd", [128, SPAN], F32))
    psXd = psXd_f[0:96, :]
    psWx = es.enter_context(nc.psum_tensor("psWx", [128, SPAN], F32))
    psZ = es.enter_context(nc.psum_tensor("psZ", [128, SPAN], F32))
    psF0 = es.enter_context(nc.psum_tensor("psF0", [128, SPAN], F32))
    psF1 = es.enter_context(nc.psum_tensor("psF1", [128, SPAN], F32))
    psY0 = es.enter_context(nc.psum_tensor("psY0", [128, SPAN], F32))

    sLD = Sem(nc, es, "sLD")
    sPE = Sem(nc, es, "sPE")
    sDV = Sem(nc, es, "sDV")
    sAC = Sem(nc, es, "sAC")
    sGP = Sem(nc, es, "sGP")
    sST = Sem(nc, es, "sST")

    P = Prog()

    def dma(dst, src):
        def fn(e, dst=dst, src=src):
            e.dma_start(out=dst, in_=src).then_inc(sLD.h, 16)
        P.add("sync", fn)
        sLD.n += 16

    def dma_store(dst, src):
        def fn(e, dst=dst, src=src):
            e.dma_start(out=dst, in_=src).then_inc(sST.h, 16)
        P.add("sync", fn)
        sST.n += 16

    def ew(eng, sem, cnt):
        if cnt <= 0:
            return
        def fn(e, s=sem.h, c=cnt):
            e.wait_ge(s, c)
        P.add(eng, fn)

    ENG2SEM = {}

    def run(eng, sem, fn, waits=()):
        for (ws, wc) in waits:
            ew(eng, ws, wc)
        def wrapped(e, fn=fn, s=sem):
            last = fn(e)
            if last is None:
                last = e.nop()
            last.then_inc(s.h, 1)
        P.add(eng, wrapped)
        sem.n += 1

    def gp_init(g):
        g.memset(ident[:], 0.0)
        g.drain()
        g.affine_select(out=ident[:], in_=ident[:], compare_op=ALU.not_equal,
                        fill=1.0, base=0, pattern=[[-1, 128]], channel_multiplier=1)
        g.memset(ones512[:], 1.0)
        g.memset(eps_c[:], EPS)
        g.memset(one_c[:], 1.0)
        g.memset(n2l2_c[:], -2.0 * LN2C)
        g.memset(d2eps_c[:], float(D) * float(D) * EPS)
        g.memset(d2e2_c[:], float(D) * float(D) * EPS * EPS)
        g.memset(lnd_c[:], float(np.log(D)))
        return g.memset(p2l2_c[:], 2.0 * LN2C)
    run("gpsimd", sGP, gp_init)
    ew("scalar", sGP, sGP.n)
    ew("vector", sGP, sGP.n)

    for i in range(DC):
        dma(Fp[0][:], wx_h[i * 128:(i + 1) * 128, :])
        run("vector", sDV, (lambda v, i=i: v.tensor_copy(wx_bf[i][:], Fp[0][:])),
            waits=[(sLD, sLD.n)])
        ew("sync", sDV, sDV.n)
    for i in range(DC):
        dma(Fp[0][:, 0:96], wxp_h[i * 128:(i + 1) * 128, :])
        run("vector", sDV, (lambda v, i=i: v.tensor_copy(wxp_bf[i][:], Fp[0][:, 0:96])),
            waits=[(sLD, sLD.n)])
        ew("sync", sDV, sDV.n)
    dma(Fp[0][0:DTR, :], wdt_h[:, :])
    run("vector", sDV, (lambda v: v.tensor_copy(wdt_bf[:], Fp[0][0:DTR, :])),
        waits=[(sLD, sLD.n)])
    ew("sync", sDV, sDV.n)
    dma(Fp[0][:], lng_h.rearrange("(one d) -> one d", one=1).broadcast_to([128, D]))
    run("vector", sDV, (lambda v: v.tensor_copy(gam_bc[:], Fp[0][:])),
        waits=[(sLD, sLD.n)])
    ew("sync", sDV, sDV.n)

    for i in range(DC):
        sl = slice(i * 128, (i + 1) * 128)
        dma(A_f[i][:], alog_h[sl, :])
        dma(cw_c[i][:], cw_h[sl, :])
        dma(cb_c[i][:], cb_h[sl].rearrange("(p one) -> p one", one=1))
        dma(bdt_c[i][:], bdt_h[sl].rearrange("(p one) -> p one", one=1))
        dma(dp_c[i][:], dp_h[sl].rearrange("(p one) -> p one", one=1))
        dma(Fp[1][:, 0:1], lnb_h[sl].rearrange("(p one) -> p one", one=1))
        run("vector", sDV, (lambda v, i=i: v.tensor_copy(beta_c[i][:], Fp[1][:, 0:1])),
            waits=[(sLD, sLD.n)])
        ew("sync", sDV, sDV.n)
    dma(bx_row, bx_h.rearrange("(one d) -> one d", one=1))
    dma(Fp[1][0:1, 0:D], fb_h.rearrange("(one d) -> one d", one=1))
    dma(da_t[:], da_h.rearrange("(one o) -> one o", one=1))
    run("vector", sDV, (lambda v: v.tensor_copy(fb_bf[:], Fp[1][0:1, 0:D])),
        waits=[(sLD, sLD.n)])

    def fA(s):
        last = None
        for i in range(DC):
            last = s.activation(Fp[2][:, i * NS:(i + 1) * NS], A_f[i][:], AF.Exp)
        return last
    run("scalar", sAC, fA, waits=[(sLD, sLD.n)])

    def fAn(v):
        last = None
        for i in range(DC):
            last = v.tensor_scalar(A_f[i][:], Fp[2][:, i * NS:(i + 1) * NS], -1.0, None, ALU.mult)
        return last
    run("vector", sDV, fAn, waits=[(sAC, sAC.n)])

    run("scalar", sAC, (lambda s: s.activation(da_t2[:], da_t[:], AF.Tanh, scale=4.0)),
        waits=[(sLD, sLD.n)])
    run("vector", sDV, (lambda v: v.tensor_scalar(da_t[:], da_t2[:], 0.5, None, ALU.mult)),
        waits=[(sAC, sAC.n)])
    ew("sync", sDV, sDV.n)
    dma(da_d[:], da_t[:])
    ew("sync", sLD, sLD.n)
    dma(da_col[:], da_d[0:1, :].broadcast_to([128, 1]))

    def fV0(t):
        last = None
        for i in range(DC):
            t.matmul(psWx[0:1, :], beta_c[i][:], wx_bf[i][:, 0:SPAN],
                     start=(i == 0), stop=(i == DC - 1))
        for i in range(DC):
            last = t.matmul(psZ[0:1, :], beta_c[i][:], wx_bf[i][:, SPAN:2 * SPAN],
                            start=(i == 0), stop=(i == DC - 1))
        return last
    run("tensor", sPE, fV0, waits=[(sDV, sDV.n), (sGP, sGP.n)])

    def fV0e(v):
        v.tensor_tensor(v0_fr[0:1, 0:SPAN], psWx[0:1, :], bx_row[0:1, 0:SPAN], ALU.add)
        v.tensor_tensor(v0_fr[0:1, SPAN:2 * SPAN], psZ[0:1, :], bx_row[0:1, SPAN:2 * SPAN], ALU.add)
        return v.tensor_copy(v0_bf[:], v0_fr[0:1, 0:D])
    run("vector", sDV, fV0e, waits=[(sPE, sPE.n), (sLD, sLD.n)])

    def quant_chain(pairs, dd_ap, scratch, d4_waits=()):
        npair = len(pairs)
        G = 32 * npair
        am, am2, l_t, eh, p2, m_t, fx, e_t = sq
        S1, S2, S3, S4 = scratch

        def d1(v, pairs=tuple(pairs), G=G):
            for p, (srcp, _) in enumerate(pairs):
                v.tensor_reduce(am[:, p * 32:(p + 1) * 32],
                                srcp.rearrange("p (a b) -> p a b", b=32),
                                AX.X, ALU.max, apply_absolute_value=True)
            v.drain()
            v.tensor_scalar(am2[:, 0:G], am[:, 0:G], 1e-30, None, ALU.max)
            v.drain()
            return v.nop()
        run("vector", sDV, d1)

        def a1(s, G=G):
            return s.activation(l_t[:, 0:G], am2[:, 0:G], AF.Ln)
        run("scalar", sAC, a1, waits=[(sDV, sDV.n)])

        def d2(v, G=G):
            v.tensor_scalar(eh[:, 0:G], l_t[:, 0:G], 1.0 / LN2C, M23, ALU.mult, ALU.add)
            v.drain()
            v.tensor_scalar(eh[:, 0:G], eh[:, 0:G], M23, None, ALU.subtract)
            v.drain()
            return v.nop()
        run("vector", sDV, d2, waits=[(sAC, sAC.n)])

        def a2(s, G=G):
            return s.activation(p2[:, 0:G], eh[:, 0:G], AF.Exp, scale=-LN2C)
        run("scalar", sAC, a2, waits=[(sDV, sDV.n)])

        def d3(v, G=G, dd=dd_ap):
            v.tensor_tensor(m_t[:, 0:G], am2[:, 0:G], p2[:, 0:G], ALU.mult)
            v.drain()
            v.tensor_scalar(fx[:, 0:G], m_t[:, 0:G], 1.0, None, ALU.is_lt)
            v.drain()
            v.tensor_tensor(e_t[:, 0:G], eh[:, 0:G], fx[:, 0:G], ALU.subtract)
            v.drain()
            if dd.shape[1] == 1:
                v.tensor_scalar(e_t[:, 0:G], e_t[:, 0:G], dd, None, ALU.add)
            else:
                v.tensor_tensor(e_t[:, 0:G], e_t[:, 0:G], dd, ALU.add)
            v.drain()
            return v.nop()
        run("vector", sDV, d3, waits=[(sAC, sAC.n)])

        def a3(s, G=G):
            s.activation(l_t[:, 0:G], e_t[:, 0:G], AF.Exp, scale=LN2C, bias=n2l2_c[:])
            return s.activation(p2[:, 0:G], e_t[:, 0:G], AF.Exp, scale=-LN2C, bias=p2l2_c[:])
        run("scalar", sAC, a3, waits=[(sDV, sDV.n)])

        def d4(v, pairs=tuple(pairs)):
            last = None
            for p, (srcp, dstp) in enumerate(pairs):
                x3 = srcp.rearrange("p (a b) -> p a b", b=32)
                isc = p2[:, p * 32:(p + 1) * 32].unsqueeze(2).broadcast_to([128, 32, 32])
                scl = l_t[:, p * 32:(p + 1) * 32].unsqueeze(2).broadcast_to([128, 32, 32])
                xs = S1
                v.tensor_tensor(xs.rearrange("p (a b) -> p a b", b=32), x3, isc, ALU.mult)
                v.tensor_scalar(xs, xs, 6.0, -6.0, ALU.min, ALU.max)
                aq = S2
                v.tensor_tensor(aq, xs, xs, ALU.mult)
                tmp = S3
                v.tensor_scalar(tmp, aq, 4.0, 0.5, ALU.is_ge, ALU.mult)
                stp = S2
                v.scalar_tensor_tensor(stp, aq, 16.0, tmp, ALU.is_ge, ALU.add)
                v.tensor_scalar(stp, stp, 0.5, None, ALU.add)
                Ms = S3
                v.tensor_scalar(Ms, stp, M23, None, ALU.mult)
                qt = S4
                v.tensor_tensor(qt, xs, Ms, ALU.add)
                v.tensor_tensor(qt, qt, Ms, ALU.subtract)
                last = v.tensor_tensor(dstp.rearrange("p (a b) -> p a b", b=32),
                                       qt.rearrange("p (a b) -> p a b", b=32), scl, ALU.mult)
            return last
        run("vector", sDV, d4, waits=[(sAC, sAC.n)] + list(d4_waits))

    for oc in range(DC):
        dma(Fp[0][:], fw_h[oc * 128:(oc + 1) * 128, :])
        dma(dwq[:], dw_h[oc * 128:(oc + 1) * 128, :])
        run("scalar", sAC, (lambda s: s.activation(sq[1][:, 0:32], dwq[:], AF.Tanh, scale=4.0)),
            waits=[(sLD, sLD.n)])
        run("vector", sDV, (lambda v: v.tensor_scalar(dwq[:], sq[1][:, 0:32], 0.5, None, ALU.mult)),
            waits=[(sAC, sAC.n)])
        wq_dst = bigs[0][:, 0:D]
        quant_chain([(Fp[0][:], wq_dst)], dwq[:], (Fp[1][:], Fp[2][:], Fp[3][:], Fp[4][:]),
                    d4_waits=[(sPE, sPE.n)])

        def t_wq(t, src=wq_dst):
            last = None
            for g in range(2):
                ps = psA if g == 0 else psB
                for q in range(4):
                    dcx = g * 4 + q
                    last = t.transpose(ps[:, q * 128:(q + 1) * 128],
                                       src[:, dcx * 128:(dcx + 1) * 128], ident[:])
            return last
        run("tensor", sPE, t_wq, waits=[(sDV, sDV.n)])

        def e_wq(v, oc=oc):
            last = None
            for g in range(2):
                ps = psA if g == 0 else psB
                for q in range(4):
                    dcx = g * 4 + q
                    last = v.tensor_copy(wqT[dcx][:, oc * 128:(oc + 1) * 128],
                                         ps[:, q * 128:(q + 1) * 128])
            return last
        run("vector", sDV, e_wq, waits=[(sPE, sPE.n)])
        ew("sync", sDV, sDV.n)

    def ln_smalls(col_n, c0=0, dbl=False):
        def d(v, n=col_n, c0=c0):
            sl = slice(c0, c0 + n)
            v.tensor_tensor(t4a_s[:, sl], sumx_s[:, sl], sumx_s[:, sl], ALU.mult)
            v.tensor_scalar(mu_s[:, sl], sumx_s[:, sl], 1.0 / D, None, ALU.mult)
            v.drain()
            v.scalar_tensor_tensor(t4b_s[:, sl], sumq_s[:, sl], float(D), t4a_s[:, sl],
                                   ALU.mult, ALU.subtract)
            v.drain()
            return v.nop()
        run("vector", sDV, d, waits=[(sAC, sAC.n)])

        def a(s, n=col_n, c0=c0, dbl=dbl):
            sl = slice(c0, c0 + n)
            if dbl:
                s.activation(t4a_s[:, sl], t4b_s[:, sl], AF.Ln, scale=1.0 + EPS,
                             bias=d2e2_c[:])
            else:
                s.activation(t4a_s[:, sl], t4b_s[:, sl], AF.Ln, bias=d2eps_c[:])
            s.drain()
            s.activation(rstd_s[:, sl], t4a_s[:, sl], AF.Exp, scale=-0.5, bias=lnd_c[:])
            s.drain()
            return s.nop()
        run("scalar", sAC, a, waits=[(sDV, sDV.n)])

    for b in range(NB):
        for sp in range(NSPAN):
            t0 = sp * SPAN
            ew("sync", sDV, sDV.n)
            ew("sync", sAC, sAC.n)
            for q in range(TPS):
                row = b * T + t0 + q * 128
                dma(Fp[q][:], f_h[row:row + 128, :])
            ld_all = sLD.n
            def sq1(s):
                last = None
                for q in range(TPS):
                    s.activation(Fp[4][:], Fp[q][:], AF.Copy, accum_out=sumx_s[:, q:q + 1])
                    last = s.activation(Fp[4][:], Fp[q][:], AF.Square,
                                        accum_out=sumq_s[:, q:q + 1])
                return last
            run("scalar", sAC, sq1, waits=[(sLD, ld_all), (sDV, sDV.n)])
            ln_smalls(TPS, dbl=True)
            def ap2(v):
                last = None
                for q in range(TPS):
                    v.tensor_scalar(Fp[q][:], Fp[q][:], mu_s[:, q:q + 1], None, ALU.subtract)
                    v.tensor_scalar(Fp[q][:], Fp[q][:], rstd_s[:, q:q + 1], None, ALU.mult)
                    last = v.tensor_tensor(fng_t[q][:], Fp[q][:], gam_bc[:], ALU.mult)
                return last
            run("vector", sDV, ap2, waits=[(sAC, sAC.n)])

            fng_done = sDV.n
            evac_marks = {}
            for dcx in range(DC):
                ps = psA if dcx % 2 == 0 else psB
                waits = [(sDV, fng_done)]
                if dcx >= 2:
                    waits.append((sAC, evac_marks[dcx - 2]))

                def tr(t, dcx=dcx, ps=ps):
                    last = None
                    for q in range(TPS):
                        last = t.transpose(ps[:, q * 128:(q + 1) * 128],
                                           fng_t[q][:, dcx * 128:(dcx + 1) * 128], ident[:])
                    return last
                run("tensor", sPE, tr, waits=waits)

                def ev(s, dcx=dcx, ps=ps):
                    return s.activation(fnT[dcx], ps[:], AF.Copy)
                run("scalar", sAC, ev, waits=[(sPE, sPE.n)])
                evac_marks[dcx] = sAC.n

            if sp == 0:
                def cc(v):
                    last = None
                    for jc in range(DC):
                        last = v.memset(x1buf[jc][:, 0:3], 0.0)
                    return last
                run("vector", sDV, cc)
            else:
                def cc(v):
                    last = None
                    for jc in range(DC):
                        last = v.tensor_copy(x1buf[jc][:, 0:3], x1buf[jc][:, SPAN:SPAN + 3])
                    return last
                run("vector", sDV, cc)

            fnT_done_ac = sAC.n
            fnT_done = sDV.n
            wx_evac = {}
            cvsets = [(a_t[:, 0:SPAN], a_t[:, SPAN:2 * SPAN]),
                      (a_t[:, 2 * SPAN:3 * SPAN], a_t[:, 3 * SPAN:4 * SPAN])]

            def emit_cnv_sil(jc):
                cvA, cvB = cvsets[jc % 2]
                def cnv(v, jc=jc, cvA=cvA, cvB=cvB):
                    v.tensor_scalar(cvA, x1buf[jc][:, 0:SPAN], cw_c[jc][:, 0:1], None, ALU.mult)
                    v.scalar_tensor_tensor(cvB, x1buf[jc][:, 1:1 + SPAN], cw_c[jc][:, 1:2],
                                           cvA, ALU.mult, ALU.add)
                    v.tensor_scalar(cvA, x1buf[jc][:, 2:2 + SPAN], cw_c[jc][:, 2:3], None, ALU.mult)
                    v.tensor_tensor(cvA, cvA, cvB, ALU.add)
                    return v.scalar_tensor_tensor(cvB, x1buf[jc][:, 3:3 + SPAN], cw_c[jc][:, 3:4],
                                                  cvA, ALU.mult, ALU.add)
                run("vector", sDV, cnv, waits=[(sAC, wx_evac[jc])])
                def sil(s, jc=jc, t0=t0, cvB=cvB):
                    return s.activation(xcy[jc][:, t0:t0 + SPAN], cvB, AF.Silu, bias=cb_c[jc][:])
                run("scalar", sAC, sil, waits=[(sDV, sDV.n)])

            for jc in range(DC):
                ps = psWx if jc % 2 == 0 else psZ
                waits = [(sDV, fnT_done), (sAC, fnT_done_ac)]
                if jc >= 2:
                    waits.append((sAC, wx_evac[jc - 2]))
                else:
                    waits.append((sAC, sAC.n))

                def mmwx(t, jc=jc, ps=ps):
                    for dcx in range(DC):
                        t.matmul(ps[:], wx_bf[dcx][:, jc * 128:(jc + 1) * 128], fnT[dcx],
                                 start=(dcx == 0), stop=False)
                    return t.matmul(ps[:], v0_bf[0:1, jc * 128:(jc + 1) * 128], ones512[0:1, :],
                                    start=False, stop=True)
                run("tensor", sPE, mmwx, waits=waits)

                def evx(s, jc=jc, ps=ps):
                    return s.activation(x1buf[jc][:, 3:3 + SPAN], ps[:], AF.Copy)
                run("scalar", sAC, evx, waits=[(sPE, sPE.n), (sDV, sDV.n)])
                wx_evac[jc] = sAC.n
                if jc >= 1:
                    emit_cnv_sil(jc - 1)
            emit_cnv_sil(DC - 1)

            def mmxp(t, t0=t0):
                last = None
                for dcx in range(DC):
                    last = t.matmul(psXd[:], wxp_bf[dcx][:], xcy[dcx][:, t0:t0 + SPAN],
                                    start=(dcx == 0), stop=(dcx == DC - 1))
                return last
            run("tensor", sPE, mmxp, waits=[(sAC, sAC.n)])

            run("scalar", sAC, (lambda s: s.activation(dtT, psXd[0:DTR, :], AF.Copy)),
                waits=[(sPE, sPE.n)])
            run("vector", sDV, (lambda v: v.tensor_copy(bcsb, psXd[DTR:96, :])),
                waits=[(sPE, sPE.n), (sLD, sLD.n)])
            ew("sync", sDV, sDV.n)
            def st_bc(e, b=b, t0=t0):
                e.dma_start(out=bcd[b][:, t0:t0 + SPAN], in_=bcsb).then_inc(sLD.h, 16)
            P.add("sync", st_bc)
            sLD.n += 16

            sp_done = sAC.n
            sp_evac = {}
            spe_marks = {}

            def emit_spd_spe(dcx):
                par = dcx % 2
                h0 = par * SPAN
                zb = Fp[0][:, h0:h0 + SPAN]
                mx = Fp[1][:, h0:h0 + SPAN]
                mn = Fp[2][:, h0:h0 + SPAN]
                ngab = Fp[3][:, h0:h0 + SPAN]
                lnb = Fp[4][:, h0:h0 + SPAN]
                def spd(v, dcx=dcx, zb=zb, mx=mx, mn=mn, ngab=ngab):
                    v.tensor_scalar(mx, zb, bdt_c[dcx][:], 0.0, ALU.add, ALU.max)
                    v.tensor_scalar(mn, zb, bdt_c[dcx][:], 0.0, ALU.add, ALU.min)
                    return v.tensor_tensor(ngab, mn, mx, ALU.subtract)
                run("vector", sDV, spd, waits=[(sAC, sp_evac[dcx])])
                spd_m = sDV.n
                def spe(s, ngab=ngab, lnb=lnb):
                    s.activation(lnb, ngab, AF.Exp)
                    s.drain()
                    s.activation(ngab, lnb, AF.Ln, bias=one_c[:])
                    s.drain()
                    return s.nop()
                run("scalar", sAC, spe, waits=[(sDV, spd_m)])
                spe_marks[dcx] = sAC.n

            def emit_spf(dcx, t0):
                par = dcx % 2
                h0 = par * SPAN
                ngab = Fp[3][:, h0:h0 + SPAN]
                mx = Fp[1][:, h0:h0 + SPAN]
                def spf(v, dcx=dcx, t0=t0, ngab=ngab, mx=mx):
                    return v.tensor_tensor(dltT[dcx][:, t0:t0 + SPAN], ngab, mx, ALU.add)
                run("vector", sDV, spf, waits=[(sAC, spe_marks[dcx])])

            for dcx in range(DC):
                ps = psWx if dcx % 2 == 0 else psZ
                waits = [(sAC, sp_done)]
                if dcx >= 2:
                    waits.append((sAC, sp_evac[dcx - 2]))

                def mmdt(t, dcx=dcx, ps=ps):
                    return t.matmul(ps[:], wdt_bf[:, dcx * 128:(dcx + 1) * 128], dtT,
                                    start=True, stop=True)
                run("tensor", sPE, mmdt, waits=waits)

                def spa(s, dcx=dcx, ps=ps):
                    par = dcx % 2
                    return s.activation(Fp[0][:, par * SPAN:(par + 1) * SPAN], ps[:], AF.Copy)
                run("scalar", sAC, spa, waits=[(sPE, sPE.n), (sDV, sDV.n)])
                sp_evac[dcx] = sAC.n
                emit_spd_spe(dcx)
                if dcx >= 1:
                    emit_spf(dcx - 1, t0)
            emit_spf(DC - 1, t0)

        def s2init(v):
            for dcx in range(DC):
                v.tensor_tensor(u_t[dcx][:], dltT[dcx][:], xcy[dcx][:], ALU.mult)
            last = None
            for dcx in range(DC):
                last = v.tensor_scalar(xcy[dcx][:], xcy[dcx][:], dp_c[dcx][:], None, ALU.mult)
            return last
        run("vector", sDV, s2init)
        ew("scalar", sDV, sDV.n)

        psy8 = [psWx, psZ, psF0, psF1, psA_f, psB_f, psXd_f, psY0]
        a_bufs = [Fp[0][:, 0:SPAN].bitcast(BF16), Fp[0][:, SPAN:2 * SPAN].bitcast(BF16),
                  Cbc[:, 0:T // 2], Cbc[:, T // 2:T]]
        z_bufs = [Fp[1][:, 0:SPAN].bitcast(BF16), Fp[1][:, SPAN:2 * SPAN].bitcast(BF16)]
        HT = T // 2
        bc_bufs = [(Fp[2][:, 0:SPAN].bitcast(BF16), Fp[3][:, 0:SPAN].bitcast(BF16)),
                   (Fp[2][:, SPAN:2 * SPAN].bitcast(BF16), Fp[3][:, SPAN:2 * SPAN].bitcast(BF16)),
                   (Fp[4][:, 0:SPAN].bitcast(BF16), Fp[4][:, SPAN:2 * SPAN].bitcast(BF16)),
                   (b_t[:, HT:2 * HT], h_t[:, HT:2 * HT])]
        NPF = 4
        if b == 0:
            ew("sync", sPE, sPE.n)
        ew("sync", sLD, sLD.n)
        ew("sync", sAC, sAC.n)
        ew("scalar", sDV, sDV.n)
        ew("scalar", sPE, sPE.n)
        s2init_dv = sDV.n
        for dcx in range(DC):
            for half in range(2):
                ht0 = half * HT
                ew("sync", sDV, sDV.n)
                bc_ld = {}
                for pf in range(NPF):
                    dma(bc_bufs[pf][0], bcd[b][pf:pf + 1, ht0:ht0 + HT].broadcast_to([128, HT]))
                    dma(bc_bufs[pf][1], bcd[b][16 + pf:17 + pf, ht0:ht0 + HT].broadcast_to([128, HT]))
                    bc_ld[pf] = sLD.n

                def tinit(t, dcx=dcx, ht0=ht0):
                    last = None
                    for q in range(8):
                        last = t.matmul(psy8[q][:, 0:128],
                                        xcy[dcx][:, ht0 + q * 128:ht0 + (q + 1) * 128],
                                        ident[:], start=True, stop=False)
                    return last
                run("tensor", sPE, tinit, waits=[(sDV, s2init_dv), (sAC, sAC.n)])

                scn_marks = {}
                zt_marks = {}
                a_marks = {}
                for n in range(NS):
                    aw = []
                    if n >= 4:
                        aw.append((sDV, scn_marks[n - 4]))
                    def abuild(s, n=n, dcx=dcx, ht0=ht0):
                        return s.activation(a_bufs[n % 4], dltT[dcx][:, ht0:ht0 + HT],
                                            AF.Copy, scale=A_f[dcx][:, n:n + 1])
                    run("scalar", sAC, abuild, waits=aw)
                    a_marks[n] = sAC.n

                    dw = [(sLD, bc_ld[n]), (sAC, a_marks[n])]
                    if n >= 2:
                        dw.append((sPE, zt_marks[n - 2]))
                    def scn(v, n=n, dcx=dcx, half=half, ht0=ht0):
                        Bb, Cb = bc_bufs[n % NPF]
                        v.tensor_tensor(b_t[:, 0:HT], u_t[dcx][:, ht0:ht0 + HT], Bb, ALU.mult)
                        init = 0.0 if half == 0 else hmid[:, n:n + 1]
                        v.tensor_tensor_scan(h_t[:, 0:HT], a_bufs[n % 4], b_t[:, 0:HT],
                                             init, ALU.mult, ALU.add)
                        last = v.tensor_tensor(z_bufs[n % 2], h_t[:, 0:HT], Cb, ALU.mult)
                        if half == 0:
                            last = v.tensor_copy(hmid[:, n:n + 1], h_t[:, HT - 1:HT])
                        return last
                    run("vector", sDV, scn, waits=dw)
                    scn_marks[n] = sDV.n

                    def zt(t, n=n):
                        last = None
                        for q in range(8):
                            last = t.matmul(psy8[q][:, 0:128],
                                            z_bufs[n % 2][:, q * 128:(q + 1) * 128], ident[:],
                                            start=False, stop=(n == NS - 1))
                        return last
                    run("tensor", sPE, zt, waits=[(sDV, scn_marks[n])])
                    zt_marks[n] = sPE.n

                    if n + NPF < NS:
                        ew("sync", sDV, scn_marks[n])
                        dma(bc_bufs[n % NPF][0],
                            bcd[b][n + NPF:n + NPF + 1, ht0:ht0 + HT].broadcast_to([128, HT]))
                        dma(bc_bufs[n % NPF][1],
                            bcd[b][16 + n + NPF:17 + n + NPF, ht0:ht0 + HT].broadcast_to([128, HT]))
                        bc_ld[n + NPF] = sLD.n

                def yev(s, dcx=dcx, half=half):
                    last = None
                    for q in range(8):
                        tt = half * 8 + q
                        last = s.activation(ytm_t[tt][:, dcx * 128:(dcx + 1) * 128],
                                            psy8[q][:, 0:128], AF.Copy)
                    return last
                run("scalar", sAC, yev, waits=[(sPE, sPE.n)])

        ew("vector", sAC, sAC.n)
        x0b4 = [xcy[2 + q // 2][:, (q % 2) * D:(q % 2 + 1) * D] for q in range(4)]
        ln34 = [xcy[4 + q // 2][:, (q % 2) * D:(q % 2 + 1) * D] for q in range(4)]
        q_scr = (b_t[:].bitcast(F32), h_t[:].bitcast(F32), a_t[:].bitcast(F32), Fp[4][:])
        for sp in range(NSPAN):
            t0 = sp * SPAN
            ew("sync", sDV, sDV.n)
            ew("sync", sAC, sAC.n)
            for q in range(TPS):
                row = b * T + t0 + q * 128
                dma(Fp[q][:], f_h[row:row + 128, :])
            ld_all = sLD.n
            def sq3(s, sp=sp):
                last = None
                for q in range(TPS):
                    s.activation(Fp[4][:], ytm_t[sp * TPS + q][:], AF.Copy,
                                 accum_out=sumx_s[:, q:q + 1])
                    last = s.activation(Fp[4][:], ytm_t[sp * TPS + q][:], AF.Square,
                                        accum_out=sumq_s[:, q:q + 1])
                return last
            run("scalar", sAC, sq3, waits=[(sDV, sDV.n)])
            ln_smalls(TPS)
            def ap3(v, sp=sp):
                last = None
                for q in range(TPS):
                    tt = sp * TPS + q
                    v.tensor_scalar(ln34[q], ytm_t[tt][:], mu_s[:, q:q + 1], None, ALU.subtract)
                    last = v.tensor_scalar(ln34[q], ln34[q], rstd_s[:, q:q + 1], None, ALU.mult)
                return last
            run("vector", sDV, ap3, waits=[(sAC, sAC.n)])
            def cst(s):
                last = None
                for q in range(TPS):
                    last = s.activation(x0b4[q], Fp[q][:], AF.Copy)
                return last
            run("scalar", sAC, cst, waits=[(sLD, ld_all), (sPE, sPE.n)])
            def res1(v, sp=sp):
                last = None
                for q in range(TPS):
                    last = v.tensor_tensor(x1_t[sp * TPS + q][:], ln34[q], x0b4[q], ALU.add)
                return last
            run("vector", sDV, res1, waits=[(sAC, sAC.n)])
            def sq4(s, sp=sp):
                last = None
                for q in range(TPS):
                    s.activation(Fp[4][:], x1_t[sp * TPS + q][:], AF.Copy,
                                 accum_out=sumx_s[:, q:q + 1])
                    last = s.activation(Fp[4][:], x1_t[sp * TPS + q][:], AF.Square,
                                        accum_out=sumq_s[:, q:q + 1])
                return last
            run("scalar", sAC, sq4, waits=[(sDV, sDV.n)])
            ln_smalls(TPS)
            def ap4(v, sp=sp):
                last = None
                for q in range(TPS):
                    tt = sp * TPS + q
                    v.tensor_scalar(Fp[q][:], x1_t[tt][:], mu_s[:, q:q + 1], None, ALU.subtract)
                    last = v.tensor_scalar(Fp[q][:], Fp[q][:], rstd_s[:, q:q + 1], None, ALU.mult)
                return last
            run("vector", sDV, ap4, waits=[(sAC, sAC.n)])
            quant_chain([(Fp[q][:], xq_t[q][:]) for q in range(TPS)], da_col[:],
                        q_scr, d4_waits=[(sPE, sPE.n)])

            xq_done = sDV.n
            evac_marks = {}
            for dcx in range(DC):
                ps = psA if dcx % 2 == 0 else psB
                waits = [(sDV, xq_done)]
                if dcx >= 2:
                    waits.append((sAC, evac_marks[dcx - 2]))

                def trq(t, dcx=dcx, ps=ps):
                    last = None
                    for q in range(TPS):
                        last = t.transpose(ps[:, q * 128:(q + 1) * 128],
                                           xq_t[q][:, dcx * 128:(dcx + 1) * 128], ident[:])
                    return last
                run("tensor", sPE, trq, waits=waits)

                def evq(s, dcx=dcx, ps=ps):
                    return s.activation(fnT[dcx], ps[:], AF.Copy)
                run("scalar", sAC, evq, waits=[(sPE, sPE.n)])
                evac_marks[dcx] = sAC.n

            xqT_done = sDV.n
            xqT_done_ac = sAC.n
            evf_marks = {}
            for q in range(TPS):
                tt = sp * TPS + q
                row = b * T + t0 + q * 128
                pF0, pF1 = (psF0, psF1) if q % 2 == 0 else (psA_f, psB_f)
                waits = [(sDV, xqT_done), (sAC, max(xqT_done_ac, sAC.n if q < 2 else evf_marks[q - 2]))]

                def mmf(t, q=q, pF0=pF0, pF1=pF1):
                    for dcx in range(DC):
                        sl = slice(q * 128, (q + 1) * 128)
                        t.matmul(pF0[:], fnT[dcx][:, sl], wqT[dcx][:, 0:SPAN],
                                 start=(dcx == 0), stop=False)
                        t.matmul(pF1[:], fnT[dcx][:, sl], wqT[dcx][:, SPAN:2 * SPAN],
                                 start=(dcx == 0), stop=False)
                    t.matmul(pF0[:], ones512[0:1, 0:128], fb_bf[0:1, 0:SPAN],
                             start=False, stop=True)
                    return t.matmul(pF1[:], ones512[0:1, 0:128], fb_bf[0:1, SPAN:2 * SPAN],
                                    start=False, stop=True)
                run("tensor", sPE, mmf, waits=waits)

                def evf(s, q=q, pF0=pF0, pF1=pF1):
                    s.activation(Fp[q][:, 0:SPAN], pF0[:], AF.Copy, accum_out=t4a_s[:, q:q + 1])
                    s.activation(Fp[q][:, SPAN:2 * SPAN], pF1[:], AF.Copy,
                                 accum_out=t4b_s[:, q:q + 1])
                    return s.activation(b_t[:].bitcast(F32), Fp[q][:], AF.Square,
                                        accum_out=sumq_s[:, q:q + 1])
                run("scalar", sAC, evf, waits=[(sPE, sPE.n)])
                evf_marks[q] = sAC.n

                def evs(v, q=q):
                    v.tensor_tensor(sumx_s[:, q:q + 1], t4a_s[:, q:q + 1],
                                    t4b_s[:, q:q + 1], ALU.add)
                    v.drain()
                    return v.nop()
                run("vector", sDV, evs, waits=[(sAC, evf_marks[q])])
                ln_smalls(1, c0=q)
                if sST.n > 0:
                    ew("vector", sST, sST.n)
                def ap5(v, tt=tt, q=q):
                    scr5 = h_t[:].bitcast(F32)
                    v.tensor_scalar(scr5, Fp[q][:], mu_s[:, q:q + 1], None, ALU.subtract)
                    v.tensor_scalar(scr5, scr5, rstd_s[:, q:q + 1], None, ALU.mult)
                    return v.tensor_tensor(out_stg[:], scr5, x1_t[tt][:], ALU.add)
                run("vector", sDV, ap5, waits=[(sAC, sAC.n)])
                ew("sync", sDV, sDV.n)
                dma_store(out_h[row:row + 128, :], out_stg[:])

    def fin(e):
        e.wait_ge(sST.h, sST.n)
    P.add("sync", fin)

    with nc.Block() as blk:
        blk.sync(lambda e: [fn(e) for fn in P.items["sync"]])
        blk.vector(lambda e: [fn(e) for fn in P.items["vector"]])
        blk.scalar(lambda e: [fn(e) for fn in P.items["scalar"]])
        blk.tensor(lambda e: [fn(e) for fn in P.items["tensor"]])
        blk.gpsimd(lambda e: [fn(e) for fn in P.items["gpsimd"]])
    es.close()
    return nc


def kernel(L0_attn_f, L0_attn_ln_g, L0_attn_ln_b, L0_attn_Wx, L0_attn_bx,
           L0_attn_conv_w, L0_attn_conv_b, L0_attn_Wxp, L0_attn_Wdt, L0_attn_bdt,
           L0_attn_A_log, L0_attn_Dp, L0_ffn_weight, L0_ffn_bias, L0_ffn_delta_w,
           L0_ffn_delta_act):
    if "nc" not in _CACHE:
        _CACHE["nc"] = build_nc()
    nc = _CACHE["nc"]
    f = np.ascontiguousarray(np.asarray(L0_attn_f, dtype=np.float32))
    common = {
        "ln_g": np.ascontiguousarray(L0_attn_ln_g, dtype=np.float32),
        "ln_b": np.ascontiguousarray(L0_attn_ln_b, dtype=np.float32),
        "Wx": np.ascontiguousarray(L0_attn_Wx, dtype=np.float32),
        "bx": np.ascontiguousarray(L0_attn_bx, dtype=np.float32),
        "conv_w": np.ascontiguousarray(L0_attn_conv_w, dtype=np.float32),
        "conv_b": np.ascontiguousarray(L0_attn_conv_b, dtype=np.float32),
        "Wxp": np.ascontiguousarray(L0_attn_Wxp, dtype=np.float32),
        "Wdt": np.ascontiguousarray(L0_attn_Wdt, dtype=np.float32),
        "bdt": np.ascontiguousarray(L0_attn_bdt, dtype=np.float32),
        "A_log": np.ascontiguousarray(L0_attn_A_log, dtype=np.float32),
        "Dp": np.ascontiguousarray(L0_attn_Dp, dtype=np.float32),
        "fw": np.ascontiguousarray(L0_ffn_weight, dtype=np.float32),
        "fb": np.ascontiguousarray(L0_ffn_bias, dtype=np.float32),
        "dw": np.ascontiguousarray(L0_ffn_delta_w, dtype=np.float32),
        "da": np.ascontiguousarray(L0_ffn_delta_act, dtype=np.float32),
    }
    in_maps = []
    for c in range(8):
        m = dict(common)
        m["f"] = np.ascontiguousarray(f[2 * c:2 * c + 2].reshape(NB * T, D))
        in_maps.append(m)
    res = run_bass_kernel_spmd(nc, in_maps, core_ids=list(range(8)))
    out = np.concatenate([r["out"].reshape(NB, T, D) for r in res.results], axis=0)
    return np.ascontiguousarray(out, dtype=np.float32)
